# version 01
# baseline (speedup 1.0000x reference)
import numpy as np
from contextlib import ExitStack
import concourse.bass as bass
import concourse.mybir as mybir
from concourse.bass_utils import run_bass_kernel_spmd

F32 = mybir.dt.float32
BF16 = mybir.dt.bfloat16
AF = mybir.ActivationFunctionType
OP = mybir.AluOpType
AX = mybir.AxisListType

NCORES = 8
LNEPS = 1e-5

_WRITE_KW = {"out", "accum_out", "out_max", "out_indices", "dest", "ap"}

ENGINES = ("t", "v", "s", "g", "y")
DMA_QUEUES = {"g": "qg", "y": "qy"}


def _ap_span(ap):
    try:
        name = ap.tensor.name
    except AttributeError:
        return None
    off = int(ap.offset)
    reach = 1
    for step, cnt in ap.ap:
        reach += abs(int(step)) * (int(cnt) - 1)
    is_psum = "PSum" in type(ap.tensor).__name__
    return (name, off, off + reach, is_psum)


class Op:
    __slots__ = ("eng", "fn", "reads", "writes", "idx", "need_inc", "is_dma")

    def __init__(self, eng, fn, reads, writes, idx, is_dma):
        self.eng, self.fn, self.reads, self.writes = eng, fn, reads, writes
        self.idx, self.is_dma = idx, is_dma
        self.need_inc = False


class _EngProxy:
    def __init__(self, prog, eng):
        self._p = prog
        self._e = eng

    def __getattr__(self, fname):
        def call(*args, _reads=(), _writes=(), **kw):
            return self._p.record(self._e, fname, args, kw, _reads, _writes)
        return call


class Prog:

    def __init__(self, nc):
        self.nc = nc
        self.ops = {e: [] for e in ENGINES}
        self.order = []
        self.segs = {}
        self.deps = {}
        for e in ENGINES:
            setattr(self, e, _EngProxy(self, e))

    def record(self, eng, fname, args, kw, extra_reads, extra_writes):
        reads, writes = [], []
        if fname.startswith("dma_start"):
            writes.append(kw["out"] if "out" in kw else args[0])
            reads.append(kw["in_"] if "in_" in kw else args[1])
        elif fname in ("matmul", "transpose"):
            writes.append(args[0])
            reads.extend(a for a in args[1:] if hasattr(a, "tensor"))
        elif fname in ("memset", "iota"):
            writes.append(args[0])
        elif fname == "affine_select":
            writes.append(kw.get("out", args[0] if args else None))
            if "in_" in kw:
                reads.append(kw["in_"])
        else:
            for i, a in enumerate(args):
                if hasattr(a, "tensor"):
                    (writes if i == 0 else reads).append(a)
            for k, v in kw.items():
                if hasattr(v, "tensor"):
                    (writes if k in _WRITE_KW else reads).append(v)
        reads.extend(extra_reads)
        writes.extend(extra_writes)
        rs = [x for x in (_ap_span(a) for a in reads if a is not None) if x]
        ws = [x for x in (_ap_span(a) for a in writes if a is not None) if x]
        is_dma = fname.startswith("dma_start")
        op = Op(eng, (fname, args, kw), rs, ws, len(self.ops[eng]), is_dma)
        self.ops[eng].append(op)
        self.order.append(op)
        self._resolve(op)
        return op

    def _resolve(self, op):
        deps = set()

        def add(producer):
            if producer is None or producer is op:
                return
            if (producer.eng == op.eng == "t"
                    and not producer.is_dma and not op.is_dma):
                return
            deps.add(producer)

        for (name, a, b, is_psum) in op.reads:
            for seg in self.segs.get(name, []):
                if seg[0] < b and a < seg[1]:
                    add(seg[2])
                    if is_psum:
                        for r in seg[3]:
                            if r.eng != op.eng:
                                add(r)
                    seg[3].append(op)
        for (name, a, b, is_psum) in op.writes:
            lst = self.segs.setdefault(name, [])
            keep = []
            for seg in lst:
                if seg[0] < b and a < seg[1]:
                    add(seg[2])
                    for r in seg[3]:
                        add(r)
                    if seg[0] < a:
                        keep.append([seg[0], a, seg[2], list(seg[3])])
                    if seg[1] > b:
                        keep.append([b, seg[1], seg[2], list(seg[3])])
                else:
                    keep.append(seg)
            keep.append([a, b, op, []])
            self.segs[name] = keep
        self.deps[op] = deps

    def emit(self, block, stack, dma_pool=16):
        nc = self.nc
        sems = {}
        for e in ENGINES:
            sems[e] = stack.enter_context(nc.semaphore(f"ss_{e}"))
        qsems = {}
        for q in DMA_QUEUES.values():
            for j in range(dma_pool):
                qsems[f"{q}{j}"] = stack.enter_context(nc.semaphore(f"ss_{q}{j}"))
        sems.update(qsems)

        for deps in self.deps.values():
            for d in deps:
                d.need_inc = True

        semval = {}
        counters = {k: 0 for k in ENGINES}
        qcount = {q: 0 for q in DMA_QUEUES.values()}
        dma_sem_of = {}
        for op in self.order:
            if op.is_dma:
                q = DMA_QUEUES[op.eng]
                i = qcount[q]
                qcount[q] += 1
                dom = f"{q}{i % dma_pool}"
                dma_sem_of[op] = dom
                semval[op] = (dom, 16 * (i // dma_pool + 1))
            elif op.need_inc:
                counters[op.eng] += 1
                semval[op] = (op.eng, counters[op.eng])

        clocks = {e: {} for e in ENGINES}
        snapshots = {}
        emitted = {e: [] for e in ENGINES}
        for op in self.order:
            ck = clocks[op.eng]
            waits = []
            deps = sorted(self.deps[op], key=lambda o: (o.eng, o.idx))
            for d in deps:
                dom, val = semval[d]
                if ck.get(dom, 0) >= val:
                    continue
                waits.append((dom, val))
                ck[dom] = val
                sn = snapshots.get((dom, val))
                if sn:
                    for d2, v2 in sn.items():
                        if ck.get(d2, 0) < v2:
                            ck[d2] = v2
            mv = semval.get(op)
            if mv is not None:
                snapshots[mv] = dict(ck)
            emitted[op.eng].append((op, waits, mv))

        eng_map = {"t": block.tensor, "v": block.vector, "s": block.scalar,
                   "g": block.gpsimd, "y": block.sync}
        for e in ENGINES:
            if not emitted[e]:
                continue

            @eng_map[e]
            def _(eh, e=e):
                for op, waits, mv in emitted[e]:
                    for dom, val in waits:
                        eh.wait_ge(sems[dom], val)
                    fname, args, kw = op.fn
                    instr = getattr(eh, fname)(*args, **kw)
                    if op.is_dma:
                        instr.then_inc(sems[dma_sem_of[op]], 16)
                    elif mv is not None:
                        instr.then_inc(sems[e], 1)
                if e in DMA_QUEUES:
                    q = DMA_QUEUES[e]
                    n = qcount[q]
                    for j in range(min(n, dma_pool)):
                        cnt = len([i for i in range(n) if i % dma_pool == j])
                        eh.wait_ge(sems[f"{q}{j}"], 16 * cnt)
        return sems


def emit_ln(p, x_tiles, out_tiles, stat, sq, eps_ap):
    for k, xk in enumerate(x_tiles):
        d = xk.shape[-1]
        P = xk.shape[0]
        st = stat[0:P, 16 * k:16 * k + 16]
        p.v.tensor_reduce(st[:, 4:5], xk, AX.X, OP.add)
        p.v.scalar_tensor_tensor(sq[0:P, 0:d], xk, 1.0, xk, OP.mult, OP.mult,
                                 accum_out=st[:, 5:6])
        p.v.tensor_scalar(st[:, 0:1], st[:, 4:5], 1.0 / d, None, OP.mult)
        p.v.tensor_scalar(st[:, 6:7], st[:, 5:6], 1.0 / d, None, OP.mult)
        p.v.tensor_scalar(st[:, 7:8], st[:, 0:1], st[:, 0:1], None, OP.mult)
        p.v.tensor_tensor(st[:, 1:2], st[:, 6:7], st[:, 7:8], OP.subtract)
        p.s.activation(st[:, 2:3], st[:, 1:2], AF.Ln, bias=eps_ap[0:P, 0:1])
        p.s.activation(st[:, 3:4], st[:, 2:3], AF.Exp, scale=-0.5)
        p.v.tensor_scalar(out_tiles[k], xk, st[:, 0:1], st[:, 3:4],
                          OP.subtract, OP.mult)


def emit_transpose(p, src_tiles, dst_fn, pst, idn, fgroups=8):
    for k, src in enumerate(src_tiles):
        for fg in range(fgroups):
            p.t.transpose(pst[:, fg * 128:(fg + 1) * 128],
                          src[:, fg * 128:(fg + 1) * 128], idn[:, :])
        p.v.tensor_copy(dst_fn(k), pst.rearrange("p (f t) -> p f t", f=fgroups))


def emit_identities(p, idb, idf, epsc):
    p.g.memset(epsc[:, :], LNEPS)
    for idn in (idb, idf):
        p.g.memset(idn[:, :], 0.0)
        p.g.affine_select(out=idn[:, :], in_=idn[:, :],
                          compare_op=OP.not_equal, fill=1.0, base=0,
                          pattern=[[-1, 128]], channel_multiplier=1)




class NS:
    pass


def alloc_common(nc, stack, ntt, tpp, ln_bufs=True):
    sb = lambda n, sh, dt: stack.enter_context(nc.sbuf_tensor(n, sh, dt))
    psal = lambda n, sh, dt: stack.enter_context(nc.psum_tensor(n, sh, dt))
    B = NS()
    B.ntt, B.tpp = ntt, tpp
    B.xt = sb("xt", [128, ntt * 1024], F32)
    if ln_bufs:
        B.xln = sb("xln", [128, ntt * 1024], BF16)
        B.xlnT = sb("xlnT", [128, 8 * tpp], BF16)
    B.stat = sb("stat", [128, 16 * ntt + 16], F32)
    B.sq = sb("sq", [128, 1024], F32)
    B.otm = sb("otm", [128, 2 * 1024], F32)
    B.oln = sb("oln", [128, 2 * 1024], F32)
    B.epsc = sb("epsc", [128, 1], F32)
    B.idb = sb("idb", [128, 128], BF16)
    B.idf = sb("idf", [128, 128], F32)
    B.pst = psal("pst", [128, 1024], BF16)
    B.pstf = psal("pstf", [128, 512], F32)
    B.psmm = [psal(f"psmm{i}", [128, 512], F32) for i in range(4)]
    return B


def emit_head(p, B, xr, pp):
    ntt, tpp = B.ntt, B.tpp
    p.y.dma_start(out=B.xt.rearrange("p (n d) -> p n d", d=1024),
                  in_=xr[:, pp * ntt:(pp + 1) * ntt, :])
    x_tiles = [B.xt[:, k * 1024:(k + 1) * 1024] for k in range(ntt)]
    ln_tiles = [B.xln[:, k * 1024:(k + 1) * 1024] for k in range(ntt)]
    emit_ln(p, x_tiles, ln_tiles, B.stat, B.sq, B.epsc)
    emit_transpose(p, ln_tiles,
                   lambda k: B.xlnT.rearrange("p (f t) -> p f t", f=8)[:, :, k * 128:(k + 1) * 128],
                   B.pst, B.idb)
    return x_tiles


def emit_tail(p, B, outT, xro, pp):
    ntt, tpp = B.ntt, B.tpp
    for k in range(ntt):
        par = k % 2
        om = B.otm[:, par * 1024:(par + 1) * 1024]
        ol = B.oln[:, par * 1024:(par + 1) * 1024]
        for half in range(2):
            for og4 in range(4):
                og = half * 4 + og4
                p.t.transpose(B.pstf[:, og4 * 128:(og4 + 1) * 128],
                              outT[:, og * tpp + k * 128:og * tpp + (k + 1) * 128],
                              B.idf[:, :])
            p.v.tensor_copy(
                om.rearrange("p (g t) -> p g t", g=8)[:, half * 4:(half + 1) * 4, :],
                B.pstf.rearrange("p (g t) -> p g t", g=4))
        emit_ln(p, [om], [ol], B.stat[:, 16 * ntt:], B.sq, B.epsc)
        p.v.tensor_tensor(ol, ol, B.xt[:, k * 1024:(k + 1) * 1024], OP.add)
        p.g.dma_start(out=xro[:, pp * ntt + k, :], in_=ol)


def emit_square_matmul(p, wT, rhs, psum, out_fn, kt_n=8, og_n=8, tpp=512,
                       evac=None):
    for og in range(og_n):
        ps = psum[og % 2]
        for kt in range(kt_n):
            p.t.matmul(ps[:, :tpp], wT[:, (og * kt_n + kt) * 128:(og * kt_n + kt + 1) * 128],
                       rhs[:, kt * tpp:(kt + 1) * tpp],
                       start=(kt == 0), stop=(kt == kt_n - 1))
        evac(og, ps)


def build_ffn0(tok=2048, tpp=512):
    nc = bass.Bass("TRN2", target_bir_lowering=False)
    DH = 4096
    OG1, KT1, KT2, OG3 = DH // 128, 1024 // 128, DH // 128, 1024 // 128
    npass, ntt = tok // tpp, tpp // 128
    assert tpp == 512

    x_in = nc.dram_tensor("x_in", [tok, 1024], F32, kind="ExternalInput").ap()
    w1 = nc.dram_tensor("w1", [OG1, 1024, 128], F32, kind="ExternalInput").ap()
    w2 = nc.dram_tensor("w2", [OG1, 1024, 128], F32, kind="ExternalInput").ap()
    h1 = nc.dram_tensor("h1", [OG1, DH, 128], F32, kind="ExternalInput").ap()
    h2 = nc.dram_tensor("h2", [OG1, DH, 128], F32, kind="ExternalInput").ap()
    w3 = nc.dram_tensor("w3", [OG3, DH, 128], F32, kind="ExternalInput").ap()
    scales = nc.dram_tensor("scales", [128, 4 * OG1 + OG3], F32, kind="ExternalInput").ap()
    x_out = nc.dram_tensor("x_out", [tok, 1024], F32, kind="ExternalOutput").ap()

    xr = x_in.rearrange("(n p) d -> p n d", p=128)
    xro = x_out.rearrange("(n p) d -> p n d", p=128)

    with ExitStack() as stack:
        sb = lambda n, sh, dt: stack.enter_context(nc.sbuf_tensor(n, sh, dt))
        psal = lambda n, sh, dt: stack.enter_context(nc.psum_tensor(n, sh, dt))
        xt = sb("xt", [128, ntt * 1024], F32)
        xln = sb("xln", [128, ntt * 1024], BF16)
        xlnT = sb("xlnT", [128, KT1 * tpp], BF16)
        hA = sb("hA", [128, OG1 * tpp], BF16)
        hB = sb("hB", [128, OG1 * tpp], BF16)
        tmpA = sb("tmpA", [128, 2 * tpp], F32)
        wsl = [sb(f"wsl{i}", [128, DH], BF16) for i in range(4)]
        scl = sb("scl", [128, 4 * OG1 + OG3], F32)
        stat = sb("stat", [128, 16 * ntt + 16], F32)
        sq = sb("sq", [128, 1024], F32)
        outT = sb("outT", [128, OG3 * tpp], F32)
        otm = sb("otm", [128, 2 * 1024], F32)
        oln = sb("oln", [128, 2 * 1024], F32)
        epsc = sb("epsc", [128, 1], F32)
        idb = sb("idb", [128, 128], BF16)
        idf = sb("idf", [128, 128], F32)
        pst = psal("pst", [128, 1024], BF16)
        pstf = psal("pstf", [128, 512], F32)
        psmm = [psal(f"psmm{i}", [128, 512], F32) for i in range(4)]

        p = Prog(nc)
        emit_identities(p, idb, idf, epsc)
        p.y.dma_start(out=scl[:, :], in_=scales[:, :])

        for pp in range(npass):
            p.y.dma_start(out=xt.rearrange("p (n d) -> p n d", d=1024),
                          in_=xr[:, pp * ntt:(pp + 1) * ntt, :])
            x_tiles = [xt[:, k * 1024:(k + 1) * 1024] for k in range(ntt)]
            ln_tiles = [xln[:, k * 1024:(k + 1) * 1024] for k in range(ntt)]
            emit_ln(p, x_tiles, ln_tiles, stat, sq, epsc)
            emit_transpose(p, ln_tiles,
                           lambda k: xlnT.rearrange("p (f t) -> p f t", f=8)[:, :, k * 128:(k + 1) * 128],
                           pst, idb)

            for og in range(OG1):
                pa, pb = psmm[2 * (og % 2)], psmm[2 * (og % 2) + 1]
                wa, wb = wsl[2 * (og % 2)], wsl[2 * (og % 2) + 1]
                p.g.dma_start(out=wa.rearrange("p (kt o) -> p kt o", o=128)[:, 0:8, :],
                              in_=w1[og].rearrange("(kt p) o -> p kt o", p=128))
                p.g.dma_start(out=wb.rearrange("p (kt o) -> p kt o", o=128)[:, 0:8, :],
                              in_=w2[og].rearrange("(kt p) o -> p kt o", p=128))
                for kt in range(KT1):
                    p.t.matmul(pa[:, :], wa[:, kt * 128:(kt + 1) * 128],
                               xlnT[:, kt * tpp:(kt + 1) * tpp],
                               start=(kt == 0), stop=(kt == KT1 - 1))
                for kt in range(KT1):
                    p.t.matmul(pb[:, :], wb[:, kt * 128:(kt + 1) * 128],
                               xlnT[:, kt * tpp:(kt + 1) * tpp],
                               start=(kt == 0), stop=(kt == KT1 - 1))
                tA = tmpA[:, (og % 2) * tpp:(og % 2 + 1) * tpp]
                p.s.activation(tA, pa[:, :], AF.Silu, scale=scl[:, og:og + 1])
                p.v.scalar_tensor_tensor(hA[:, og * tpp:(og + 1) * tpp], pb[:, :],
                                         scl[:, OG1 + og:OG1 + og + 1], tA,
                                         OP.mult, OP.mult)

            for stage, (wsrc, hin, hout, soff) in enumerate((
                    (h1, hA, hB, 2 * OG1), (h2, hB, hA, 3 * OG1))):
                for og in range(OG1):
                    pa, wa = psmm[og % 2], wsl[og % 2]
                    p.g.dma_start(out=wa.rearrange("p (kt o) -> p kt o", o=128),
                                  in_=wsrc[og].rearrange("(kt p) o -> p kt o", p=128))
                    for kt in range(KT2):
                        p.t.matmul(pa[:, :], wa[:, kt * 128:(kt + 1) * 128],
                                   hin[:, kt * tpp:(kt + 1) * tpp],
                                   start=(kt == 0), stop=(kt == KT2 - 1))
                    p.s.activation(hout[:, og * tpp:(og + 1) * tpp], pa[:, :],
                                   AF.Gelu, scale=scl[:, soff + og:soff + og + 1])

            for og in range(OG3):
                pa, wa = psmm[og % 2], wsl[og % 2]
                p.g.dma_start(out=wa.rearrange("p (kt o) -> p kt o", o=128),
                              in_=w3[og].rearrange("(kt p) o -> p kt o", p=128))
                for kt in range(KT2):
                    p.t.matmul(pa[:, :], wa[:, kt * 128:(kt + 1) * 128],
                               hA[:, kt * tpp:(kt + 1) * tpp],
                               start=(kt == 0), stop=(kt == KT2 - 1))
                p.v.tensor_scalar(outT[:, og * tpp:(og + 1) * tpp], pa[:, :],
                                  scl[:, 4 * OG1 + og:4 * OG1 + og + 1], None, OP.mult)

            for k in range(ntt):
                par = k % 2
                om = otm[:, par * 1024:(par + 1) * 1024]
                ol = oln[:, par * 1024:(par + 1) * 1024]
                for half in range(2):
                    for og4 in range(4):
                        og = half * 4 + og4
                        p.t.transpose(pstf[:, og4 * 128:(og4 + 1) * 128],
                                      outT[:, og * tpp + k * 128:og * tpp + (k + 1) * 128],
                                      idf[:, :])
                    p.v.tensor_copy(
                        om.rearrange("p (g t) -> p g t", g=8)[:, half * 4:(half + 1) * 4, :],
                        pstf.rearrange("p (g t) -> p g t", g=4))
                emit_ln(p, [om], [ol], stat[:, 16 * ntt:], sq, epsc)
                p.v.tensor_tensor(ol, ol, xt[:, k * 1024:(k + 1) * 1024], OP.add)
                p.g.dma_start(out=xro[:, pp * ntt + k, :], in_=ol)

        with nc.Block() as block:
            p.emit(block, stack)

    return nc


def _lhsT_tiles(w):
    O, K = w.shape
    return np.ascontiguousarray(w.reshape(O // 128, 128, K).transpose(0, 2, 1))


def _pack_scales(*scales):
    cols = [np.ascontiguousarray(s.reshape(-1, 128).T) for s in scales]
    return np.ascontiguousarray(np.concatenate(cols, axis=1))


def ffn0_inputs(p):
    return {
        "w1": _lhsT_tiles(p["w1_w"]), "w2": _lhsT_tiles(p["w2_w"]),
        "h1": _lhsT_tiles(p["h1_w"]), "h2": _lhsT_tiles(p["h2_w"]),
        "w3": _lhsT_tiles(p["w3_w"]),
        "scales": _pack_scales(p["w1_s"], p["w2_s"], p["h1_s"], p["h2_s"], p["w3_s"]),
    }


def run_ffn0(x_full, p, tpp=512):
    xf = np.ascontiguousarray(x_full.reshape(16384, 1024).astype(np.float32))
    tok = xf.shape[0] // NCORES
    nc = build_ffn0(tok=tok, tpp=tpp)
    base = ffn0_inputs(p)
    in_maps = []
    for c in range(NCORES):
        m = dict(base)
        m["x_in"] = xf[c * tok:(c + 1) * tok]
        in_maps.append(m)
    res = run_bass_kernel_spmd(nc, in_maps, core_ids=list(range(NCORES)))
    out = np.concatenate([res.results[c]["x_out"] for c in range(NCORES)], axis=0)
    return out.reshape(2, 8192, 1024)




def build_rwkv(tok=2048, tpp=512):
    nc = bass.Bass("TRN2", target_bir_lowering=False)
    npass, ntt = tok // tpp, tpp // 128
    assert tpp == 512

    x_in = nc.dram_tensor("x_in", [tok, 1024], F32, kind="ExternalInput").ap()
    halo = nc.dram_tensor("halo", [1, 1024], F32, kind="ExternalInput").ap()
    wr = nc.dram_tensor("wr", [128, 8192], F32, kind="ExternalInput").ap()
    wk = nc.dram_tensor("wk", [128, 8192], F32, kind="ExternalInput").ap()
    wv = nc.dram_tensor("wv", [128, 8192], F32, kind="ExternalInput").ap()
    mus = nc.dram_tensor("mus", [128, 16], F32, kind="ExternalInput").ap()
    x_out = nc.dram_tensor("x_out", [tok, 1024], F32, kind="ExternalOutput").ap()

    xr_ = x_in.rearrange("(n p) d -> p n d", p=128)
    xro = x_out.rearrange("(n p) d -> p n d", p=128)

    with ExitStack() as stack:
        sb = lambda n, sh, dt: stack.enter_context(nc.sbuf_tensor(n, sh, dt))
        B = alloc_common(nc, stack, ntt, tpp)
        wrT = sb("wrT", [128, 8192], BF16)
        wkT = sb("wkT", [128, 8192], BF16)
        wvT = sb("wvT", [128, 8192], BF16)
        muc = sb("muc", [128, 16], F32)
        xprevT = sb("xprevT", [128, 8 * tpp], F32)
        mixr = sb("mixr", [128, 8 * tpp], BF16)
        mixk = sb("mixk", [128, 8 * tpp], BF16)
        dT = sb("dT", [128, tpp], F32)
        rT = sb("rT", [128, 8 * tpp], F32)
        kact = sb("kact", [128, 8 * tpp], BF16)
        relu_t = sb("relu_t", [128, 2 * tpp], F32)
        outT = sb("outT", [128, 8 * tpp], F32)
        carry = sb("carry", [128, 8], F32)
        hx = sb("hx", [1, 1024], F32)
        hln = sb("hln", [1, 1024], BF16)

        p = Prog(nc)
        emit_identities(p, B.idb, B.idf, B.epsc)
        for w_ap, wt in ((wr, wrT), (wk, wkT), (wv, wvT)):
            p.g.dma_start(out=wt[:, :], in_=w_ap[:, :])
        p.y.dma_start(out=muc[:, :], in_=mus[:, :])
        p.y.dma_start(out=hx[:, :], in_=halo[:, :])
        emit_ln(p, [hx[:, :]], [hln[:, :]], B.stat[:, 16 * ntt:], B.sq, B.epsc)
        for fg in range(8):
            p.t.transpose(B.pst[:, 2 * fg:2 * fg + 1], hln[:, fg * 128:(fg + 1) * 128],
                          B.idb[0:1, 0:1])
        p.v.tensor_copy(carry[:, :], B.pst.rearrange("p (a b) -> p a b", b=2)[:, 0:8, 0])

        for pp in range(npass):
            emit_head(p, B, xr_, pp)
            for fg in range(8):
                blk = slice(fg * tpp, (fg + 1) * tpp)
                p.v.tensor_copy(xprevT[:, fg * tpp + 1:(fg + 1) * tpp],
                                B.xlnT[:, fg * tpp:(fg + 1) * tpp - 1])
                p.v.tensor_copy(xprevT[:, fg * tpp:fg * tpp + 1], carry[:, fg:fg + 1])
                p.v.tensor_copy(carry[:, fg:fg + 1], B.xlnT[:, (fg + 1) * tpp - 1:(fg + 1) * tpp])
                p.v.tensor_tensor(dT[:, :], B.xlnT[:, blk], xprevT[:, blk], OP.subtract)
                p.v.scalar_tensor_tensor(mixr[:, blk], dT[:, :], muc[:, fg:fg + 1],
                                         xprevT[:, blk], OP.mult, OP.add)
                p.v.scalar_tensor_tensor(mixk[:, blk], dT[:, :], muc[:, 8 + fg:9 + fg],
                                         xprevT[:, blk], OP.mult, OP.add)

            def evac_r(og, ps):
                p.s.activation(rT[:, og * tpp:(og + 1) * tpp], ps[:, :tpp], AF.Sigmoid)
            emit_square_matmul(p, wrT, mixr, B.psmm, None, evac=evac_r, tpp=tpp)

            def evac_k(og, ps):
                rl = relu_t[:, (og % 2) * tpp:(og % 2 + 1) * tpp]
                p.s.activation(rl, ps[:, :tpp], AF.Relu)
                p.s.activation(kact[:, og * tpp:(og + 1) * tpp], rl, AF.Square)
            emit_square_matmul(p, wkT, mixk, B.psmm, None, evac=evac_k, tpp=tpp)

            def evac_v(og, ps):
                p.v.tensor_tensor(outT[:, og * tpp:(og + 1) * tpp], ps[:, :tpp],
                                  rT[:, og * tpp:(og + 1) * tpp], OP.mult)
            emit_square_matmul(p, wvT, kact, B.psmm, None, evac=evac_v, tpp=tpp)

            emit_tail(p, B, outT, xro, pp)

        with nc.Block() as block:
            p.emit(block, stack)
    return nc


def _lhsT_resident(w):
    O, K = w.shape
    wT = np.ascontiguousarray(w.T)
    a = wT.reshape(K // 128, 128, O // 128, 128).transpose(1, 2, 0, 3)
    return np.ascontiguousarray(a.reshape(128, (K // 128) * (O // 128) * 128))


def rwkv_inputs(pm):
    return {
        "wr": _lhsT_resident(pm["W_r"]), "wk": _lhsT_resident(pm["W_k"]),
        "wv": _lhsT_resident(pm["W_v"]),
        "mus": np.concatenate([pm["mu_r"].reshape(8, 128).T, pm["mu_k"].reshape(8, 128).T], axis=1).astype(np.float32),
    }


def run_rwkv(x_full, pm, tpp=512):
    xf = np.ascontiguousarray(x_full.reshape(16384, 1024).astype(np.float32))
    tok = xf.shape[0] // NCORES
    nc = build_rwkv(tok=tok, tpp=tpp)
    base = rwkv_inputs(pm)
    in_maps = []
    for c in range(NCORES):
        m = dict(base)
        m["x_in"] = xf[c * tok:(c + 1) * tok]
        if c % 4 == 0:
            m["halo"] = np.zeros((1, 1024), np.float32)
        else:
            m["halo"] = xf[c * tok - 1:c * tok]
        in_maps.append(m)
    res = run_bass_kernel_spmd(nc, in_maps, core_ids=list(range(NCORES)))
    out = np.concatenate([res.results[c]["x_out"] for c in range(NCORES)], axis=0)
    return out.reshape(2, 8192, 1024)




def build_moe(tok=2048, tpp=512):
    nc = bass.Bass("TRN2", target_bir_lowering=False)
    npass, ntt = tok // tpp, tpp // 128
    NE, HID = 16, 512
    assert tpp == 512

    x_in = nc.dram_tensor("x_in", [tok, 1024], F32, kind="ExternalInput").ap()
    gw = nc.dram_tensor("gw", [128, 8 * 16], F32, kind="ExternalInput").ap()
    w1 = nc.dram_tensor("w1", [NE, 128, 4096], F32, kind="ExternalInput").ap()
    w3 = nc.dram_tensor("w3", [NE, 128, 4096], F32, kind="ExternalInput").ap()
    w2 = nc.dram_tensor("w2", [NE, 128, 4096], F32, kind="ExternalInput").ap()
    sw1 = nc.dram_tensor("sw1", [128, 4096], F32, kind="ExternalInput").ap()
    sw3 = nc.dram_tensor("sw3", [128, 4096], F32, kind="ExternalInput").ap()
    sw2 = nc.dram_tensor("sw2", [128, 4096], F32, kind="ExternalInput").ap()
    x_out = nc.dram_tensor("x_out", [tok, 1024], F32, kind="ExternalOutput").ap()

    xr_ = x_in.rearrange("(n p) d -> p n d", p=128)
    xro = x_out.rearrange("(n p) d -> p n d", p=128)

    with ExitStack() as stack:
        sb = lambda n, sh, dt: stack.enter_context(nc.sbuf_tensor(n, sh, dt))
        B = alloc_common(nc, stack, ntt, tpp)
        gwc = sb("gwc", [128, 8 * 16], BF16)
        esel = sb("esel", [16, 16 * 128], F32)
        logit = sb("logit", [128, ntt * 16], F32)
        gstat = sb("gstat", [128, ntt * 8], F32)
        rem = sb("rem", [128, ntt * 16], F32)
        selm = sb("selm", [128, ntt * 16], F32)
        tmpm = sb("tmpm", [128, ntt * 16], F32)
        cb = sb("cb", [128, ntt * 16], F32)
        cbT = sb("cbT", [16, ntt * 128], F32)
        cbB = sb("cbB", [128, tpp], F32)
        wsa = [sb(f"wsa{i}", [128, 4096], BF16) for i in range(2)]
        wsb = [sb(f"wsb{i}", [128, 4096], BF16) for i in range(2)]
        wsc = [sb(f"wsc{i}", [128, 4096], BF16) for i in range(2)]
        swc = [sb(f"swc{i}", [128, 4096], BF16) for i in range(3)]
        act = sb("act", [128, 8 * 4 * tpp], BF16)
        acts = sb("acts", [128, 4 * tpp], BF16)
        tmpa = sb("tmpa", [128, 2 * tpp], F32)
        tmpu = sb("tmpu", [128, 2 * tpp], F32)
        outT = sb("outT", [128, 8 * tpp], F32)

        p = Prog(nc)
        emit_identities(p, B.idb, B.idf, B.epsc)
        p.g.dma_start(out=gwc[:, :], in_=gw[:, :])
        for s_ap, st in ((sw1, swc[0]), (sw3, swc[1]), (sw2, swc[2])):
            p.g.dma_start(out=st[:, :], in_=s_ap[:, :])
        p.g.memset(esel[:, :], 0.0)
        p.g.affine_select(out=esel.rearrange("p (e m) -> p e m", e=16),
                          in_=esel.rearrange("p (e m) -> p e m", e=16),
                          compare_op=OP.not_equal, fill=1.0, base=0,
                          pattern=[[-1, 16], [0, 128]], channel_multiplier=1)

        for pp in range(npass):
            emit_head(p, B, xr_, pp)

            for k in range(ntt):
                ps = B.psmm[2]
                for kt in range(8):
                    p.t.matmul(ps[:, 0:16],
                               B.xlnT[:, kt * tpp + k * 128:kt * tpp + (k + 1) * 128],
                               gwc[:, kt * 16:(kt + 1) * 16],
                               start=(kt == 0), stop=(kt == 7))
                p.v.tensor_copy(logit[:, k * 16:(k + 1) * 16], ps[:, 0:16])
            for k in range(ntt):
                lg = logit[:, k * 16:(k + 1) * 16]
                st = gstat[:, k * 8:(k + 1) * 8]
                p.v.tensor_reduce(st[:, 0:1], lg, AX.X, OP.max)
                p.v.tensor_scalar(st[:, 1:2], st[:, 0:1], -1.0, None, OP.mult)
                p.s.activation(lg, lg, AF.Exp, bias=st[:, 1:2])
                p.v.tensor_reduce(st[:, 2:3], lg, AX.X, OP.add)
                p.v.reciprocal(st[:, 3:4], st[:, 2:3])
                p.v.tensor_scalar(lg, lg, st[:, 3:4], None, OP.mult)
                rm = rem[:, k * 16:(k + 1) * 16]
                sl = selm[:, k * 16:(k + 1) * 16]
                tp = tmpm[:, k * 16:(k + 1) * 16]
                p.v.tensor_copy(rm, lg)
                p.v.memset(sl, 0.0)
                for it in range(4):
                    p.v.tensor_reduce(st[:, 4:5], rm, AX.X, OP.max)
                    p.v.tensor_scalar(tp, rm, st[:, 4:5], None, OP.is_equal)
                    p.v.tensor_tensor(sl, sl, tp, OP.add)
                    if it < 3:
                        p.v.tensor_tensor(tp, tp, rm, OP.mult)
                        p.v.tensor_tensor(rm, rm, tp, OP.subtract)
                cbk = cb[:, k * 16:(k + 1) * 16]
                p.v.tensor_tensor(cbk, lg, sl, OP.mult)
                p.v.tensor_reduce(st[:, 5:6], cbk, AX.X, OP.add)
                p.v.reciprocal(st[:, 6:7], st[:, 5:6])
                p.v.tensor_scalar(cbk, cbk, st[:, 6:7], None, OP.mult)
                p.t.transpose(B.pstf[0:16, 0:128], cbk, B.idf[:, :])
                p.v.tensor_copy(cbT[:, k * 128:(k + 1) * 128], B.pstf[0:16, 0:128])

            for og in range(4):
                psA, psB = B.psmm[0], B.psmm[1]
                for kt in range(8):
                    p.t.matmul(psA[:, :], swc[0][:, (og * 8 + kt) * 128:(og * 8 + kt + 1) * 128],
                               B.xlnT[:, kt * tpp:(kt + 1) * tpp],
                               start=(kt == 0), stop=(kt == 7))
                for kt in range(8):
                    p.t.matmul(psB[:, :], swc[1][:, (og * 8 + kt) * 128:(og * 8 + kt + 1) * 128],
                               B.xlnT[:, kt * tpp:(kt + 1) * tpp],
                               start=(kt == 0), stop=(kt == 7))
                ta = tmpa[:, (og % 2) * tpp:(og % 2 + 1) * tpp]
                p.s.activation(ta, psA[:, :], AF.Silu)
                p.v.tensor_tensor(acts[:, og * tpp:(og + 1) * tpp], psB[:, :], ta, OP.mult)

            for half in range(2):
                for ee in range(8):
                    e = half * 8 + ee
                    wa, wb = wsa[e % 2], wsb[e % 2]
                    p.g.dma_start(out=wa[:, :], in_=w1[e])
                    p.g.dma_start(out=wb[:, :], in_=w3[e])
                    p.t.matmul(B.pstf[:, 0:tpp], esel[:, e * 128:(e + 1) * 128],
                               cbT[:, 0:tpp], start=True, stop=True)
                    p.v.tensor_copy(cbB[:, :], B.pstf[:, 0:tpp])
                    for og in range(4):
                        psA, psB = B.psmm[0], B.psmm[1]
                        for kt in range(8):
                            p.t.matmul(psA[:, :], wa[:, (og * 8 + kt) * 128:(og * 8 + kt + 1) * 128],
                                       B.xlnT[:, kt * tpp:(kt + 1) * tpp],
                                       start=(kt == 0), stop=(kt == 7))
                        for kt in range(8):
                            p.t.matmul(psB[:, :], wb[:, (og * 8 + kt) * 128:(og * 8 + kt + 1) * 128],
                                       B.xlnT[:, kt * tpp:(kt + 1) * tpp],
                                       start=(kt == 0), stop=(kt == 7))
                        ta = tmpa[:, (og % 2) * tpp:(og % 2 + 1) * tpp]
                        tu = tmpu[:, (og % 2) * tpp:(og % 2 + 1) * tpp]
                        p.s.activation(ta, psA[:, :], AF.Silu)
                        p.v.tensor_tensor(tu, psB[:, :], ta, OP.mult)
                        p.v.tensor_tensor(act[:, (ee * 4 + og) * tpp:(ee * 4 + og + 1) * tpp],
                                          tu, cbB[:, :], OP.mult)

                for ogg in range(2):
                    for ee in range(8):
                        e = half * 8 + ee
                        wc = wsc[ee % 2]
                        p.g.dma_start(out=wc[:, 0:2048],
                                      in_=w2[e][:, ogg * 2048:(ogg + 1) * 2048])
                        for og4 in range(4):
                            ps = B.psmm[og4]
                            for kt in range(4):
                                stop = (half == 0 and ee == 7 and kt == 3)
                                p.t.matmul(ps[:, :], wc[:, (og4 * 4 + kt) * 128:(og4 * 4 + kt + 1) * 128],
                                           act[:, (ee * 4 + kt) * tpp:(ee * 4 + kt + 1) * tpp],
                                           start=(ee == 0 and kt == 0), stop=stop)
                    for og4 in range(4):
                        og = ogg * 4 + og4
                        ps = B.psmm[og4]
                        if half == 0:
                            p.v.tensor_copy(outT[:, og * tpp:(og + 1) * tpp], ps[:, :])
                        else:
                            for kt in range(4):
                                p.t.matmul(ps[:, :], swc[2][:, (og * 4 + kt) * 128:(og * 4 + kt + 1) * 128],
                                           acts[:, kt * tpp:(kt + 1) * tpp],
                                           start=False, stop=(kt == 3))
                            p.v.tensor_tensor(outT[:, og * tpp:(og + 1) * tpp],
                                              outT[:, og * tpp:(og + 1) * tpp], ps[:, :], OP.add)

            emit_tail(p, B, outT, xro, pp)

        with nc.Block() as block:
            p.emit(block, stack)
    return nc


def moe_inputs(pm):
    def stack_lhsT(ws):
        return np.stack([_lhsT_resident(w) for w in ws], axis=0)
    gwT = pm["gate_w"].T
    gpack = np.ascontiguousarray(gwT.reshape(8, 128, 16).transpose(1, 0, 2).reshape(128, 128))
    return {
        "gw": gpack.astype(np.float32),
        "w1": stack_lhsT(pm["w1"]), "w3": stack_lhsT(pm["w3"]), "w2": stack_lhsT(pm["w2"]),
        "sw1": _lhsT_resident(pm["sw1"]), "sw3": _lhsT_resident(pm["sw3"]),
        "sw2": _lhsT_resident(pm["sw2"]),
    }


def run_moe(x_full, pm, tpp=512):
    xf = np.ascontiguousarray(x_full.reshape(16384, 1024).astype(np.float32))
    tok = xf.shape[0] // NCORES
    nc = build_moe(tok=tok, tpp=tpp)
    base = moe_inputs(pm)
    in_maps = []
    for c in range(NCORES):
        m = dict(base)
        m["x_in"] = xf[c * tok:(c + 1) * tok]
        in_maps.append(m)
    res = run_bass_kernel_spmd(nc, in_maps, core_ids=list(range(NCORES)))
    out = np.concatenate([res.results[c]["x_out"] for c in range(NCORES)], axis=0)
    return out.reshape(2, 8192, 1024)




def build_l1a_qkv(tok=2048, tpp=512):
    nc = bass.Bass("TRN2", target_bir_lowering=False)
    npass, ntt = tok // tpp, tpp // 128
    assert tpp == 512

    x_in = nc.dram_tensor("x_in", [tok, 1024], F32, kind="ExternalInput").ap()
    wq = nc.dram_tensor("wq", [128, 8192], F32, kind="ExternalInput").ap()
    wk = nc.dram_tensor("wk", [128, 8192], F32, kind="ExternalInput").ap()
    wvr = nc.dram_tensor("wvr", [128, 8192], F32, kind="ExternalInput").ap()
    wkr = nc.dram_tensor("wkr", [128, 8192], F32, kind="ExternalInput").ap()
    qT_o = nc.dram_tensor("qT_o", [128, 8 * tok], BF16, kind="ExternalOutput").ap()
    kT_o = nc.dram_tensor("kT_o", [128, 8 * tok], BF16, kind="ExternalOutput").ap()
    ktm_o = nc.dram_tensor("ktm_o", [tok, 1024], BF16, kind="ExternalOutput").ap()
    vtm_o = nc.dram_tensor("vtm_o", [tok, 1024], BF16, kind="ExternalOutput").ap()
    S_o = nc.dram_tensor("S_o", [128, 8 * 128], F32, kind="ExternalOutput").ap()
    Z_o = nc.dram_tensor("Z_o", [128, 8], F32, kind="ExternalOutput").ap()

    xr_ = x_in.rearrange("(n p) d -> p n d", p=128)
    ktm_r = ktm_o.rearrange("(n p) d -> p n d", p=128)
    vtm_r = vtm_o.rearrange("(n p) d -> p n d", p=128)

    with ExitStack() as stack:
        sb = lambda n, sh, dt: stack.enter_context(nc.sbuf_tensor(n, sh, dt))
        B = alloc_common(nc, stack, ntt, tpp)
        wqc = sb("wqc", [128, 8192], BF16)
        wkc = sb("wkc", [128, 8192], BF16)
        wvrc = sb("wvrc", [128, 8192], BF16)
        wkrc = sb("wkrc", [128, 8192], BF16)
        qT = sb("qT", [128, 8 * tpp], BF16)
        kT = sb("kT", [128, 8 * tpp], BF16)
        ktm = sb("ktm", [128, ntt * 1024], BF16)
        vtm = sb("vtm", [128, ntt * 1024], BF16)
        Ssb = sb("Ssb", [128, 8 * 128], F32)
        Zsb = sb("Zsb", [128, 8], F32)
        ones = sb("ones", [128, 1], BF16)

        p = Prog(nc)
        emit_identities(p, B.idb, B.idf, B.epsc)
        p.g.memset(ones[:, :], 1.0)
        p.v.memset(Ssb[:, :], 0.0)
        p.v.memset(Zsb[:, :], 0.0)
        for w_ap, wt in ((wq, wqc), (wk, wkc), (wvr, wvrc), (wkr, wkrc)):
            p.g.dma_start(out=wt[:, :], in_=w_ap[:, :])

        for pp in range(npass):
            emit_head(p, B, xr_, pp)
            for name, wt, dst, hbm in (("q", wqc, qT, qT_o), ("k", wkc, kT, kT_o)):
                for og in range(8):
                    ps = B.psmm[og % 2]
                    for kt in range(8):
                        p.t.matmul(ps[:, :], wt[:, (og * 8 + kt) * 128:(og * 8 + kt + 1) * 128],
                                   B.xlnT[:, kt * tpp:(kt + 1) * tpp],
                                   start=(kt == 0), stop=(kt == 7))
                    p.v.tensor_copy(dst[:, og * tpp:(og + 1) * tpp], ps[:, :])
                p.g.dma_start(out=hbm.rearrange("p (h t) -> p h t", h=8)[:, :, pp * tpp:(pp + 1) * tpp],
                              in_=dst.rearrange("p (h t) -> p h t", h=8))
            for name, wt, dst, hbm in (("kt", wkrc, ktm, ktm_r), ("vt", wvrc, vtm, vtm_r)):
                for k in range(ntt):
                    for oc in range(2):
                        ps = B.psmm[2 + oc]
                        for kt in range(8):
                            p.t.matmul(ps[:, :], B.xlnT[:, kt * tpp + k * 128:kt * tpp + (k + 1) * 128],
                                       wt[:, kt * 1024 + oc * 512:kt * 1024 + (oc + 1) * 512],
                                       start=(kt == 0), stop=(kt == 7))
                        p.v.tensor_copy(dst[:, k * 1024 + oc * 512:k * 1024 + (oc + 1) * 512], ps[:, :])
                    p.g.dma_start(out=hbm[:, pp * ntt + k, :], in_=dst[:, k * 1024:(k + 1) * 1024])
            for k in range(ntt):
                for h in range(8):
                    ps = B.psmm[h % 2]
                    p.t.matmul(ps[:, 0:128], ktm[:, k * 1024 + h * 128:k * 1024 + (h + 1) * 128],
                               vtm[:, k * 1024 + h * 128:k * 1024 + (h + 1) * 128],
                               start=True, stop=True)
                    p.v.tensor_tensor(Ssb[:, h * 128:(h + 1) * 128], Ssb[:, h * 128:(h + 1) * 128],
                                      ps[:, 0:128], OP.add)
                ps = B.psmm[2]
                for h in range(8):
                    p.t.matmul(ps[:, 2 * h:2 * h + 1],
                               ktm[:, k * 1024 + h * 128:k * 1024 + (h + 1) * 128],
                               ones[:, :], start=True, stop=True)
                p.v.tensor_tensor(Zsb[:, :], Zsb[:, :],
                                  ps.rearrange("p (a b) -> p a b", b=2)[:, 0:8, 0], OP.add)

        p.g.dma_start(out=S_o[:, :], in_=Ssb[:, :])
        p.g.dma_start(out=Z_o[:, :], in_=Zsb[:, :])

        with nc.Block() as block:
            p.emit(block, stack)
    return nc


def build_l1a_out(tok=2048, tpp=512):
    nc = bass.Bass("TRN2", target_bir_lowering=False)
    npass, ntt = tok // tpp, tpp // 128
    assert tpp == 512

    x_in = nc.dram_tensor("x_in", [tok, 1024], F32, kind="ExternalInput").ap()
    qT_i = nc.dram_tensor("qT_i", [128, 8 * tok], BF16, kind="ExternalInput").ap()
    kT_i = nc.dram_tensor("kT_i", [128, 8 * tok], BF16, kind="ExternalInput").ap()
    ktm_i = nc.dram_tensor("ktm_i", [tok, 1024], BF16, kind="ExternalInput").ap()
    vtm_i = nc.dram_tensor("vtm_i", [tok, 1024], BF16, kind="ExternalInput").ap()
    S_all = nc.dram_tensor("S_all", [8, 128, 8 * 128], F32, kind="ExternalInput").ap()
    Z_all = nc.dram_tensor("Z_all", [8, 128, 8], F32, kind="ExternalInput").ap()
    pmask = nc.dram_tensor("pmask", [128, 8], F32, kind="ExternalInput").ap()
    wo = nc.dram_tensor("wo", [128, 8192], F32, kind="ExternalInput").ap()
    nw = nc.dram_tensor("nw", [1, 128], F32, kind="ExternalInput").ap()
    x_out = nc.dram_tensor("x_out", [tok, 1024], F32, kind="ExternalOutput").ap()

    xr_ = x_in.rearrange("(n p) d -> p n d", p=128)
    xro = x_out.rearrange("(n p) d -> p n d", p=128)
    ktm_r = ktm_i.rearrange("(n p) d -> p n d", p=128)
    vtm_r = vtm_i.rearrange("(n p) d -> p n d", p=128)

    with ExitStack() as stack:
        sb = lambda n, sh, dt: stack.enter_context(nc.sbuf_tensor(n, sh, dt))
        B = alloc_common(nc, stack, ntt, tpp)
        woc = sb("woc", [128, 8192], BF16)
        Ssb = sb("Ssb", [128, 8 * 128], F32)
        Sbf = sb("Sbf", [128, 128], BF16)
        Zsb = sb("Zsb", [128, 8], F32)
        Zbf = sb("Zbf", [128, 1], BF16)
        Stmp = sb("Stmp", [128, 8 * 128], F32)
        Ztmp = sb("Ztmp", [128, 8], F32)
        msk = sb("msk", [128, 8], F32)
        triu = sb("triu", [128, 128], BF16)
        ones = sb("ones", [128, 1], BF16)
        nwb = sb("nwb", [128, 128], F32)
        nwrow = sb("nwrow", [1, 128], F32)
        onesrow = sb("onesrow", [1, 128], F32)
        qTl = sb("qTl", [128, 8 * tpp], BF16)
        kTl = sb("kTl", [128, 8 * tpp], BF16)
        ktm = sb("ktm", [128, ntt * 1024], BF16)
        vtm = sb("vtm", [128, ntt * 1024], BF16)
        sTm = sb("sTm", [128, 2 * 128], BF16)
        osb = sb("osb", [128, 2 * 128], F32)
        dstat = sb("dstat", [128, 16], F32)
        onb = sb("onb", [128, 2 * 128], BF16)
        oT = sb("oT", [128, 8 * tpp], BF16)
        outT = sb("outT", [128, 8 * tpp], F32)

        p = Prog(nc)
        emit_identities(p, B.idb, B.idf, B.epsc)
        p.g.memset(ones[:, :], 1.0)
        p.g.memset(triu[:, :], 1.0)
        p.g.affine_select(out=triu[:, :], in_=triu[:, :],
                          compare_op=OP.is_ge, fill=0.0, base=0,
                          pattern=[[1, 128]], channel_multiplier=-1)
        p.g.dma_start(out=woc[:, :], in_=wo[:, :])
        p.y.dma_start(out=msk[:, :], in_=pmask[:, :])
        p.y.dma_start(out=nwrow[:, :], in_=nw[:, :])
        p.g.memset(onesrow[:, :], 1.0)
        p.t.matmul(B.pstf[:, 0:128], onesrow[:, :], nwrow[:, :], start=True, stop=True)
        p.v.tensor_copy(nwb[:, :], B.pstf[:, 0:128])
        p.v.memset(Ssb[:, :], 0.0)
        p.v.memset(Zsb[:, :], 0.0)
        for r in range(8):
            p.g.dma_start(out=Stmp[:, :], in_=S_all[r])
            p.g.dma_start(out=Ztmp[:, :], in_=Z_all[r])
            p.v.scalar_tensor_tensor(Ssb[:, :], Stmp[:, :], msk[:, r:r + 1], Ssb[:, :],
                                     OP.mult, OP.add)
            p.v.scalar_tensor_tensor(Zsb[:, :], Ztmp[:, :], msk[:, r:r + 1], Zsb[:, :],
                                     OP.mult, OP.add)

        for pp in range(npass):
            emit_head(p, B, xr_, pp)
            p.y.dma_start(out=qTl.rearrange("p (h t) -> p h t", h=8),
                          in_=qT_i.rearrange("p (h t) -> p h t", h=8)[:, :, pp * tpp:(pp + 1) * tpp])
            p.y.dma_start(out=kTl.rearrange("p (h t) -> p h t", h=8),
                          in_=kT_i.rearrange("p (h t) -> p h t", h=8)[:, :, pp * tpp:(pp + 1) * tpp])
            p.y.dma_start(out=ktm.rearrange("p (n d) -> p n d", d=1024),
                          in_=ktm_r[:, pp * ntt:(pp + 1) * ntt, :])
            p.y.dma_start(out=vtm.rearrange("p (n d) -> p n d", d=1024),
                          in_=vtm_r[:, pp * ntt:(pp + 1) * ntt, :])
            for k in range(ntt):
                for h in range(8):
                    par = h % 2
                    qblk = qTl[:, h * tpp + k * 128:h * tpp + (k + 1) * 128]
                    kblk = kTl[:, h * tpp + k * 128:h * tpp + (k + 1) * 128]
                    ktb = ktm[:, k * 1024 + h * 128:k * 1024 + (h + 1) * 128]
                    vtb = vtm[:, k * 1024 + h * 128:k * 1024 + (h + 1) * 128]
                    Sh = Ssb[:, h * 128:(h + 1) * 128]
                    ps = B.psmm[par]
                    p.t.matmul(ps[:, 0:128], kblk, qblk, start=True, stop=True)
                    stm = sTm[:, par * 128:(par + 1) * 128]
                    p.v.tensor_tensor(stm, ps[:, 0:128], triu[:, :], OP.mult)
                    p.v.tensor_copy(Sbf[:, :], Sh)
                    po = B.psmm[2 + par]
                    p.t.matmul(po[:, 0:128], stm, vtb, start=True, stop=False)
                    p.t.matmul(po[:, 0:128], qblk, Sbf[:, :], start=False, stop=True)
                    p.v.tensor_copy(Zbf[:, :], Zsb[:, h:h + 1])
                    p.t.matmul(ps[:, 192:193], stm, ones[:, :], start=True, stop=False)
                    p.t.matmul(ps[:, 192:193], qblk, Zbf[:, :], start=False, stop=True)
                    p.v.tensor_scalar(dstat[:, 0:1], ps[:, 192:193], -1.0, None, OP.mult)
                    p.v.tensor_tensor(dstat[:, 1:2], ps[:, 192:193], dstat[:, 0:1], OP.max)
                    p.v.tensor_scalar(dstat[:, 2:3], dstat[:, 1:2], 1.0, None, OP.max)
                    p.v.reciprocal(dstat[:, 3:4], dstat[:, 2:3])
                    ob = osb[:, par * 128:(par + 1) * 128]
                    p.v.tensor_scalar(ob, po[:, 0:128], dstat[:, 3:4], None, OP.mult)
                    p.v.scalar_tensor_tensor(B.sq[:, 0:128], ob, 1.0, ob, OP.mult, OP.mult,
                                             accum_out=dstat[:, 4:5])
                    p.v.tensor_scalar(dstat[:, 5:6], dstat[:, 4:5], 1.0 / 128, None, OP.mult)
                    p.s.activation(dstat[:, 6:7], dstat[:, 5:6], AF.Ln, bias=B.epsc[:, 0:1])
                    p.s.activation(dstat[:, 7:8], dstat[:, 6:7], AF.Exp, scale=-0.5)
                    onbb = onb[:, par * 128:(par + 1) * 128]
                    p.v.tensor_scalar(B.sq[:, 128:256], ob, dstat[:, 7:8], None, OP.mult)
                    p.v.tensor_tensor(onbb, B.sq[:, 128:256], nwb[:, :], OP.mult)
                    p.t.transpose(B.pst[:, par * 128:(par + 1) * 128], onbb, B.idb[:, :])
                    p.v.tensor_copy(oT.rearrange("p (h t) -> p h t", h=8)[:, h, k * 128:(k + 1) * 128],
                                    B.pst[:, par * 128:(par + 1) * 128])
                    p.t.matmul(ps[:, 0:128], ktb, vtb, start=True, stop=True)
                    p.v.tensor_tensor(Sh, Sh, ps[:, 0:128], OP.add)
                    p.t.matmul(ps[:, 128:129], ktb, ones[:, :], start=True, stop=True)
                    p.v.tensor_tensor(Zsb[:, h:h + 1], Zsb[:, h:h + 1], ps[:, 128:129], OP.add)
            for og in range(8):
                ps = B.psmm[og % 2]
                for kt in range(8):
                    p.t.matmul(ps[:, :], woc[:, (og * 8 + kt) * 128:(og * 8 + kt + 1) * 128],
                               oT[:, kt * tpp:(kt + 1) * tpp],
                               start=(kt == 0), stop=(kt == 7))
                p.v.tensor_copy(outT[:, og * tpp:(og + 1) * tpp], ps[:, :])
            emit_tail(p, B, outT, xro, pp)

        with nc.Block() as block:
            p.emit(block, stack)
    return nc


def _rhs_resident(w):
    wT = np.ascontiguousarray(w.T)
    F_, O = wT.shape
    a = wT.reshape(F_ // 128, 128, O).transpose(1, 0, 2)
    return np.ascontiguousarray(a.reshape(128, (F_ // 128) * O))


def run_l1_attn(x_full, pm, tpp=512):
    xf = np.ascontiguousarray(x_full.reshape(16384, 1024).astype(np.float32))
    tok = xf.shape[0] // NCORES
    ncA = build_l1a_qkv(tok=tok, tpp=tpp)
    baseA = {
        "wq": _lhsT_resident(pm["Wq"]), "wk": _lhsT_resident(pm["Wk"]),
        "wvr": _rhs_resident(pm["Wv"]), "wkr": _rhs_resident(pm["Wk"]),
    }
    in_maps = []
    for c in range(NCORES):
        m = dict(baseA)
        m["x_in"] = xf[c * tok:(c + 1) * tok]
        in_maps.append(m)
    resA = run_bass_kernel_spmd(ncA, in_maps, core_ids=list(range(NCORES)))
    S_all = np.stack([resA.results[c]["S_o"] for c in range(NCORES)], axis=0)
    Z_all = np.stack([resA.results[c]["Z_o"] for c in range(NCORES)], axis=0)
    ncB = build_l1a_out(tok=tok, tpp=tpp)
    baseB = {"wo": _lhsT_resident(pm["Wo"]), "S_all": S_all, "Z_all": Z_all,
             "nw": pm["norm_w"].reshape(1, 128).astype(np.float32)}
    in_maps = []
    for c in range(NCORES):
        m = dict(baseB)
        m["x_in"] = xf[c * tok:(c + 1) * tok]
        m["qT_i"] = resA.results[c]["qT_o"]
        m["kT_i"] = resA.results[c]["kT_o"]
        m["ktm_i"] = resA.results[c]["ktm_o"]
        m["vtm_i"] = resA.results[c]["vtm_o"]
        pmask = np.zeros((128, 8), np.float32)
        for r in range((c // 4) * 4, c):
            pmask[:, r] = 1.0
        m["pmask"] = pmask
        in_maps.append(m)
    resB = run_bass_kernel_spmd(ncB, in_maps, core_ids=list(range(NCORES)))
    out = np.concatenate([resB.results[c]["x_out"] for c in range(NCORES)], axis=0)
    return out.reshape(2, 8192, 1024)




CH = 128
QS = 16


def emit_gla_state_update(p, B, S, kT, Lc, vtm_chunk, kfac, khf, khtm, gam, psg):
    for kb in range(8):
        Lkb = Lc[kb]
        p.v.tensor_scalar(kfac[:, :], Lkb, Lkb[:, CH - 1:CH], None, OP.subtract)
        p.s.activation(kfac[:, :], kfac[:, :], AF.Exp, scale=-1.0)
        p.v.tensor_tensor(khf[:, :], kT[kb], kfac[:, :], OP.mult)
        p.t.transpose(B.pst[:, 0:128], khf[:, :], B.idb[:, :])
        p.v.tensor_copy(khtm[:, kb * 128:(kb + 1) * 128], B.pst[:, 0:128])
        p.s.activation(gam[:, kb:kb + 1], Lkb[:, CH - 1:CH], AF.Exp)
    for kb in range(8):
        p.v.tensor_scalar(S[:, kb * 1024:(kb + 1) * 1024],
                          S[:, kb * 1024:(kb + 1) * 1024],
                          gam[:, kb:kb + 1], None, OP.mult)
        for vh in range(2):
            ps = psg[(kb * 2 + vh) % 2]
            p.t.matmul(ps[:, :], khtm[:, kb * 128:(kb + 1) * 128],
                       vtm_chunk[:, vh * 512:(vh + 1) * 512], start=True, stop=True)
            p.v.tensor_tensor(S[:, kb * 1024 + vh * 512:kb * 1024 + (vh + 1) * 512],
                              S[:, kb * 1024 + vh * 512:kb * 1024 + (vh + 1) * 512],
                              ps[:, :], OP.add)


def build_l2a_qkv(tok=2048, tpp=512):
    nc = bass.Bass("TRN2", target_bir_lowering=False)
    npass, ntt = tok // tpp, tpp // 128
    assert tpp == 512

    x_in = nc.dram_tensor("x_in", [tok, 1024], F32, kind="ExternalInput").ap()
    wq = nc.dram_tensor("wq", [128, 8192], F32, kind="ExternalInput").ap()
    wk = nc.dram_tensor("wk", [128, 8192], F32, kind="ExternalInput").ap()
    wg = nc.dram_tensor("wg", [128, 8192], F32, kind="ExternalInput").ap()
    wvr = nc.dram_tensor("wvr", [128, 8192], F32, kind="ExternalInput").ap()
    bqkg = nc.dram_tensor("bqkg", [128, 24], F32, kind="ExternalInput").ap()
    bvrow = nc.dram_tensor("bvrow", [1, 1024], F32, kind="ExternalInput").ap()
    qT_o = nc.dram_tensor("qT_o", [128, 8 * tok], BF16, kind="ExternalOutput").ap()
    kT_o = nc.dram_tensor("kT_o", [128, 8 * tok], BF16, kind="ExternalOutput").ap()
    vtm_o = nc.dram_tensor("vtm_o", [tok, 1024], BF16, kind="ExternalOutput").ap()
    L_o = nc.dram_tensor("L_o", [128, 8 * tok], F32, kind="ExternalOutput").ap()
    S_o = nc.dram_tensor("S_o", [128, 8 * 1024], F32, kind="ExternalOutput").ap()
    lg_o = nc.dram_tensor("lg_o", [128, 8], F32, kind="ExternalOutput").ap()

    xr_ = x_in.rearrange("(n p) d -> p n d", p=128)
    vtm_r = vtm_o.rearrange("(n p) d -> p n d", p=128)

    with ExitStack() as stack:
        sb = lambda n, sh, dt: stack.enter_context(nc.sbuf_tensor(n, sh, dt))
        B = alloc_common(nc, stack, ntt, tpp)
        wqc = sb("wqc", [128, 8192], BF16)
        wkc = sb("wkc", [128, 8192], BF16)
        wgc = sb("wgc", [128, 8192], BF16)
        wvrc = sb("wvrc", [128, 8192], BF16)
        bqk = sb("bqk", [128, 24], F32)
        bvb = sb("bvb", [128, 1024], F32)
        bvr = sb("bvr", [1, 1024], F32)
        onesrow = sb("onesrow", [1, 128], F32)
        qT = sb("qT", [128, 8 * tpp], BF16)
        kT = sb("kT", [128, 8 * tpp], BF16)
        vtm = sb("vtm", [128, ntt * 1024], BF16)
        Lb = sb("Lb", [128, 8 * tpp], F32)
        lgT = sb("lgT", [128, tpp], F32)
        Ssb = sb("Ssb", [128, 8 * 1024], F32)
        logG = sb("logG", [128, 8], F32)
        kfac = sb("kfac", [128, CH], F32)
        khf = sb("khf", [128, CH], BF16)
        khtm = sb("khtm", [128, 1024], BF16)
        gam = sb("gam", [128, 8], F32)
        onesc = sb("onesc", [128, CH], F32)

        p = Prog(nc)
        emit_identities(p, B.idb, B.idf, B.epsc)
        p.g.memset(onesc[:, :], 1.0)
        p.g.memset(onesrow[:, :], 1.0)
        p.v.memset(Ssb[:, :], 0.0)
        p.v.memset(logG[:, :], 0.0)
        for w_ap, wt in ((wq, wqc), (wk, wkc), (wg, wgc), (wvr, wvrc)):
            p.g.dma_start(out=wt[:, :], in_=w_ap[:, :])
        p.y.dma_start(out=bqk[:, :], in_=bqkg[:, :])
        p.y.dma_start(out=bvr[:, :], in_=bvrow[:, :])
        p.t.matmul(B.pstf[:, 0:512], onesrow[:, :], bvr[:, 0:512], start=True, stop=True)
        p.v.tensor_copy(bvb[:, 0:512], B.pstf[:, 0:512])
        p.t.matmul(B.pstf[:, 0:512], onesrow[:, :], bvr[:, 512:1024], start=True, stop=True)
        p.v.tensor_copy(bvb[:, 512:1024], B.pstf[:, 0:512])

        for pp in range(npass):
            emit_head(p, B, xr_, pp)
            for wt, dst, boff, fn in ((wqc, qT, 0, AF.Identity), (wkc, kT, 8, AF.Sigmoid)):
                for og in range(8):
                    ps = B.psmm[og % 2]
                    for kt in range(8):
                        p.t.matmul(ps[:, :], wt[:, (og * 8 + kt) * 128:(og * 8 + kt + 1) * 128],
                                   B.xlnT[:, kt * tpp:(kt + 1) * tpp],
                                   start=(kt == 0), stop=(kt == 7))
                    p.s.activation(dst[:, og * tpp:(og + 1) * tpp], ps[:, :],
                                   fn, bias=bqk[:, boff + og:boff + og + 1])
            for og in range(8):
                ps = B.psmm[og % 2]
                for kt in range(8):
                    p.t.matmul(ps[:, :], wgc[:, (og * 8 + kt) * 128:(og * 8 + kt + 1) * 128],
                               B.xlnT[:, kt * tpp:(kt + 1) * tpp],
                               start=(kt == 0), stop=(kt == 7))
                p.s.activation(lgT[:, :], ps[:, :], AF.Sigmoid, bias=bqk[:, 16 + og:16 + og + 1])
                p.s.activation(lgT[:, :], lgT[:, :], AF.Ln)
                for cc in range(tpp // CH):
                    p.v.tensor_tensor_scan(Lb[:, og * tpp + cc * CH:og * tpp + (cc + 1) * CH],
                                           onesc[:, :], lgT[:, cc * CH:(cc + 1) * CH],
                                           0.0, OP.mult, OP.add)
            for k in range(ntt):
                for oc in range(2):
                    ps = B.psmm[2 + oc]
                    for kt in range(8):
                        p.t.matmul(ps[:, :], B.xlnT[:, kt * tpp + k * 128:kt * tpp + (k + 1) * 128],
                                   wvrc[:, kt * 1024 + oc * 512:kt * 1024 + (oc + 1) * 512],
                                   start=(kt == 0), stop=(kt == 7))
                    p.v.tensor_tensor(vtm[:, k * 1024 + oc * 512:k * 1024 + (oc + 1) * 512],
                                      ps[:, :], bvb[:, oc * 512:(oc + 1) * 512], OP.add)
                p.g.dma_start(out=vtm_r[:, pp * ntt + k, :], in_=vtm[:, k * 1024:(k + 1) * 1024])
            p.g.dma_start(out=qT_o.rearrange("p (h t) -> p h t", h=8)[:, :, pp * tpp:(pp + 1) * tpp],
                          in_=qT.rearrange("p (h t) -> p h t", h=8))
            p.g.dma_start(out=kT_o.rearrange("p (h t) -> p h t", h=8)[:, :, pp * tpp:(pp + 1) * tpp],
                          in_=kT.rearrange("p (h t) -> p h t", h=8))
            p.g.dma_start(out=L_o.rearrange("p (h t) -> p h t", h=8)[:, :, pp * tpp:(pp + 1) * tpp],
                          in_=Lb.rearrange("p (h t) -> p h t", h=8))
            for cc in range(tpp // CH):
                Lc = [Lb[:, kb * tpp + cc * CH:kb * tpp + (cc + 1) * CH] for kb in range(8)]
                kTc = [kT[:, kb * tpp + cc * CH:kb * tpp + (cc + 1) * CH] for kb in range(8)]
                vtmc = vtm[:, cc * 1024:(cc + 1) * 1024]
                emit_gla_state_update(p, B, Ssb, kTc, Lc, vtmc, kfac, khf, khtm, gam,
                                      (B.psmm[2], B.psmm[3]))
                for kb in range(8):
                    p.v.tensor_tensor(logG[:, kb:kb + 1], logG[:, kb:kb + 1],
                                      Lc[kb][:, CH - 1:CH], OP.add)

        p.g.dma_start(out=S_o[:, :], in_=Ssb[:, :])
        p.g.dma_start(out=lg_o[:, :], in_=logG[:, :])
        with nc.Block() as block:
            p.emit(block, stack)
    return nc


def build_l2a_out(tok=2048, tpp=512):
    nc = bass.Bass("TRN2", target_bir_lowering=False)
    npass, ntt = tok // tpp, tpp // 128
    assert tpp == 512

    x_in = nc.dram_tensor("x_in", [tok, 1024], F32, kind="ExternalInput").ap()
    qT_i = nc.dram_tensor("qT_i", [128, 8 * tok], BF16, kind="ExternalInput").ap()
    kT_i = nc.dram_tensor("kT_i", [128, 8 * tok], BF16, kind="ExternalInput").ap()
    vtm_i = nc.dram_tensor("vtm_i", [tok, 1024], BF16, kind="ExternalInput").ap()
    L_i = nc.dram_tensor("L_i", [128, 8 * tok], F32, kind="ExternalInput").ap()
    S_all = nc.dram_tensor("S_all", [8, 128, 8 * 1024], F32, kind="ExternalInput").ap()
    lg_all = nc.dram_tensor("lg_all", [8, 128, 8], F32, kind="ExternalInput").ap()
    pmask = nc.dram_tensor("pmask", [128, 8], F32, kind="ExternalInput").ap()
    m2 = nc.dram_tensor("m2", [128, 64], F32, kind="ExternalInput").ap()
    wo = nc.dram_tensor("wo", [128, 8192], F32, kind="ExternalInput").ap()
    borow = nc.dram_tensor("borow", [128, 8], F32, kind="ExternalInput").ap()
    x_out = nc.dram_tensor("x_out", [tok, 1024], F32, kind="ExternalOutput").ap()

    xr_ = x_in.rearrange("(n p) d -> p n d", p=128)
    xro = x_out.rearrange("(n p) d -> p n d", p=128)
    vtm_r = vtm_i.rearrange("(n p) d -> p n d", p=128)

    with ExitStack() as stack:
        sb = lambda n, sh, dt: stack.enter_context(nc.sbuf_tensor(n, sh, dt))
        B = alloc_common(nc, stack, ntt, tpp)
        woc = sb("woc", [128, 8192], BF16)
        bo = sb("bo", [128, 8], F32)
        Ssb = sb("Ssb", [128, 8 * 1024], F32)
        Sbf = sb("Sbf", [128, 8 * 1024], BF16)
        Stmp = sb("Stmp", [128, 1024], F32)
        lgtmp = sb("lgtmp", [128, 8], F32)
        wr_ = sb("wr_", [128, 8], F32)
        msk = sb("msk", [128, 8], F32)
        m2c = sb("m2c", [128, 64], F32)
        qTl = sb("qTl", [128, 8 * tpp], BF16)
        kTl = sb("kTl", [128, 8 * tpp], BF16)
        Ll = sb("Ll", [128, 8 * tpp], F32)
        vtm = sb("vtm", [128, ntt * 1024], BF16)
        qt_ = sb("qt_", [128, 8 * CH], BF16)
        efac = sb("efac", [128, CH], F32)
        qs_ = sb("qs_", [128, 8 * QS], BF16)
        kt_ = sb("kt_", [128, CH], BF16)
        sTb = sb("sTb", [128, QS], BF16)
        maskall = sb("maskall", [128, (CH // QS) * QS], BF16)
        kfac = sb("kfac", [128, CH], F32)
        khf = sb("khf", [128, CH], BF16)
        khtm = sb("khtm", [128, 1024], BF16)
        gam = sb("gam", [128, 8], F32)
        oT = sb("oT", [128, 8 * tpp], BF16)
        outT = sb("outT", [128, 8 * tpp], F32)

        p = Prog(nc)
        emit_identities(p, B.idb, B.idf, B.epsc)
        p.g.memset(maskall[:, :], 1.0)
        p.g.affine_select(out=maskall.rearrange("p (i q) -> p i q", q=QS),
                          in_=maskall.rearrange("p (i q) -> p i q", q=QS),
                          compare_op=OP.is_ge, fill=0.0, base=0,
                          pattern=[[QS, CH // QS], [1, QS]], channel_multiplier=-1)
        p.g.dma_start(out=woc[:, :], in_=wo[:, :])
        p.y.dma_start(out=bo[:, :], in_=borow[:, :])
        p.y.dma_start(out=msk[:, :], in_=pmask[:, :])
        p.y.dma_start(out=m2c[:, :], in_=m2[:, :])
        p.v.memset(Ssb[:, :], 0.0)
        for r in range(8):
            p.v.memset(wr_[:, :], 0.0)
            for j in range(8):
                p.g.dma_start(out=lgtmp[:, :], in_=lg_all[j])
                p.v.scalar_tensor_tensor(wr_[:, :], lgtmp[:, :], m2c[:, r * 8 + j:r * 8 + j + 1],
                                         wr_[:, :], OP.mult, OP.add)
            p.s.activation(wr_[:, :], wr_[:, :], AF.Exp)
            p.v.tensor_scalar(wr_[:, :], wr_[:, :], msk[:, r:r + 1], None, OP.mult)
            for kb in range(8):
                p.g.dma_start(out=Stmp[:, :], in_=S_all[r][:, kb * 1024:(kb + 1) * 1024])
                p.v.scalar_tensor_tensor(Ssb[:, kb * 1024:(kb + 1) * 1024], Stmp[:, :],
                                         wr_[:, kb:kb + 1], Ssb[:, kb * 1024:(kb + 1) * 1024],
                                         OP.mult, OP.add)

        for pp in range(npass):
            emit_head(p, B, xr_, pp)
            p.y.dma_start(out=qTl.rearrange("p (h t) -> p h t", h=8),
                          in_=qT_i.rearrange("p (h t) -> p h t", h=8)[:, :, pp * tpp:(pp + 1) * tpp])
            p.y.dma_start(out=kTl.rearrange("p (h t) -> p h t", h=8),
                          in_=kT_i.rearrange("p (h t) -> p h t", h=8)[:, :, pp * tpp:(pp + 1) * tpp])
            p.y.dma_start(out=Ll.rearrange("p (h t) -> p h t", h=8),
                          in_=L_i.rearrange("p (h t) -> p h t", h=8)[:, :, pp * tpp:(pp + 1) * tpp])
            p.y.dma_start(out=vtm.rearrange("p (n d) -> p n d", d=1024),
                          in_=vtm_r[:, pp * ntt:(pp + 1) * ntt, :])
            for cc in range(tpp // CH):
                Lc = [Ll[:, kb * tpp + cc * CH:kb * tpp + (cc + 1) * CH] for kb in range(8)]
                kTc = [kTl[:, kb * tpp + cc * CH:kb * tpp + (cc + 1) * CH] for kb in range(8)]
                qTc = [qTl[:, kb * tpp + cc * CH:kb * tpp + (cc + 1) * CH] for kb in range(8)]
                vtmc = vtm[:, cc * 1024:(cc + 1) * 1024]
                for kb in range(8):
                    p.v.tensor_copy(Sbf[:, kb * 1024:(kb + 1) * 1024],
                                    Ssb[:, kb * 1024:(kb + 1) * 1024])
                for kb in range(8):
                    p.s.activation(efac[:, :], Lc[kb], AF.Exp)
                    p.v.tensor_tensor(qt_[:, kb * CH:(kb + 1) * CH], qTc[kb], efac[:, :], OP.mult)
                for half in range(2):
                    po = B.psmm[half]
                    for vb4 in range(4):
                        vb = half * 4 + vb4
                        for kb in range(8):
                            p.t.matmul(po[:, vb4 * 128:(vb4 + 1) * 128],
                                       Sbf[:, kb * 1024 + vb * 128:kb * 1024 + (vb + 1) * 128],
                                       qt_[:, kb * CH:(kb + 1) * CH],
                                       start=(vb4 == 0 and kb == 0), stop=False)
                for i in range(CH // QS):
                    i0 = i * QS
                    for kb in range(8):
                        p.v.tensor_scalar(efac[:, 0:QS], Lc[kb][:, i0:i0 + QS],
                                          Lc[kb][:, i0:i0 + 1], None, OP.subtract)
                        p.s.activation(efac[:, 0:QS], efac[:, 0:QS], AF.Exp)
                        p.v.tensor_tensor(qs_[:, kb * QS:(kb + 1) * QS],
                                          qTc[kb][:, i0:i0 + QS], efac[:, 0:QS], OP.mult)
                    ns = i0 + QS
                    pss = B.pstf
                    for kb in range(8):
                        p.v.tensor_scalar(kfac[:, 0:ns], Lc[kb][:, 0:ns],
                                          Lc[kb][:, i0:i0 + 1], None, OP.subtract)
                        p.s.activation(kfac[:, 0:ns], kfac[:, 0:ns], AF.Exp, scale=-1.0)
                        p.v.tensor_tensor(kt_[:, 0:ns], kTc[kb][:, 0:ns], kfac[:, 0:ns], OP.mult)
                        p.t.matmul(pss[0:ns, 0:QS], kt_[:, 0:ns], qs_[:, kb * QS:(kb + 1) * QS],
                                   start=(kb == 0), stop=(kb == 7))
                    p.v.tensor_tensor(sTb[0:ns, :], pss[0:ns, 0:QS],
                                      maskall[0:ns, i * QS:(i + 1) * QS], OP.mult)
                    for half in range(2):
                        po = B.psmm[half]
                        for vb4 in range(4):
                            vb = half * 4 + vb4
                            stop = (i == CH // QS - 1 and vb4 == 3)
                            p.t.matmul(po[:, vb4 * 128 + i * QS:vb4 * 128 + (i + 1) * QS],
                                       vtmc[0:ns, vb * 128:(vb + 1) * 128], sTb[0:ns, :],
                                       start=False, stop=stop)
                for half in range(2):
                    po = B.psmm[half]
                    for vb4 in range(4):
                        vb = half * 4 + vb4
                        p.v.tensor_copy(oT[:, vb * tpp + cc * CH:vb * tpp + (cc + 1) * CH],
                                        po[:, vb4 * 128:(vb4 + 1) * 128])
                emit_gla_state_update(p, B, Ssb, kTc, Lc, vtmc, kfac, khf, khtm, gam,
                                      (B.psmm[2], B.psmm[3]))
            for og in range(8):
                ps = B.psmm[2 + og % 2]
                for kt in range(8):
                    p.t.matmul(ps[:, :], woc[:, (og * 8 + kt) * 128:(og * 8 + kt + 1) * 128],
                               oT[:, kt * tpp:(kt + 1) * tpp],
                               start=(kt == 0), stop=(kt == 7))
                p.s.activation(outT[:, og * tpp:(og + 1) * tpp], ps[:, :],
                               AF.Identity, bias=bo[:, og:og + 1])
            emit_tail(p, B, outT, xro, pp)

        with nc.Block() as block:
            p.emit(block, stack)
    return nc


def l2_inputs_A(pm):
    return {
        "wq": _lhsT_resident(pm["Wq"].T), "wk": _lhsT_resident(pm["Wk"].T),
        "wg": _lhsT_resident(pm["Wg"].T), "wvr": _rhs_resident(pm["Wv"].T),
        "bqkg": np.concatenate([pm["bq"].reshape(8, 128).T, pm["bk"].reshape(8, 128).T,
                                pm["bg"].reshape(8, 128).T], axis=1).astype(np.float32),
        "bvrow": pm["bv"].reshape(1, 1024).astype(np.float32),
    }


def run_l2_attn(x_full, pm, tpp=512):
    xf = np.ascontiguousarray(x_full.reshape(16384, 1024).astype(np.float32))
    tok = xf.shape[0] // NCORES
    ncA = build_l2a_qkv(tok=tok, tpp=tpp)
    baseA = l2_inputs_A(pm)
    in_maps = []
    for c in range(NCORES):
        m = dict(baseA)
        m["x_in"] = xf[c * tok:(c + 1) * tok]
        in_maps.append(m)
    resA = run_bass_kernel_spmd(ncA, in_maps, core_ids=list(range(NCORES)))
    S_all = np.stack([resA.results[c]["S_o"] for c in range(NCORES)], axis=0)
    lg_all = np.stack([resA.results[c]["lg_o"] for c in range(NCORES)], axis=0)
    ncB = build_l2a_out(tok=tok, tpp=tpp)
    baseB = {"wo": _lhsT_resident(pm["Wo"].T), "S_all": S_all, "lg_all": lg_all,
             "borow": pm["bo"].reshape(8, 128).T.astype(np.float32).copy()}
    in_maps = []
    for c in range(NCORES):
        m = dict(baseB)
        m["x_in"] = xf[c * tok:(c + 1) * tok]
        for nm in ("qT_o", "kT_o", "vtm_o", "L_o"):
            m[nm.replace("_o", "_i")] = resA.results[c][nm]
        pmask = np.zeros((128, 8), np.float32)
        m2 = np.zeros((128, 64), np.float32)
        for r in range((c // 4) * 4, c):
            pmask[:, r] = 1.0
            for j in range(r + 1, c):
                m2[:, r * 8 + j] = 1.0
        m["pmask"] = pmask
        m["m2"] = m2
        in_maps.append(m)
    resB = run_bass_kernel_spmd(ncB, in_maps, core_ids=list(range(NCORES)))
    out = np.concatenate([resB.results[c]["x_out"] for c in range(NCORES)], axis=0)
    return out.reshape(2, 8192, 1024)




def build_l3a_qkv(tok=2048, tpp=512):
    nc = bass.Bass("TRN2", target_bir_lowering=False)
    npass, ntt = tok // tpp, tpp // 128
    assert tpp == 512

    x_in = nc.dram_tensor("x_in", [tok, 1024], F32, kind="ExternalInput").ap()
    wq = nc.dram_tensor("wq", [128, 8192], F32, kind="ExternalInput").ap()
    wk = nc.dram_tensor("wk", [128, 8192], F32, kind="ExternalInput").ap()
    wvr = nc.dram_tensor("wvr", [128, 8192], F32, kind="ExternalInput").ap()
    qT_o = nc.dram_tensor("qT_o", [128, 8 * tok], BF16, kind="ExternalOutput").ap()
    kT_o = nc.dram_tensor("kT_o", [128, 8 * tok], BF16, kind="ExternalOutput").ap()
    vtm_o = nc.dram_tensor("vtm_o", [tok, 1024], BF16, kind="ExternalOutput").ap()

    xr_ = x_in.rearrange("(n p) d -> p n d", p=128)
    vtm_r = vtm_o.rearrange("(n p) d -> p n d", p=128)

    with ExitStack() as stack:
        sb = lambda n, sh, dt: stack.enter_context(nc.sbuf_tensor(n, sh, dt))
        B = alloc_common(nc, stack, ntt, tpp)
        wqc = sb("wqc", [128, 8192], BF16)
        wkc = sb("wkc", [128, 8192], BF16)
        wvrc = sb("wvrc", [128, 8192], BF16)
        qT = sb("qT", [128, 8 * tpp], BF16)
        kT = sb("kT", [128, 8 * tpp], BF16)
        vtm = sb("vtm", [128, ntt * 1024], BF16)

        p = Prog(nc)
        emit_identities(p, B.idb, B.idf, B.epsc)
        for w_ap, wt in ((wq, wqc), (wk, wkc), (wvr, wvrc)):
            p.g.dma_start(out=wt[:, :], in_=w_ap[:, :])
        for pp in range(npass):
            emit_head(p, B, xr_, pp)
            for wt, dst, hbm in ((wqc, qT, qT_o), (wkc, kT, kT_o)):
                for og in range(8):
                    ps = B.psmm[og % 2]
                    for kt in range(8):
                        p.t.matmul(ps[:, :], wt[:, (og * 8 + kt) * 128:(og * 8 + kt + 1) * 128],
                                   B.xlnT[:, kt * tpp:(kt + 1) * tpp],
                                   start=(kt == 0), stop=(kt == 7))
                    p.v.tensor_copy(dst[:, og * tpp:(og + 1) * tpp], ps[:, :])
                p.g.dma_start(out=hbm.rearrange("p (h t) -> p h t", h=8)[:, :, pp * tpp:(pp + 1) * tpp],
                              in_=dst.rearrange("p (h t) -> p h t", h=8))
            for k in range(ntt):
                for oc in range(2):
                    ps = B.psmm[2 + oc]
                    for kt in range(8):
                        p.t.matmul(ps[:, :], B.xlnT[:, kt * tpp + k * 128:kt * tpp + (k + 1) * 128],
                                   wvrc[:, kt * 1024 + oc * 512:kt * 1024 + (oc + 1) * 512],
                                   start=(kt == 0), stop=(kt == 7))
                    p.v.tensor_copy(vtm[:, k * 1024 + oc * 512:k * 1024 + (oc + 1) * 512], ps[:, :])
                p.g.dma_start(out=vtm_r[:, pp * ntt + k, :], in_=vtm[:, k * 1024:(k + 1) * 1024])
        with nc.Block() as block:
            p.emit(block, stack)
    return nc


def build_l3a_out(tok=2048, tpp=512, seq=8192):
    nc = bass.Bass("TRN2", target_bir_lowering=False)
    npass, ntt = tok // tpp, tpp // 128
    KVQ = 2048
    KB = 128
    assert tpp == 512 and ntt == 4

    x_in = nc.dram_tensor("x_in", [tok, 1024], F32, kind="ExternalInput").ap()
    qT_i = nc.dram_tensor("qT_i", [128, 8 * tok], BF16, kind="ExternalInput").ap()
    kT_f = nc.dram_tensor("kT_f", [128, 8 * seq], BF16, kind="ExternalInput").ap()
    vtm_f = nc.dram_tensor("vtm_f", [seq, 1040], BF16, kind="ExternalInput").ap()
    wo = nc.dram_tensor("wo", [128, 8192], F32, kind="ExternalInput").ap()
    x_out = nc.dram_tensor("x_out", [tok, 1024], F32, kind="ExternalOutput").ap()

    xr_ = x_in.rearrange("(n p) d -> p n d", p=128)
    xro = x_out.rearrange("(n p) d -> p n d", p=128)
    vtm_fr = vtm_f.rearrange("(n p) d -> p n d", p=128)

    with ExitStack() as stack:
        sb = lambda n, sh, dt: stack.enter_context(nc.sbuf_tensor(n, sh, dt))
        B = alloc_common(nc, stack, ntt, tpp, ln_bufs=False)
        woc = sb("woc", [128, 8192], BF16)
        qTp = sb("qTp", [128, 8 * tpp], BF16)
        kq = sb("kq", [128, 8 * KVQ], BF16)
        vq = sb("vq", [128, (KVQ // 128) * 1040], BF16)
        cmask = sb("cmask", [128, 4 * 512], BF16)
        pexp = sb("pexp", [128, 512], BF16)
        oacc = sb("oacc", [128, 16 * ntt * 65], F32)
        lrec = sb("lrec", [128, 4], F32)
        onb = sb("onb", [128, 64], BF16)
        oT = sb("oT", [128, 8 * tpp], BF16)
        outT = sb("outT", [128, 8 * tpp], F32)
        neg8 = sb("neg8", [128, 1], F32)

        p = Prog(nc)
        emit_identities(p, B.idb, B.idf, B.epsc)
        p.g.memset(neg8[:, :], -8.0)
        p.g.memset(cmask[:, :], 1.0)
        p.g.affine_select(out=cmask.rearrange("p (r c) -> p r c", r=4),
                          in_=cmask.rearrange("p (r c) -> p r c", r=4),
                          compare_op=OP.is_ge, fill=0.0, base=0,
                          pattern=[[-128, 4], [1, 512]], channel_multiplier=-1)
        p.g.dma_start(out=woc[:, :], in_=wo[:, :])

        QOFF = seq - tok
        for pp in range(npass):
            p.y.dma_start(out=B.xt.rearrange("p (n d) -> p n d", d=1024),
                          in_=xr_[:, pp * ntt:(pp + 1) * ntt, :])
            p.y.dma_start(out=qTp.rearrange("p (h t) -> p h t", h=8),
                          in_=qT_i.rearrange("p (h t) -> p h t", h=8)[:, :, pp * tpp:(pp + 1) * tpp])
            p.v.memset(oacc[:, :], 0.0)
            qpos0 = QOFF + pp * tpp
            frontier = qpos0 + tpp
            nq = (frontier + KVQ - 1) // KVQ
            for qq in range(nq):
                kb0 = qq * KVQ
                nkc = min(KVQ, frontier - kb0) // KB
                p.y.dma_start(out=kq.rearrange("p (h t) -> p h t", h=8)[:, :, 0:nkc * KB],
                              in_=kT_f.rearrange("p (h t) -> p h t", h=8)[:, :, kb0:kb0 + nkc * KB])
                p.y.dma_start(out=vq.rearrange("p (n d) -> p n d", d=1040)[:, 0:nkc, :],
                              in_=vtm_fr[:, kb0 // 128:kb0 // 128 + nkc, :])
                for h in range(16):
                    og, row = h // 2, (h % 2) * 64
                    po = B.psmm[2 + h % 2]
                    pss = B.psmm[h % 2]
                    qs = qTp[row:row + 64, og * tpp:(og + 1) * tpp]
                    plan = []
                    for c in range(nkc):
                        kidx = kb0 + c * KB
                        ks_valid = [k for k in range(ntt)
                                    if kidx < qpos0 + (k + 1) * 128]
                        if ks_valid:
                            plan.append((c, kidx, ks_valid))
                    n_omm = sum(len(ksv) for (_, _, ksv) in plan)
                    iomm = 0
                    for pi, (c, kidx, ksv) in enumerate(plan):
                        ks = kq[row:row + 64, og * KVQ + c * KB:og * KVQ + (c + 1) * KB]
                        p.t.matmul(pss[:, :], ks, qs, start=True, stop=True)
                        p.s.activation(pexp[:, :], pss[:, :], AF.Exp, scale=0.125, bias=neg8[:, 0:1])
                        r = (kidx - qpos0) // 128
                        if 0 <= r <= 3:
                            p.v.tensor_tensor(pexp[:, :], pexp[:, :],
                                              cmask[:, r * 512:(r + 1) * 512], OP.mult)
                        vs = vq[:, c * 1040 + h * 65:c * 1040 + (h + 1) * 65]
                        for k in ksv:
                            p.t.matmul(po[:, k * 65:(k + 1) * 65],
                                       pexp[:, k * 128:(k + 1) * 128], vs,
                                       start=(iomm == 0), stop=(iomm == n_omm - 1))
                            iomm += 1
                    for k in range(ntt):
                        if kb0 < qpos0 + (k + 1) * 128:
                            sl = oacc[:, (h * ntt + k) * 65:(h * ntt + k + 1) * 65]
                            p.v.tensor_tensor(sl, sl, po[:, k * 65:(k + 1) * 65], OP.add)
            for k in range(ntt):
                for h in range(16):
                    og, row = h // 2, (h % 2) * 64
                    sl = oacc[:, (h * ntt + k) * 65:(h * ntt + k + 1) * 65]
                    p.v.reciprocal(lrec[:, 0:1], sl[:, 64:65])
                    p.v.tensor_scalar(onb[:, :], sl[:, 0:64], lrec[:, 0:1], None, OP.mult)
                    p.t.transpose(B.pst[0:64, 0:128], onb[:, :], B.idb[:, :])
                    p.v.tensor_copy(oT[row:row + 64, og * tpp + k * 128:og * tpp + (k + 1) * 128],
                                    B.pst[0:64, 0:128])
            for og in range(8):
                ps = B.psmm[og % 2]
                for kt in range(8):
                    p.t.matmul(ps[:, :], woc[:, (og * 8 + kt) * 128:(og * 8 + kt + 1) * 128],
                               oT[:, kt * tpp:(kt + 1) * tpp],
                               start=(kt == 0), stop=(kt == 7))
                p.v.tensor_copy(outT[:, og * tpp:(og + 1) * tpp], ps[:, :])
            emit_tail(p, B, outT, xro, pp)

        with nc.Block() as block:
            p.emit(block, stack)
    return nc


def run_l3_attn(x_full, pm, tpp=512):
    xf = np.ascontiguousarray(x_full.reshape(16384, 1024).astype(np.float32))
    tok = xf.shape[0] // NCORES
    seq = 8192
    ncA = build_l3a_qkv(tok=tok, tpp=tpp)
    WQ = pm["W_Q"].reshape(1024, 1024)
    WK = pm["W_K"].reshape(1024, 1024)
    WV = pm["W_V"].reshape(1024, 1024)
    baseA = {"wq": _lhsT_resident(WQ), "wk": _lhsT_resident(WK), "wvr": _rhs_resident(WV)}
    in_maps = []
    for c in range(NCORES):
        m = dict(baseA)
        m["x_in"] = xf[c * tok:(c + 1) * tok]
        in_maps.append(m)
    resA = run_bass_kernel_spmd(ncA, in_maps, core_ids=list(range(NCORES)))
    ncB = build_l3a_out(tok=tok, tpp=tpp, seq=seq)
    baseB = {"wo": _lhsT_resident(pm["W_O"])}
    kT_b = {}
    v_b = {}
    for b in range(2):
        kT_b[b] = np.concatenate([resA.results[b * 4 + i]["kT_o"].reshape(128, 8, tok)
                                  for i in range(4)], axis=2)
        v_b[b] = np.concatenate([resA.results[b * 4 + i]["vtm_o"] for i in range(4)], axis=0)
    in_maps = []
    for c in range(NCORES):
        b, posq = c // 4, (c % 4)
        valid = (posq + 1) * tok
        m = dict(baseB)
        m["x_in"] = xf[c * tok:(c + 1) * tok]
        m["qT_i"] = resA.results[c]["qT_o"]
        kT = np.zeros((128, 8, seq), dtype=kT_b[b].dtype)
        kT[:, :, seq - valid:] = kT_b[b][:, :, :valid]
        m["kT_f"] = np.ascontiguousarray(kT.reshape(128, 8 * seq))
        v = np.zeros((seq, 1040), dtype=v_b[b].dtype)
        vv = v_b[b][:valid].reshape(valid, 16, 64)
        v[seq - valid:, :] = np.concatenate(
            [np.concatenate([vv[:, i, :], np.ones((valid, 1), v_b[b].dtype)], axis=1)
             for i in range(16)], axis=1)
        m["vtm_f"] = np.ascontiguousarray(v)
        in_maps.append(m)
    resB = run_bass_kernel_spmd(ncB, in_maps, core_ids=list(range(NCORES)))
    out = np.concatenate([resB.results[c]["x_out"] for c in range(NCORES)], axis=0)
    return out.reshape(2, 8192, 1024)




def build_m1(tok=2048):
    nc = bass.Bass("TRN2", target_bir_lowering=False)
    ntt = tok // 128
    x_in = nc.dram_tensor("x_in", [tok, 1024], F32, kind="ExternalInput").ap()
    xln_o = nc.dram_tensor("xln_o", [tok, 1024], BF16, kind="ExternalOutput").ap()
    xr_ = x_in.rearrange("(n p) d -> p n d", p=128)
    xlo = xln_o.rearrange("(n p) d -> p n d", p=128)
    with ExitStack() as stack:
        sb = lambda n, sh, dt: stack.enter_context(nc.sbuf_tensor(n, sh, dt))
        xt = sb("xt", [128, 4 * 1024], F32)
        xl = sb("xl", [128, 4 * 1024], BF16)
        stat = sb("stat", [128, 64], F32)
        sq = sb("sq", [128, 1024], F32)
        epsc = sb("epsc", [128, 1], F32)
        p = Prog(nc)
        p.g.memset(epsc[:, :], LNEPS)
        for pp in range(tok // 512):
            p.y.dma_start(out=xt.rearrange("p (n d) -> p n d", d=1024),
                          in_=xr_[:, pp * 4:(pp + 1) * 4, :])
            xts = [xt[:, k * 1024:(k + 1) * 1024] for k in range(4)]
            xls = [xl[:, k * 1024:(k + 1) * 1024] for k in range(4)]
            emit_ln(p, xts, xls, stat, sq, epsc)
            for k in range(4):
                p.g.dma_start(out=xlo[:, pp * 4 + k, :], in_=xl[:, k * 1024:(k + 1) * 1024])
        with nc.Block() as block:
            p.emit(block, stack)
    return nc


def build_m2(T=16384, tc=512):
    nc = bass.Bass("TRN2", target_bir_lowering=False)
    nch = T // tc
    BCH = 8192 // tc
    win = nc.dram_tensor("win", [2, 128, 4096], F32, kind="ExternalInput").ap()
    xlnT = nc.dram_tensor("xlnT", [1024, T], BF16, kind="ExternalInput").ap()
    xlnTr = nc.dram_tensor("xlnTr", [1024, T], BF16, kind="ExternalInput").ap()
    convw = nc.dram_tensor("convw", [128, 16], F32, kind="ExternalInput").ap()
    convb = nc.dram_tensor("convb", [128, 4], F32, kind="ExternalInput").ap()
    wx = nc.dram_tensor("wx", [2, 128, 192], F32, kind="ExternalInput").ap()
    u_o = nc.dram_tensor("u_o", [2, 256, T], BF16, kind="ExternalOutput").ap()
    sz_o = nc.dram_tensor("sz_o", [2, 256, T], BF16, kind="ExternalOutput").ap()
    px_o = nc.dram_tensor("px_o", [2, 96, T], F32, kind="ExternalOutput").ap()

    with ExitStack() as stack:
        sb = lambda n, sh, dt: stack.enter_context(nc.sbuf_tensor(n, sh, dt))
        psal = lambda n, sh, dt: stack.enter_context(nc.psum_tensor(n, sh, dt))
        wc = [sb(f"wc{s}", [128, 4096], BF16) for s in range(2)]
        wxc = [sb(f"wxc{s}", [128, 192], BF16) for s in range(2)]
        cwc = sb("cwc", [128, 16], F32)
        cbc = sb("cbc", [128, 4], F32)
        xb = sb("xb", [128, 8 * tc], BF16)
        xs = sb("xs", [128, 2 * 2 * (tc + 3)], F32)
        cacc = sb("cacc", [128, 2 * tc], F32)
        ub = sb("ub", [128, 2 * tc], BF16)
        zb = sb("zb", [128, 2 * tc], BF16)
        pxs = sb("pxs", [128, tc], F32)
        ps = [psal(f"ps{i}", [128, 512], F32) for i in range(4)]
        psx = psal("psx", [128, 512], F32)

        p = Prog(nc)
        for s in range(2):
            p.g.dma_start(out=wc[s][:, :], in_=win[s])
            p.g.dma_start(out=wxc[s][:, :], in_=wx[s])
        p.y.dma_start(out=cwc[:, :], in_=convw[:, :])
        p.y.dma_start(out=cbc[:, :], in_=convb[:, :])

        for s in range(2):
            xsr = (xlnT if s == 0 else xlnTr).rearrange("(kt p) t -> p kt t", p=128)
            for c in range(nch):
                p.y.dma_start(out=xb.rearrange("p (kt t) -> p kt t", kt=8),
                              in_=xsr[:, :, c * tc:(c + 1) * tc])
                for og in range(4):
                    for kt in range(8):
                        p.t.matmul(ps[og][:, :], wc[s][:, (og * 8 + kt) * 128:(og * 8 + kt + 1) * 128],
                                   xb[:, kt * tc:(kt + 1) * tc],
                                   start=(kt == 0), stop=(kt == 7))
                for g in range(2):
                    hbuf = xs[:, (s * 2 + g) * (tc + 3):(s * 2 + g) * (tc + 3) + tc + 3]
                    if c % BCH == 0:
                        p.v.memset(hbuf[:, 0:3], 0.0)
                    else:
                        p.v.tensor_copy(hbuf[:, 0:3], hbuf[:, tc:tc + 3])
                    p.v.tensor_copy(hbuf[:, 3:3 + tc], ps[g][:, :])
                    ca = cacc[:, g * tc:(g + 1) * tc]
                    p.v.tensor_scalar(ca, hbuf[:, 0:tc], cwc[:, (s * 2 + g) * 4:(s * 2 + g) * 4 + 1],
                                      None, OP.mult)
                    for k in range(1, 4):
                        p.v.scalar_tensor_tensor(ca, hbuf[:, k:k + tc],
                                                 cwc[:, (s * 2 + g) * 4 + k:(s * 2 + g) * 4 + k + 1],
                                                 ca, OP.mult, OP.add)
                    p.s.activation(ub[:, g * tc:(g + 1) * tc], ca, AF.Silu,
                                   bias=cbc[:, s * 2 + g:s * 2 + g + 1])
                    p.s.activation(zb[:, g * tc:(g + 1) * tc], ps[2 + g][:, :], AF.Silu)
                p.g.dma_start(out=u_o[s].rearrange("(g p) t -> p g t", p=128)[:, :, c * tc:(c + 1) * tc],
                              in_=ub.rearrange("p (g t) -> p g t", g=2))
                p.g.dma_start(out=sz_o[s].rearrange("(g p) t -> p g t", p=128)[:, :, c * tc:(c + 1) * tc],
                              in_=zb.rearrange("p (g t) -> p g t", g=2))
                for g in range(2):
                    p.t.matmul(psx[0:96, 0:tc], wxc[s][:, g * 96:(g + 1) * 96],
                               ub[:, g * tc:(g + 1) * tc], start=(g == 0), stop=(g == 1))
                p.v.tensor_copy(pxs[0:96, :], psx[0:96, 0:tc])
                p.g.dma_start(out=px_o[s][:, c * tc:(c + 1) * tc], in_=pxs[0:96, :])

        with nc.Block() as block:
            p.emit(block, stack)
    return nc


def build_m3(T=16384, tc=512):
    nc = bass.Bass("TRN2", target_bir_lowering=False)
    nch = T // tc
    BCH = 8192 // tc
    u_i = nc.dram_tensor("u_i", [2, 256, T], BF16, kind="ExternalInput").ap()
    sz_i = nc.dram_tensor("sz_i", [2, 256, T], BF16, kind="ExternalInput").ap()
    px_all = nc.dram_tensor("px_all", [8, 2, 96, T], F32, kind="ExternalInput").ap()
    wdt = nc.dram_tensor("wdt", [2, 64, 256], F32, kind="ExternalInput").ap()
    bdt = nc.dram_tensor("bdt", [128, 4], F32, kind="ExternalInput").ap()
    alog = nc.dram_tensor("alog", [128, 64], F32, kind="ExternalInput").ap()
    dvec = nc.dram_tensor("dvec", [128, 4], F32, kind="ExternalInput").ap()
    y_o = nc.dram_tensor("y_o", [2, 256, T], BF16, kind="ExternalOutput").ap()

    with ExitStack() as stack:
        sb = lambda n, sh, dt: stack.enter_context(nc.sbuf_tensor(n, sh, dt))
        psal = lambda n, sh, dt: stack.enter_context(nc.psum_tensor(n, sh, dt))
        wdtc = [sb(f"wdtc{s}", [64, 256], F32) for s in range(2)]
        bdtc = sb("bdtc", [128, 4], F32)
        acol = sb("acol", [128, 64], F32)
        dcol = sb("dcol", [128, 4], F32)
        sel32 = sb("sel32", [32, 32 * 128], F32)
        pxsum = sb("pxsum", [128, tc], F32)
        pxtmp = sb("pxtmp", [128, tc], F32)
        bc32 = sb("bc32", [32, tc], F32)
        dtb = sb("dtb", [128, 2 * tc], F32)
        dtt = sb("dtt", [128, tc], F32)
        Bb = sb("Bb", [128, 16 * tc], F32)
        Cb = sb("Cb", [128, 16 * tc], F32)
        ubb = sb("ubb", [128, 2 * tc], BF16)
        szb = sb("szb", [128, 2 * tc], BF16)
        dub = sb("dub", [128, 2 * tc], F32)
        dab = sb("dab", [128, tc], F32)
        bbb = sb("bbb", [128, tc], F32)
        hbb = sb("hbb", [128, tc], F32)
        hcb = sb("hcb", [128, tc], F32)
        yac = sb("yac", [128, 2 * tc], F32)
        ybf = sb("ybf", [128, 2 * tc], BF16)
        carry = sb("carry", [128, 64], F32)
        psA = psal("psA", [128, 512], F32)
        psB = psal("psB", [128, 512], F32)

        p = Prog(nc)
        p.g.memset(sel32[:, :], 0.0)
        p.g.affine_select(out=sel32.rearrange("p (n m) -> p n m", n=32),
                          in_=sel32.rearrange("p (n m) -> p n m", n=32),
                          compare_op=OP.not_equal, fill=1.0, base=0,
                          pattern=[[-1, 32], [0, 128]], channel_multiplier=1)
        for s in range(2):
            p.y.dma_start(out=wdtc[s][:, :], in_=wdt[s])
        p.y.dma_start(out=bdtc[:, :], in_=bdt[:, :])
        p.y.dma_start(out=dcol[:, :], in_=dvec[:, :])
        p.y.dma_start(out=acol[:, :], in_=alog[:, :])
        p.s.activation(acol[:, :], acol[:, :], AF.Exp)
        p.v.tensor_scalar(acol[:, :], acol[:, :], -1.0, None, OP.mult)

        for s in range(2):
            for c in range(nch):
                p.g.dma_start(out=pxsum[0:96, :], in_=px_all[0][s][:, c * tc:(c + 1) * tc])
                for r in range(1, 8):
                    p.g.dma_start(out=pxtmp[0:96, :], in_=px_all[r][s][:, c * tc:(c + 1) * tc])
                    p.v.tensor_tensor(pxsum[0:96, :], pxsum[0:96, :], pxtmp[0:96, :], OP.add)
                p.v.tensor_copy(bc32[:, :], pxsum[64:96, :])
                for g in range(2):
                    p.t.matmul(psA[:, :], wdtc[s][:, g * 128:(g + 1) * 128],
                               pxsum[0:64, :], start=True, stop=True)
                    bcol = bdtc[:, s * 2 + g:s * 2 + g + 1]
                    dgt = dtb[:, g * tc:(g + 1) * tc]
                    p.v.tensor_scalar(dgt, psA[:, :], bcol, 0.0, OP.add, OP.max)
                    p.s.activation(dtt[:, :], psA[:, :], AF.Abs, bias=bcol)
                    p.s.activation(dtt[:, :], dtt[:, :], AF.Exp, scale=-1.0)
                    p.s.activation(dtt[:, :], dtt[:, :], AF.Ln, bias=1.0)
                    p.v.tensor_tensor(dgt, dgt, dtt[:, :], OP.add)
                for n in range(16):
                    p.t.matmul(psB[:, :], sel32[:, n * 128:(n + 1) * 128], bc32[:, :],
                               start=True, stop=True)
                    p.s.copy(Bb[:, n * tc:(n + 1) * tc], psB[:, :])
                    p.t.matmul(psA[:, :], sel32[:, (16 + n) * 128:(17 + n) * 128], bc32[:, :],
                               start=True, stop=True)
                    p.v.tensor_copy(Cb[:, n * tc:(n + 1) * tc], psA[:, :])
                p.y.dma_start(out=ubb.rearrange("p (g t) -> p g t", g=2),
                              in_=u_i[s].rearrange("(g p) t -> p g t", p=128)[:, :, c * tc:(c + 1) * tc])
                p.y.dma_start(out=szb.rearrange("p (g t) -> p g t", g=2),
                              in_=sz_i[s].rearrange("(g p) t -> p g t", p=128)[:, :, c * tc:(c + 1) * tc])
                for g in range(2):
                    dgt = dtb[:, g * tc:(g + 1) * tc]
                    du = dub[:, g * tc:(g + 1) * tc]
                    p.v.tensor_tensor(du, dgt, ubb[:, g * tc:(g + 1) * tc], OP.mult)
                    ya = yac[:, g * tc:(g + 1) * tc]
                    for n in range(16):
                        ac = acol[:, (s * 2 + g) * 16 + n:(s * 2 + g) * 16 + n + 1]
                        p.s.activation(dab[:, :], dgt, AF.Exp, scale=ac)
                        p.v.tensor_tensor(bbb[:, :], du, Bb[:, n * tc:(n + 1) * tc], OP.mult)
                        cidx = (s * 2 + g) * 16 + n
                        init = 0.0 if c % BCH == 0 else carry[:, cidx:cidx + 1]
                        p.v.tensor_tensor_scan(hbb[:, :], dab[:, :], bbb[:, :], init,
                                               OP.mult, OP.add)
                        p.v.tensor_copy(carry[:, cidx:cidx + 1], hbb[:, tc - 1:tc])
                        p.v.tensor_tensor(hcb[:, :], hbb[:, :], Cb[:, n * tc:(n + 1) * tc], OP.mult)
                        if n == 0:
                            p.v.tensor_copy(ya, hcb[:, :])
                        else:
                            p.v.tensor_tensor(ya, ya, hcb[:, :], OP.add)
                    p.v.scalar_tensor_tensor(ya, ubb[:, g * tc:(g + 1) * tc],
                                             dcol[:, s * 2 + g:s * 2 + g + 1], ya,
                                             OP.mult, OP.add)
                    p.v.tensor_tensor(ybf[:, g * tc:(g + 1) * tc], ya,
                                      szb[:, g * tc:(g + 1) * tc], OP.mult)
                p.g.dma_start(out=y_o[s].rearrange("(g p) t -> p g t", p=128)[:, :, c * tc:(c + 1) * tc],
                              in_=ybf.rearrange("p (g t) -> p g t", g=2))

        with nc.Block() as block:
            p.emit(block, stack)
    return nc


def build_m4(tok=2048, tpp=512):
    nc = bass.Bass("TRN2", target_bir_lowering=False)
    npass, ntt = tok // tpp, tpp // 128
    assert tpp == 512
    x_in = nc.dram_tensor("x_in", [tok, 1024], F32, kind="ExternalInput").ap()
    yfw = nc.dram_tensor("yfw", [2048, tok], BF16, kind="ExternalInput").ap()
    ybw = nc.dram_tensor("ybw", [2048, tok], BF16, kind="ExternalInput").ap()
    wof = nc.dram_tensor("wof", [128, 16384], F32, kind="ExternalInput").ap()
    wob = nc.dram_tensor("wob", [128, 16384], F32, kind="ExternalInput").ap()
    wfu = nc.dram_tensor("wfu", [128, 16384], F32, kind="ExternalInput").ap()
    fbr = nc.dram_tensor("fbr", [128, 8], F32, kind="ExternalInput").ap()
    x_out = nc.dram_tensor("x_out", [tok, 1024], F32, kind="ExternalOutput").ap()

    xr_ = x_in.rearrange("(n p) d -> p n d", p=128)
    xro = x_out.rearrange("(n p) d -> p n d", p=128)

    with ExitStack() as stack:
        sb = lambda n, sh, dt: stack.enter_context(nc.sbuf_tensor(n, sh, dt))
        B = alloc_common(nc, stack, ntt, tpp, ln_bufs=False)
        wofc = sb("wofc", [128, 16384], BF16)
        wobc = sb("wobc", [128, 16384], BF16)
        wfuc = sb("wfuc", [128, 16384], BF16)
        fbc = sb("fbc", [128, 8], F32)
        yfb = sb("yfb", [128, 16 * tpp], BF16)
        ybb = sb("ybb", [128, 16 * tpp], BF16)
        foT = sb("foT", [128, 8 * tpp], BF16)
        boT = sb("boT", [128, 8 * tpp], BF16)
        outT = sb("outT", [128, 8 * tpp], F32)

        p = Prog(nc)
        emit_identities(p, B.idb, B.idf, B.epsc)
        p.g.dma_start(out=wofc[:, :], in_=wof[:, :])
        p.g.dma_start(out=wobc[:, :], in_=wob[:, :])
        p.g.dma_start(out=wfuc[:, :], in_=wfu[:, :])
        p.y.dma_start(out=fbc[:, :], in_=fbr[:, :])

        for pp in range(npass):
            p.y.dma_start(out=B.xt.rearrange("p (n d) -> p n d", d=1024),
                          in_=xr_[:, pp * ntt:(pp + 1) * ntt, :])
            p.y.dma_start(out=yfb.rearrange("p (kt t) -> p kt t", kt=16),
                          in_=yfw.rearrange("(kt p) t -> p kt t", p=128)[:, :, pp * tpp:(pp + 1) * tpp])
            p.y.dma_start(out=ybb.rearrange("p (kt t) -> p kt t", kt=16),
                          in_=ybw.rearrange("(kt p) t -> p kt t", p=128)[:, :, pp * tpp:(pp + 1) * tpp])
            for wsrc, ysrc, dst in ((wofc, yfb, foT), (wobc, ybb, boT)):
                for og in range(8):
                    ps = B.psmm[og % 2]
                    for kt in range(16):
                        p.t.matmul(ps[:, :], wsrc[:, (og * 16 + kt) * 128:(og * 16 + kt + 1) * 128],
                                   ysrc[:, kt * tpp:(kt + 1) * tpp],
                                   start=(kt == 0), stop=(kt == 15))
                    p.v.tensor_copy(dst[:, og * tpp:(og + 1) * tpp], ps[:, :])
            for og in range(8):
                ps = B.psmm[2 + og % 2]
                for kt in range(16):
                    rhs = foT[:, kt * tpp:(kt + 1) * tpp] if kt < 8 else boT[:, (kt - 8) * tpp:(kt - 7) * tpp]
                    p.t.matmul(ps[:, :], wfuc[:, (og * 16 + kt) * 128:(og * 16 + kt + 1) * 128],
                               rhs, start=(kt == 0), stop=(kt == 15))
                p.s.activation(outT[:, og * tpp:(og + 1) * tpp], ps[:, :],
                               AF.Identity, bias=fbc[:, og:og + 1])
            emit_tail(p, B, outT, xro, pp)

        with nc.Block() as block:
            p.emit(block, stack)
    return nc


def run_l0_attn(x_full, pm, T=16384):
    xf = np.ascontiguousarray(x_full.reshape(T, 1024).astype(np.float32))
    tok = T // NCORES
    nc1 = build_m1(tok=tok)
    res1 = run_bass_kernel_spmd(nc1, [{"x_in": xf[c * tok:(c + 1) * tok]} for c in range(NCORES)],
                                core_ids=list(range(NCORES)))
    xln = np.concatenate([res1.results[c]["xln_o"] for c in range(NCORES)], axis=0)
    xlnT = np.ascontiguousarray(xln.T)
    xlnr = xln.reshape(2, T // 2, 1024)[:, ::-1, :].reshape(T, 1024)
    xlnTr = np.ascontiguousarray(xlnr.T)
    nc2 = build_m2(T=T)
    in_maps = []
    for c in range(NCORES):
        sl = slice(c * 256, (c + 1) * 256)
        m = {"xlnT": xlnT, "xlnTr": xlnTr}
        wrows = {}
        for s, pref in ((0, "fw"), (1, "bw")):
            Win = pm[pref + "_Win"]
            rows = np.concatenate([Win[sl], Win[2048 + (c * 256):(2048 + (c + 1) * 256)]], axis=0)
            wrows[s] = _lhsT_resident(rows)
        m["win"] = np.stack([wrows[0], wrows[1]], axis=0)
        cw = np.zeros((128, 16), np.float32)
        cb = np.zeros((128, 4), np.float32)
        wxs = []
        for s, pref in ((0, "fw"), (1, "bw")):
            cwf = pm[pref + "_convw"][sl, 0, :]
            cw[:, s * 8:s * 8 + 4] = cwf[:128]
            cw[:, s * 8 + 4:s * 8 + 8] = cwf[128:]
            cbf = pm[pref + "_convb"][sl]
            cb[:, s * 2] = cbf[:128]
            cb[:, s * 2 + 1] = cbf[128:]
            Wx = pm[pref + "_Wx"]
            wxr = Wx[:, sl]
            a = np.zeros((128, 192), np.float32)
            a[:, 0:96] = wxr.T[:128]
            a[:, 96:192] = wxr.T[128:]
            wxs.append(a)
        m["convw"] = np.ascontiguousarray(cw[:, [0,1,2,3,8,9,10,11,4,5,6,7,12,13,14,15]]) if False else cw
        m["convb"] = cb
        m["wx"] = np.stack(wxs, axis=0)
        in_maps.append(m)
    res2 = run_bass_kernel_spmd(nc2, in_maps, core_ids=list(range(NCORES)))
    px_all = np.stack([res2.results[c]["px_o"] for c in range(NCORES)], axis=0)
    nc3 = build_m3(T=T)
    in_maps = []
    for c in range(NCORES):
        sl = slice(c * 256, (c + 1) * 256)
        m = {"px_all": px_all,
             "u_i": res2.results[c]["u_o"], "sz_i": res2.results[c]["sz_o"]}
        wdts, bdt_, alog_, dv_ = [], np.zeros((128, 4), np.float32), np.zeros((128, 64), np.float32), np.zeros((128, 4), np.float32)
        for s, pref in ((0, "fw"), (1, "bw")):
            Wdt = pm[pref + "_Wdt"][sl]
            wdts.append(np.ascontiguousarray(Wdt.T))
            b = pm[pref + "_bdt"][sl]
            bdt_[:, s * 2] = b[:128]
            bdt_[:, s * 2 + 1] = b[128:]
            Al = pm[pref + "_Alog"][sl]
            alog_[:, (s * 2) * 16:(s * 2 + 1) * 16] = Al[:128]
            alog_[:, (s * 2 + 1) * 16:(s * 2 + 2) * 16] = Al[128:]
            D = pm[pref + "_D"][sl]
            dv_[:, s * 2] = D[:128]
            dv_[:, s * 2 + 1] = D[128:]
        m["wdt"] = np.stack(wdts, axis=0)
        m["bdt"] = bdt_
        m["alog"] = alog_
        m["dvec"] = dv_
        in_maps.append(m)
    res3 = run_bass_kernel_spmd(nc3, in_maps, core_ids=list(range(NCORES)))
    yfw = np.concatenate([res3.results[c]["y_o"][0] for c in range(NCORES)], axis=0)
    ybw_r = np.concatenate([res3.results[c]["y_o"][1] for c in range(NCORES)], axis=0)
    ybw = ybw_r.reshape(2048, 2, T // 2)[:, :, ::-1].reshape(2048, T)
    nc4 = build_m4(tok=tok)
    base = {
        "wof": _lhsT_resident(pm["fw_Wout"]), "wob": _lhsT_resident(pm["bw_Wout"]),
        "wfu": _lhsT_resident(pm["fuse_W"]),
        "fbr": np.ascontiguousarray(pm["fuse_b"].reshape(8, 128).T),
    }
    in_maps = []
    for c in range(NCORES):
        m = dict(base)
        m["x_in"] = xf[c * tok:(c + 1) * tok]
        m["yfw"] = np.ascontiguousarray(yfw[:, c * tok:(c + 1) * tok])
        m["ybw"] = np.ascontiguousarray(ybw[:, c * tok:(c + 1) * tok])
        in_maps.append(m)
    res4 = run_bass_kernel_spmd(nc4, in_maps, core_ids=list(range(NCORES)))
    out = np.concatenate([res4.results[c]["x_out"] for c in range(NCORES)], axis=0)
    return out.reshape(2, 8192, 1024)


def _grp(inputs, pref):
    return {k[len(pref):]: np.asarray(v) for k, v in inputs.items() if k.startswith(pref)}


def kernel(**inputs):
    x = np.asarray(inputs["L0_attn_x"]).astype(np.float32)
    x = run_l0_attn(x, _grp(inputs, "L0_attn_"))
    x = run_ffn0(x, _grp(inputs, "L0_ffn_"))
    x = run_l1_attn(x, _grp(inputs, "L1_attn_"))
    x = run_rwkv(x, _grp(inputs, "L1_ffn_"))
    x = run_l2_attn(x, _grp(inputs, "L2_attn_"))
    x = run_moe(x, _grp(inputs, "L2_ffn_"))
    x = run_l3_attn(x, _grp(inputs, "L3_attn_"))
    x = run_rwkv(x, _grp(inputs, "L3_ffn_"))
    return np.ascontiguousarray(x.astype(np.float32))


# version 04
# speedup vs baseline: 1.2381x; 1.2381x over previous
import numpy as np
from contextlib import ExitStack
import concourse.bass as bass
import concourse.mybir as mybir
from concourse.bass_utils import run_bass_kernel_spmd

F32 = mybir.dt.float32
BF16 = mybir.dt.bfloat16
AF = mybir.ActivationFunctionType
OP = mybir.AluOpType
AX = mybir.AxisListType

NCORES = 8
LNEPS = 1e-5

_WRITE_KW = {"out", "accum_out", "out_max", "out_indices", "dest", "ap"}

ENGINES = ("t", "v", "s", "g", "y")
DMA_QUEUES = {"g": "qg", "y": "qy"}


def _ap_span(ap):
    try:
        name = ap.tensor.name
    except AttributeError:
        return None
    off = int(ap.offset)
    reach = 1
    for step, cnt in ap.ap:
        reach += abs(int(step)) * (int(cnt) - 1)
    is_psum = "PSum" in type(ap.tensor).__name__
    return (name, off, off + reach, is_psum)


class Op:
    __slots__ = ("eng", "fn", "reads", "writes", "idx", "need_inc", "is_dma")

    def __init__(self, eng, fn, reads, writes, idx, is_dma):
        self.eng, self.fn, self.reads, self.writes = eng, fn, reads, writes
        self.idx, self.is_dma = idx, is_dma
        self.need_inc = False


class _EngProxy:
    def __init__(self, prog, eng):
        self._p = prog
        self._e = eng

    def __getattr__(self, fname):
        def call(*args, _reads=(), _writes=(), **kw):
            return self._p.record(self._e, fname, args, kw, _reads, _writes)
        return call


class Prog:

    def __init__(self, nc):
        self.nc = nc
        self.ops = {e: [] for e in ENGINES}
        self.order = []
        self.segs = {}
        self.deps = {}
        for e in ENGINES:
            setattr(self, e, _EngProxy(self, e))

    def record(self, eng, fname, args, kw, extra_reads, extra_writes):
        reads, writes = [], []
        if fname.startswith("dma_start"):
            writes.append(kw["out"] if "out" in kw else args[0])
            reads.append(kw["in_"] if "in_" in kw else args[1])
        elif fname in ("matmul", "transpose"):
            writes.append(args[0])
            reads.extend(a for a in args[1:] if hasattr(a, "tensor"))
        elif fname in ("memset", "iota"):
            writes.append(args[0])
        elif fname == "affine_select":
            writes.append(kw.get("out", args[0] if args else None))
            if "in_" in kw:
                reads.append(kw["in_"])
        else:
            for i, a in enumerate(args):
                if hasattr(a, "tensor"):
                    (writes if i == 0 else reads).append(a)
            for k, v in kw.items():
                if hasattr(v, "tensor"):
                    (writes if k in _WRITE_KW else reads).append(v)
        reads.extend(extra_reads)
        writes.extend(extra_writes)
        rs = [x for x in (_ap_span(a) for a in reads if a is not None) if x]
        ws = [x for x in (_ap_span(a) for a in writes if a is not None) if x]
        is_dma = fname.startswith("dma_start")
        op = Op(eng, (fname, args, kw), rs, ws, len(self.ops[eng]), is_dma)
        self.ops[eng].append(op)
        self.order.append(op)
        self._resolve(op)
        return op

    def _resolve(self, op):
        deps = set()

        def add(producer):
            if producer is None or producer is op:
                return
            if (producer.eng == op.eng == "t"
                    and not producer.is_dma and not op.is_dma):
                return
            deps.add(producer)

        for (name, a, b, is_psum) in op.reads:
            for seg in self.segs.get(name, []):
                if seg[0] < b and a < seg[1]:
                    add(seg[2])
                    if is_psum:
                        for r in seg[3]:
                            if r.eng != op.eng:
                                add(r)
                    seg[3].append(op)
        for (name, a, b, is_psum) in op.writes:
            lst = self.segs.setdefault(name, [])
            keep = []
            for seg in lst:
                if seg[0] < b and a < seg[1]:
                    add(seg[2])
                    for r in seg[3]:
                        add(r)
                    if seg[0] < a:
                        keep.append([seg[0], a, seg[2], list(seg[3])])
                    if seg[1] > b:
                        keep.append([b, seg[1], seg[2], list(seg[3])])
                else:
                    keep.append(seg)
            keep.append([a, b, op, []])
            self.segs[name] = keep
        self.deps[op] = deps

    def emit(self, block, stack, dma_pool=16):
        nc = self.nc
        sems = {}
        for e in ENGINES:
            sems[e] = stack.enter_context(nc.semaphore(f"ss_{e}"))
        qsems = {}
        for q in DMA_QUEUES.values():
            for j in range(dma_pool):
                qsems[f"{q}{j}"] = stack.enter_context(nc.semaphore(f"ss_{q}{j}"))
        sems.update(qsems)

        for deps in self.deps.values():
            for d in deps:
                d.need_inc = True

        semval = {}
        counters = {k: 0 for k in ENGINES}
        qcount = {q: 0 for q in DMA_QUEUES.values()}
        dma_sem_of = {}
        for op in self.order:
            if op.is_dma:
                q = DMA_QUEUES[op.eng]
                i = qcount[q]
                qcount[q] += 1
                dom = f"{q}{i % dma_pool}"
                dma_sem_of[op] = dom
                semval[op] = (dom, 16 * (i // dma_pool + 1))
            elif op.need_inc:
                counters[op.eng] += 1
                semval[op] = (op.eng, counters[op.eng])

        clocks = {e: {} for e in ENGINES}
        snapshots = {}
        emitted = {e: [] for e in ENGINES}
        for op in self.order:
            ck = clocks[op.eng]
            waits = []
            deps = sorted(self.deps[op], key=lambda o: (o.eng, o.idx))
            for d in deps:
                dom, val = semval[d]
                if ck.get(dom, 0) >= val:
                    continue
                waits.append((dom, val))
                ck[dom] = val
                sn = snapshots.get((dom, val))
                if sn:
                    for d2, v2 in sn.items():
                        if ck.get(d2, 0) < v2:
                            ck[d2] = v2
            mv = semval.get(op)
            if op.is_dma and mv is not None and mv[1] > 16:
                dom, val = mv
                if ck.get(dom, 0) < val - 16:
                    waits.append((dom, val - 16))
                    ck[dom] = val - 16
                    sn = snapshots.get((dom, val - 16))
                    if sn:
                        for d2, v2 in sn.items():
                            if ck.get(d2, 0) < v2:
                                ck[d2] = v2
            if mv is not None:
                snapshots[mv] = dict(ck)
            emitted[op.eng].append((op, waits, mv))

        eng_map = {"t": block.tensor, "v": block.vector, "s": block.scalar,
                   "g": block.gpsimd, "y": block.sync}
        for e in ENGINES:
            if not emitted[e]:
                continue

            @eng_map[e]
            def _(eh, e=e):
                for op, waits, mv in emitted[e]:
                    for dom, val in waits:
                        eh.wait_ge(sems[dom], val)
                    fname, args, kw = op.fn
                    instr = getattr(eh, fname)(*args, **kw)
                    if op.is_dma:
                        instr.then_inc(sems[dma_sem_of[op]], 16)
                    elif mv is not None:
                        instr.then_inc(sems[e], 1)
                if e in DMA_QUEUES:
                    q = DMA_QUEUES[e]
                    n = qcount[q]
                    for j in range(min(n, dma_pool)):
                        cnt = len([i for i in range(n) if i % dma_pool == j])
                        eh.wait_ge(sems[f"{q}{j}"], 16 * cnt)
        return sems


def emit_ln(p, x_tiles, out_tiles, stat, sq, eps_ap):
    for k, xk in enumerate(x_tiles):
        d = xk.shape[-1]
        P = xk.shape[0]
        st = stat[0:P, 16 * k:16 * k + 16]
        p.v.tensor_reduce(st[:, 4:5], xk, AX.X, OP.add)
        p.v.scalar_tensor_tensor(sq[0:P, 0:d], xk, 1.0, xk, OP.mult, OP.mult,
                                 accum_out=st[:, 5:6])
        p.v.tensor_scalar(st[:, 0:1], st[:, 4:5], 1.0 / d, None, OP.mult)
        p.v.tensor_scalar(st[:, 6:7], st[:, 5:6], 1.0 / d, None, OP.mult)
        p.v.tensor_scalar(st[:, 7:8], st[:, 0:1], st[:, 0:1], None, OP.mult)
        p.v.tensor_tensor(st[:, 1:2], st[:, 6:7], st[:, 7:8], OP.subtract)
        p.s.activation(st[:, 2:3], st[:, 1:2], AF.Ln, bias=eps_ap[0:P, 0:1])
        p.s.activation(st[:, 3:4], st[:, 2:3], AF.Exp, scale=-0.5)
        p.v.tensor_scalar(out_tiles[k], xk, st[:, 0:1], st[:, 3:4],
                          OP.subtract, OP.mult)


def emit_transpose(p, src_tiles, dst_fn, pst, idn, fgroups=8):
    for k, src in enumerate(src_tiles):
        for fg in range(fgroups):
            p.t.transpose(pst[:, fg * 128:(fg + 1) * 128],
                          src[:, fg * 128:(fg + 1) * 128], idn[:, :])
        p.v.tensor_copy(dst_fn(k), pst.rearrange("p (f t) -> p f t", f=fgroups))


def emit_identities(p, idb, idf, epsc):
    p.g.memset(epsc[:, :], LNEPS)
    for idn in (idb, idf):
        p.g.memset(idn[:, :], 0.0)
        p.g.affine_select(out=idn[:, :], in_=idn[:, :],
                          compare_op=OP.not_equal, fill=1.0, base=0,
                          pattern=[[-1, 128]], channel_multiplier=1)




class NS:
    pass


def alloc_common(nc, stack, ntt, tpp, ln_bufs=True, psmm_n=4, psum_tp=True):
    sb = lambda n, sh, dt: stack.enter_context(nc.sbuf_tensor(n, sh, dt))
    psal = lambda n, sh, dt: stack.enter_context(nc.psum_tensor(n, sh, dt))
    B = NS()
    B.ntt, B.tpp = ntt, tpp
    B.xt = sb("xt", [128, ntt * 1024], F32)
    if ln_bufs:
        B.xln = sb("xln", [128, ntt * 1024], BF16)
        B.xlnT = sb("xlnT", [128, 8 * tpp], BF16)
    B.stat = sb("stat", [128, 16 * ntt + 16], F32)
    B.sq = sb("sq", [128, 1024], F32)
    B.otm = sb("otm", [128, 2 * 1024], F32)
    B.oln = sb("oln", [128, 2 * 1024], F32)
    B.epsc = sb("epsc", [128, 1], F32)
    B.idb = sb("idb", [128, 128], BF16)
    B.idf = sb("idf", [128, 128], F32)
    if psum_tp:
        B.pst = psal("pst", [128, 1024], BF16)
        B.pstf = psal("pstf", [128, 512], F32)
    B.psmm = [psal(f"psmm{i}", [128, 512], F32) for i in range(psmm_n)]
    return B


def emit_head(p, B, xr, pp):
    ntt, tpp = B.ntt, B.tpp
    p.y.dma_start(out=B.xt.rearrange("p (n d) -> p n d", d=1024),
                  in_=xr[:, pp * ntt:(pp + 1) * ntt, :])
    x_tiles = [B.xt[:, k * 1024:(k + 1) * 1024] for k in range(ntt)]
    ln_tiles = [B.xln[:, k * 1024:(k + 1) * 1024] for k in range(ntt)]
    emit_ln(p, x_tiles, ln_tiles, B.stat, B.sq, B.epsc)
    emit_transpose(p, ln_tiles,
                   lambda k: B.xlnT.rearrange("p (f t) -> p f t", f=8)[:, :, k * 128:(k + 1) * 128],
                   B.pst, B.idb)
    return x_tiles


def emit_tail(p, B, outT, xro, pp):
    ntt, tpp = B.ntt, B.tpp
    for k in range(ntt):
        par = k % 2
        om = B.otm[:, par * 1024:(par + 1) * 1024]
        ol = B.oln[:, par * 1024:(par + 1) * 1024]
        for half in range(2):
            for og4 in range(4):
                og = half * 4 + og4
                p.t.transpose(B.pstf[:, og4 * 128:(og4 + 1) * 128],
                              outT[:, og * tpp + k * 128:og * tpp + (k + 1) * 128],
                              B.idf[:, :])
            p.v.tensor_copy(
                om.rearrange("p (g t) -> p g t", g=8)[:, half * 4:(half + 1) * 4, :],
                B.pstf.rearrange("p (g t) -> p g t", g=4))
        emit_ln(p, [om], [ol], B.stat[:, 16 * ntt:], B.sq, B.epsc)
        p.v.tensor_tensor(ol, ol, B.xt[:, k * 1024:(k + 1) * 1024], OP.add)
        p.g.dma_start(out=xro[:, pp * ntt + k, :], in_=ol)


def emit_square_matmul(p, wT, rhs, psum, out_fn, kt_n=8, og_n=8, tpp=512,
                       evac=None):
    for og in range(og_n):
        ps = psum[og % 2]
        for kt in range(kt_n):
            p.t.matmul(ps[:, :tpp], wT[:, (og * kt_n + kt) * 128:(og * kt_n + kt + 1) * 128],
                       rhs[:, kt * tpp:(kt + 1) * tpp],
                       start=(kt == 0), stop=(kt == kt_n - 1))
        evac(og, ps)


def build_ffn0(tok=2048, tpp=512):
    nc = bass.Bass("TRN2", target_bir_lowering=False)
    DH = 4096
    OG1, KT1, KT2, OG3 = DH // 128, 1024 // 128, DH // 128, 1024 // 128
    npass, ntt = tok // tpp, tpp // 128
    assert tpp == 512

    x_in = nc.dram_tensor("x_in", [tok, 1024], F32, kind="ExternalInput").ap()
    w1 = nc.dram_tensor("w1", [OG1, 1024, 128], F32, kind="ExternalInput").ap()
    w2 = nc.dram_tensor("w2", [OG1, 1024, 128], F32, kind="ExternalInput").ap()
    h1 = nc.dram_tensor("h1", [OG1, DH, 128], F32, kind="ExternalInput").ap()
    h2 = nc.dram_tensor("h2", [OG1, DH, 128], F32, kind="ExternalInput").ap()
    w3 = nc.dram_tensor("w3", [OG3, DH, 128], F32, kind="ExternalInput").ap()
    scales = nc.dram_tensor("scales", [128, 4 * OG1 + OG3], F32, kind="ExternalInput").ap()
    x_out = nc.dram_tensor("x_out", [tok, 1024], F32, kind="ExternalOutput").ap()

    xr = x_in.rearrange("(n p) d -> p n d", p=128)
    xro = x_out.rearrange("(n p) d -> p n d", p=128)

    with ExitStack() as stack:
        sb = lambda n, sh, dt: stack.enter_context(nc.sbuf_tensor(n, sh, dt))
        psal = lambda n, sh, dt: stack.enter_context(nc.psum_tensor(n, sh, dt))
        xt = sb("xt", [128, ntt * 1024], F32)
        xln = sb("xln", [128, ntt * 1024], BF16)
        xlnT = sb("xlnT", [128, KT1 * tpp], BF16)
        hA = sb("hA", [128, OG1 * tpp], BF16)
        hB = sb("hB", [128, OG1 * tpp], BF16)
        tmpA = sb("tmpA", [128, 2 * tpp], F32)
        wsl = [sb(f"wsl{i}", [128, DH], BF16) for i in range(4)]
        scl = sb("scl", [128, 4 * OG1 + OG3], F32)
        stat = sb("stat", [128, 16 * ntt + 16], F32)
        sq = sb("sq", [128, 1024], F32)
        outT = sb("outT", [128, OG3 * tpp], F32)
        otm = sb("otm", [128, 2 * 1024], F32)
        oln = sb("oln", [128, 2 * 1024], F32)
        epsc = sb("epsc", [128, 1], F32)
        idb = sb("idb", [128, 128], BF16)
        idf = sb("idf", [128, 128], F32)
        pst = psal("pst", [128, 1024], BF16)
        pstf = psal("pstf", [128, 512], F32)
        psmm = [psal(f"psmm{i}", [128, 512], F32) for i in range(4)]

        p = Prog(nc)
        emit_identities(p, idb, idf, epsc)
        p.y.dma_start(out=scl[:, :], in_=scales[:, :])

        for pp in range(npass):
            p.y.dma_start(out=xt.rearrange("p (n d) -> p n d", d=1024),
                          in_=xr[:, pp * ntt:(pp + 1) * ntt, :])
            x_tiles = [xt[:, k * 1024:(k + 1) * 1024] for k in range(ntt)]
            ln_tiles = [xln[:, k * 1024:(k + 1) * 1024] for k in range(ntt)]
            emit_ln(p, x_tiles, ln_tiles, stat, sq, epsc)
            emit_transpose(p, ln_tiles,
                           lambda k: xlnT.rearrange("p (f t) -> p f t", f=8)[:, :, k * 128:(k + 1) * 128],
                           pst, idb)

            for og in range(OG1):
                pa, pb = psmm[2 * (og % 2)], psmm[2 * (og % 2) + 1]
                wa, wb = wsl[2 * (og % 2)], wsl[2 * (og % 2) + 1]
                p.g.dma_start(out=wa.rearrange("p (kt o) -> p kt o", o=128)[:, 0:8, :],
                              in_=w1[og].rearrange("(kt p) o -> p kt o", p=128))
                p.g.dma_start(out=wb.rearrange("p (kt o) -> p kt o", o=128)[:, 0:8, :],
                              in_=w2[og].rearrange("(kt p) o -> p kt o", p=128))
                for kt in range(KT1):
                    p.t.matmul(pa[:, :], wa[:, kt * 128:(kt + 1) * 128],
                               xlnT[:, kt * tpp:(kt + 1) * tpp],
                               start=(kt == 0), stop=(kt == KT1 - 1))
                for kt in range(KT1):
                    p.t.matmul(pb[:, :], wb[:, kt * 128:(kt + 1) * 128],
                               xlnT[:, kt * tpp:(kt + 1) * tpp],
                               start=(kt == 0), stop=(kt == KT1 - 1))
                tA = tmpA[:, (og % 2) * tpp:(og % 2 + 1) * tpp]
                p.s.activation(tA, pa[:, :], AF.Silu, scale=scl[:, og:og + 1])
                p.v.scalar_tensor_tensor(hA[:, og * tpp:(og + 1) * tpp], pb[:, :],
                                         scl[:, OG1 + og:OG1 + og + 1], tA,
                                         OP.mult, OP.mult)

            for stage, (wsrc, hin, hout, soff) in enumerate((
                    (h1, hA, hB, 2 * OG1), (h2, hB, hA, 3 * OG1))):
                for og in range(OG1):
                    pa, wa = psmm[og % 2], wsl[og % 2]
                    p.g.dma_start(out=wa.rearrange("p (kt o) -> p kt o", o=128),
                                  in_=wsrc[og].rearrange("(kt p) o -> p kt o", p=128))
                    for kt in range(KT2):
                        p.t.matmul(pa[:, :], wa[:, kt * 128:(kt + 1) * 128],
                                   hin[:, kt * tpp:(kt + 1) * tpp],
                                   start=(kt == 0), stop=(kt == KT2 - 1))
                    p.s.activation(hout[:, og * tpp:(og + 1) * tpp], pa[:, :],
                                   AF.Gelu, scale=scl[:, soff + og:soff + og + 1])

            for og in range(OG3):
                pa, wa = psmm[og % 2], wsl[og % 2]
                p.g.dma_start(out=wa.rearrange("p (kt o) -> p kt o", o=128),
                              in_=w3[og].rearrange("(kt p) o -> p kt o", p=128))
                for kt in range(KT2):
                    p.t.matmul(pa[:, :], wa[:, kt * 128:(kt + 1) * 128],
                               hA[:, kt * tpp:(kt + 1) * tpp],
                               start=(kt == 0), stop=(kt == KT2 - 1))
                p.v.tensor_scalar(outT[:, og * tpp:(og + 1) * tpp], pa[:, :],
                                  scl[:, 4 * OG1 + og:4 * OG1 + og + 1], None, OP.mult)

            for k in range(ntt):
                par = k % 2
                om = otm[:, par * 1024:(par + 1) * 1024]
                ol = oln[:, par * 1024:(par + 1) * 1024]
                for half in range(2):
                    for og4 in range(4):
                        og = half * 4 + og4
                        p.t.transpose(pstf[:, og4 * 128:(og4 + 1) * 128],
                                      outT[:, og * tpp + k * 128:og * tpp + (k + 1) * 128],
                                      idf[:, :])
                    p.v.tensor_copy(
                        om.rearrange("p (g t) -> p g t", g=8)[:, half * 4:(half + 1) * 4, :],
                        pstf.rearrange("p (g t) -> p g t", g=4))
                emit_ln(p, [om], [ol], stat[:, 16 * ntt:], sq, epsc)
                p.v.tensor_tensor(ol, ol, xt[:, k * 1024:(k + 1) * 1024], OP.add)
                p.g.dma_start(out=xro[:, pp * ntt + k, :], in_=ol)

        with nc.Block() as block:
            p.emit(block, stack)

    return nc


def _lhsT_tiles(w):
    O, K = w.shape
    return np.ascontiguousarray(w.reshape(O // 128, 128, K).transpose(0, 2, 1))


def _pack_scales(*scales):
    cols = [np.ascontiguousarray(s.reshape(-1, 128).T) for s in scales]
    return np.ascontiguousarray(np.concatenate(cols, axis=1))


def ffn0_inputs(p):
    return {
        "w1": _lhsT_tiles(p["w1_w"]), "w2": _lhsT_tiles(p["w2_w"]),
        "h1": _lhsT_tiles(p["h1_w"]), "h2": _lhsT_tiles(p["h2_w"]),
        "w3": _lhsT_tiles(p["w3_w"]),
        "scales": _pack_scales(p["w1_s"], p["w2_s"], p["h1_s"], p["h2_s"], p["w3_s"]),
    }


def run_ffn0(x_full, p, tpp=512):
    xf = np.ascontiguousarray(x_full.reshape(16384, 1024).astype(np.float32))
    tok = xf.shape[0] // NCORES
    nc = build_ffn0(tok=tok, tpp=tpp)
    base = ffn0_inputs(p)
    in_maps = []
    for c in range(NCORES):
        m = dict(base)
        m["x_in"] = xf[c * tok:(c + 1) * tok]
        in_maps.append(m)
    res = run_bass_kernel_spmd(nc, in_maps, core_ids=list(range(NCORES)))
    out = np.concatenate([res.results[c]["x_out"] for c in range(NCORES)], axis=0)
    return out.reshape(2, 8192, 1024)




def build_rwkv(tok=2048, tpp=512):
    nc = bass.Bass("TRN2", target_bir_lowering=False)
    npass, ntt = tok // tpp, tpp // 128
    assert tpp == 512

    x_in = nc.dram_tensor("x_in", [tok, 1024], F32, kind="ExternalInput").ap()
    halo = nc.dram_tensor("halo", [1, 1024], F32, kind="ExternalInput").ap()
    wr = nc.dram_tensor("wr", [128, 8192], F32, kind="ExternalInput").ap()
    wk = nc.dram_tensor("wk", [128, 8192], F32, kind="ExternalInput").ap()
    wv = nc.dram_tensor("wv", [128, 8192], F32, kind="ExternalInput").ap()
    mus = nc.dram_tensor("mus", [128, 16], F32, kind="ExternalInput").ap()
    x_out = nc.dram_tensor("x_out", [tok, 1024], F32, kind="ExternalOutput").ap()

    xr_ = x_in.rearrange("(n p) d -> p n d", p=128)
    xro = x_out.rearrange("(n p) d -> p n d", p=128)

    with ExitStack() as stack:
        sb = lambda n, sh, dt: stack.enter_context(nc.sbuf_tensor(n, sh, dt))
        B = alloc_common(nc, stack, ntt, tpp)
        wrT = sb("wrT", [128, 8192], BF16)
        wkT = sb("wkT", [128, 8192], BF16)
        wvT = sb("wvT", [128, 8192], BF16)
        muc = sb("muc", [128, 16], F32)
        xprevT = sb("xprevT", [128, 8 * tpp], F32)
        mixr = sb("mixr", [128, 8 * tpp], BF16)
        mixk = sb("mixk", [128, 8 * tpp], BF16)
        dT = sb("dT", [128, tpp], F32)
        rT = sb("rT", [128, 8 * tpp], F32)
        kact = sb("kact", [128, 8 * tpp], BF16)
        relu_t = sb("relu_t", [128, 2 * tpp], F32)
        outT = sb("outT", [128, 8 * tpp], F32)
        carry = sb("carry", [128, 8], F32)
        hx = sb("hx", [1, 1024], F32)
        hln = sb("hln", [1, 1024], BF16)

        p = Prog(nc)
        emit_identities(p, B.idb, B.idf, B.epsc)
        for w_ap, wt in ((wr, wrT), (wk, wkT), (wv, wvT)):
            p.g.dma_start(out=wt[:, :], in_=w_ap[:, :])
        p.y.dma_start(out=muc[:, :], in_=mus[:, :])
        p.y.dma_start(out=hx[:, :], in_=halo[:, :])
        emit_ln(p, [hx[:, :]], [hln[:, :]], B.stat[:, 16 * ntt:], B.sq, B.epsc)
        for fg in range(8):
            p.t.transpose(B.pst[:, 2 * fg:2 * fg + 1], hln[:, fg * 128:(fg + 1) * 128],
                          B.idb[0:1, 0:1])
        p.v.tensor_copy(carry[:, :], B.pst.rearrange("p (a b) -> p a b", b=2)[:, 0:8, 0])

        for pp in range(npass):
            emit_head(p, B, xr_, pp)
            for fg in range(8):
                blk = slice(fg * tpp, (fg + 1) * tpp)
                p.v.tensor_copy(xprevT[:, fg * tpp + 1:(fg + 1) * tpp],
                                B.xlnT[:, fg * tpp:(fg + 1) * tpp - 1])
                p.v.tensor_copy(xprevT[:, fg * tpp:fg * tpp + 1], carry[:, fg:fg + 1])
                p.v.tensor_copy(carry[:, fg:fg + 1], B.xlnT[:, (fg + 1) * tpp - 1:(fg + 1) * tpp])
                p.v.tensor_tensor(dT[:, :], B.xlnT[:, blk], xprevT[:, blk], OP.subtract)
                p.v.scalar_tensor_tensor(mixr[:, blk], dT[:, :], muc[:, fg:fg + 1],
                                         xprevT[:, blk], OP.mult, OP.add)
                p.v.scalar_tensor_tensor(mixk[:, blk], dT[:, :], muc[:, 8 + fg:9 + fg],
                                         xprevT[:, blk], OP.mult, OP.add)

            def evac_r(og, ps):
                p.s.activation(rT[:, og * tpp:(og + 1) * tpp], ps[:, :tpp], AF.Sigmoid)
            emit_square_matmul(p, wrT, mixr, B.psmm, None, evac=evac_r, tpp=tpp)

            def evac_k(og, ps):
                rl = relu_t[:, (og % 2) * tpp:(og % 2 + 1) * tpp]
                p.s.activation(rl, ps[:, :tpp], AF.Relu)
                p.s.activation(kact[:, og * tpp:(og + 1) * tpp], rl, AF.Square)
            emit_square_matmul(p, wkT, mixk, B.psmm, None, evac=evac_k, tpp=tpp)

            def evac_v(og, ps):
                p.v.tensor_tensor(outT[:, og * tpp:(og + 1) * tpp], ps[:, :tpp],
                                  rT[:, og * tpp:(og + 1) * tpp], OP.mult)
            emit_square_matmul(p, wvT, kact, B.psmm, None, evac=evac_v, tpp=tpp)

            emit_tail(p, B, outT, xro, pp)

        with nc.Block() as block:
            p.emit(block, stack)
    return nc


def _lhsT_resident(w):
    O, K = w.shape
    wT = np.ascontiguousarray(w.T)
    a = wT.reshape(K // 128, 128, O // 128, 128).transpose(1, 2, 0, 3)
    return np.ascontiguousarray(a.reshape(128, (K // 128) * (O // 128) * 128))


def rwkv_inputs(pm):
    return {
        "wr": _lhsT_resident(pm["W_r"]), "wk": _lhsT_resident(pm["W_k"]),
        "wv": _lhsT_resident(pm["W_v"]),
        "mus": np.concatenate([pm["mu_r"].reshape(8, 128).T, pm["mu_k"].reshape(8, 128).T], axis=1).astype(np.float32),
    }


def run_rwkv(x_full, pm, tpp=512):
    xf = np.ascontiguousarray(x_full.reshape(16384, 1024).astype(np.float32))
    tok = xf.shape[0] // NCORES
    nc = build_rwkv(tok=tok, tpp=tpp)
    base = rwkv_inputs(pm)
    in_maps = []
    for c in range(NCORES):
        m = dict(base)
        m["x_in"] = xf[c * tok:(c + 1) * tok]
        if c % 4 == 0:
            m["halo"] = np.zeros((1, 1024), np.float32)
        else:
            m["halo"] = xf[c * tok - 1:c * tok]
        in_maps.append(m)
    res = run_bass_kernel_spmd(nc, in_maps, core_ids=list(range(NCORES)))
    out = np.concatenate([res.results[c]["x_out"] for c in range(NCORES)], axis=0)
    return out.reshape(2, 8192, 1024)




def build_moe(tok=2048, tpp=512):
    nc = bass.Bass("TRN2", target_bir_lowering=False)
    npass, ntt = tok // tpp, tpp // 128
    NE, HID = 16, 512
    assert tpp == 512

    x_in = nc.dram_tensor("x_in", [tok, 1024], F32, kind="ExternalInput").ap()
    gw = nc.dram_tensor("gw", [128, 8 * 16], F32, kind="ExternalInput").ap()
    w1 = nc.dram_tensor("w1", [NE, 128, 4096], F32, kind="ExternalInput").ap()
    w3 = nc.dram_tensor("w3", [NE, 128, 4096], F32, kind="ExternalInput").ap()
    w2 = nc.dram_tensor("w2", [NE, 128, 4096], F32, kind="ExternalInput").ap()
    sw1 = nc.dram_tensor("sw1", [128, 4096], F32, kind="ExternalInput").ap()
    sw3 = nc.dram_tensor("sw3", [128, 4096], F32, kind="ExternalInput").ap()
    sw2 = nc.dram_tensor("sw2", [128, 4096], F32, kind="ExternalInput").ap()
    x_out = nc.dram_tensor("x_out", [tok, 1024], F32, kind="ExternalOutput").ap()

    xr_ = x_in.rearrange("(n p) d -> p n d", p=128)
    xro = x_out.rearrange("(n p) d -> p n d", p=128)

    with ExitStack() as stack:
        sb = lambda n, sh, dt: stack.enter_context(nc.sbuf_tensor(n, sh, dt))
        B = alloc_common(nc, stack, ntt, tpp)
        gwc = sb("gwc", [128, 8 * 16], BF16)
        esel = sb("esel", [16, 16 * 128], F32)
        logit = sb("logit", [128, ntt * 16], F32)
        gstat = sb("gstat", [128, ntt * 8], F32)
        rem = sb("rem", [128, ntt * 16], F32)
        selm = sb("selm", [128, ntt * 16], F32)
        tmpm = sb("tmpm", [128, ntt * 16], F32)
        cb = sb("cb", [128, ntt * 16], F32)
        cbT = sb("cbT", [16, ntt * 128], F32)
        cbB = sb("cbB", [128, tpp], F32)
        wsa = [sb(f"wsa{i}", [128, 4096], BF16) for i in range(2)]
        wsb = [sb(f"wsb{i}", [128, 4096], BF16) for i in range(2)]
        wsc = [sb(f"wsc{i}", [128, 4096], BF16) for i in range(2)]
        swc = [sb(f"swc{i}", [128, 4096], BF16) for i in range(3)]
        act = sb("act", [128, 8 * 4 * tpp], BF16)
        acts = sb("acts", [128, 4 * tpp], BF16)
        tmpa = sb("tmpa", [128, 2 * tpp], F32)
        tmpu = sb("tmpu", [128, 2 * tpp], F32)
        outT = sb("outT", [128, 8 * tpp], F32)

        p = Prog(nc)
        emit_identities(p, B.idb, B.idf, B.epsc)
        p.g.dma_start(out=gwc[:, :], in_=gw[:, :])
        for s_ap, st in ((sw1, swc[0]), (sw3, swc[1]), (sw2, swc[2])):
            p.g.dma_start(out=st[:, :], in_=s_ap[:, :])
        p.g.memset(esel[:, :], 0.0)
        p.g.affine_select(out=esel.rearrange("p (e m) -> p e m", e=16),
                          in_=esel.rearrange("p (e m) -> p e m", e=16),
                          compare_op=OP.not_equal, fill=1.0, base=0,
                          pattern=[[-1, 16], [0, 128]], channel_multiplier=1)

        for pp in range(npass):
            emit_head(p, B, xr_, pp)

            for k in range(ntt):
                ps = B.psmm[2]
                for kt in range(8):
                    p.t.matmul(ps[:, 0:16],
                               B.xlnT[:, kt * tpp + k * 128:kt * tpp + (k + 1) * 128],
                               gwc[:, kt * 16:(kt + 1) * 16],
                               start=(kt == 0), stop=(kt == 7))
                p.v.tensor_copy(logit[:, k * 16:(k + 1) * 16], ps[:, 0:16])
            for k in range(ntt):
                lg = logit[:, k * 16:(k + 1) * 16]
                st = gstat[:, k * 8:(k + 1) * 8]
                p.v.tensor_reduce(st[:, 0:1], lg, AX.X, OP.max)
                p.v.tensor_scalar(st[:, 1:2], st[:, 0:1], -1.0, None, OP.mult)
                p.s.activation(lg, lg, AF.Exp, bias=st[:, 1:2])
                p.v.tensor_reduce(st[:, 2:3], lg, AX.X, OP.add)
                p.v.reciprocal(st[:, 3:4], st[:, 2:3])
                p.v.tensor_scalar(lg, lg, st[:, 3:4], None, OP.mult)
                rm = rem[:, k * 16:(k + 1) * 16]
                sl = selm[:, k * 16:(k + 1) * 16]
                tp = tmpm[:, k * 16:(k + 1) * 16]
                p.v.tensor_copy(rm, lg)
                p.v.memset(sl, 0.0)
                for it in range(4):
                    p.v.tensor_reduce(st[:, 4:5], rm, AX.X, OP.max)
                    p.v.tensor_scalar(tp, rm, st[:, 4:5], None, OP.is_equal)
                    p.v.tensor_tensor(sl, sl, tp, OP.add)
                    if it < 3:
                        p.v.tensor_tensor(tp, tp, rm, OP.mult)
                        p.v.tensor_tensor(rm, rm, tp, OP.subtract)
                cbk = cb[:, k * 16:(k + 1) * 16]
                p.v.tensor_tensor(cbk, lg, sl, OP.mult)
                p.v.tensor_reduce(st[:, 5:6], cbk, AX.X, OP.add)
                p.v.reciprocal(st[:, 6:7], st[:, 5:6])
                p.v.tensor_scalar(cbk, cbk, st[:, 6:7], None, OP.mult)
                p.t.transpose(B.pstf[0:16, 0:128], cbk, B.idf[:, :])
                p.v.tensor_copy(cbT[:, k * 128:(k + 1) * 128], B.pstf[0:16, 0:128])

            for og in range(4):
                psA, psB = B.psmm[0], B.psmm[1]
                for kt in range(8):
                    p.t.matmul(psA[:, :], swc[0][:, (og * 8 + kt) * 128:(og * 8 + kt + 1) * 128],
                               B.xlnT[:, kt * tpp:(kt + 1) * tpp],
                               start=(kt == 0), stop=(kt == 7))
                for kt in range(8):
                    p.t.matmul(psB[:, :], swc[1][:, (og * 8 + kt) * 128:(og * 8 + kt + 1) * 128],
                               B.xlnT[:, kt * tpp:(kt + 1) * tpp],
                               start=(kt == 0), stop=(kt == 7))
                ta = tmpa[:, (og % 2) * tpp:(og % 2 + 1) * tpp]
                p.s.activation(ta, psA[:, :], AF.Silu)
                p.v.tensor_tensor(acts[:, og * tpp:(og + 1) * tpp], psB[:, :], ta, OP.mult)

            for half in range(2):
                for ee in range(8):
                    e = half * 8 + ee
                    wa, wb = wsa[e % 2], wsb[e % 2]
                    p.g.dma_start(out=wa[:, :], in_=w1[e])
                    p.g.dma_start(out=wb[:, :], in_=w3[e])
                    p.t.matmul(B.pstf[:, 0:tpp], esel[:, e * 128:(e + 1) * 128],
                               cbT[:, 0:tpp], start=True, stop=True)
                    p.v.tensor_copy(cbB[:, :], B.pstf[:, 0:tpp])
                    for og in range(4):
                        psA, psB = B.psmm[0], B.psmm[1]
                        for kt in range(8):
                            p.t.matmul(psA[:, :], wa[:, (og * 8 + kt) * 128:(og * 8 + kt + 1) * 128],
                                       B.xlnT[:, kt * tpp:(kt + 1) * tpp],
                                       start=(kt == 0), stop=(kt == 7))
                        for kt in range(8):
                            p.t.matmul(psB[:, :], wb[:, (og * 8 + kt) * 128:(og * 8 + kt + 1) * 128],
                                       B.xlnT[:, kt * tpp:(kt + 1) * tpp],
                                       start=(kt == 0), stop=(kt == 7))
                        ta = tmpa[:, (og % 2) * tpp:(og % 2 + 1) * tpp]
                        tu = tmpu[:, (og % 2) * tpp:(og % 2 + 1) * tpp]
                        p.s.activation(ta, psA[:, :], AF.Silu)
                        p.v.tensor_tensor(tu, psB[:, :], ta, OP.mult)
                        p.v.tensor_tensor(act[:, (ee * 4 + og) * tpp:(ee * 4 + og + 1) * tpp],
                                          tu, cbB[:, :], OP.mult)

                for ogg in range(2):
                    for ee in range(8):
                        e = half * 8 + ee
                        wc = wsc[ee % 2]
                        p.g.dma_start(out=wc[:, 0:2048],
                                      in_=w2[e][:, ogg * 2048:(ogg + 1) * 2048])
                        for og4 in range(4):
                            ps = B.psmm[og4]
                            for kt in range(4):
                                stop = (half == 0 and ee == 7 and kt == 3)
                                p.t.matmul(ps[:, :], wc[:, (og4 * 4 + kt) * 128:(og4 * 4 + kt + 1) * 128],
                                           act[:, (ee * 4 + kt) * tpp:(ee * 4 + kt + 1) * tpp],
                                           start=(ee == 0 and kt == 0), stop=stop)
                    for og4 in range(4):
                        og = ogg * 4 + og4
                        ps = B.psmm[og4]
                        if half == 0:
                            p.v.tensor_copy(outT[:, og * tpp:(og + 1) * tpp], ps[:, :])
                        else:
                            for kt in range(4):
                                p.t.matmul(ps[:, :], swc[2][:, (og * 4 + kt) * 128:(og * 4 + kt + 1) * 128],
                                           acts[:, kt * tpp:(kt + 1) * tpp],
                                           start=False, stop=(kt == 3))
                            p.v.tensor_tensor(outT[:, og * tpp:(og + 1) * tpp],
                                              outT[:, og * tpp:(og + 1) * tpp], ps[:, :], OP.add)

            emit_tail(p, B, outT, xro, pp)

        with nc.Block() as block:
            p.emit(block, stack)
    return nc


def moe_inputs(pm):
    def stack_lhsT(ws):
        return np.stack([_lhsT_resident(w) for w in ws], axis=0)
    gwT = pm["gate_w"].T
    gpack = np.ascontiguousarray(gwT.reshape(8, 128, 16).transpose(1, 0, 2).reshape(128, 128))
    return {
        "gw": gpack.astype(np.float32),
        "w1": stack_lhsT(pm["w1"]), "w3": stack_lhsT(pm["w3"]), "w2": stack_lhsT(pm["w2"]),
        "sw1": _lhsT_resident(pm["sw1"]), "sw3": _lhsT_resident(pm["sw3"]),
        "sw2": _lhsT_resident(pm["sw2"]),
    }


def run_moe(x_full, pm, tpp=512):
    xf = np.ascontiguousarray(x_full.reshape(16384, 1024).astype(np.float32))
    tok = xf.shape[0] // NCORES
    nc = build_moe(tok=tok, tpp=tpp)
    base = moe_inputs(pm)
    in_maps = []
    for c in range(NCORES):
        m = dict(base)
        m["x_in"] = xf[c * tok:(c + 1) * tok]
        in_maps.append(m)
    res = run_bass_kernel_spmd(nc, in_maps, core_ids=list(range(NCORES)))
    out = np.concatenate([res.results[c]["x_out"] for c in range(NCORES)], axis=0)
    return out.reshape(2, 8192, 1024)




def build_l1a_qkv(tok=2048, tpp=512):
    nc = bass.Bass("TRN2", target_bir_lowering=False)
    npass, ntt = tok // tpp, tpp // 128
    assert tpp == 512

    x_in = nc.dram_tensor("x_in", [tok, 1024], F32, kind="ExternalInput").ap()
    wq = nc.dram_tensor("wq", [128, 8192], F32, kind="ExternalInput").ap()
    wk = nc.dram_tensor("wk", [128, 8192], F32, kind="ExternalInput").ap()
    wvr = nc.dram_tensor("wvr", [128, 8192], F32, kind="ExternalInput").ap()
    wkr = nc.dram_tensor("wkr", [128, 8192], F32, kind="ExternalInput").ap()
    qT_o = nc.dram_tensor("qT_o", [128, 8 * tok], BF16, kind="ExternalOutput").ap()
    kT_o = nc.dram_tensor("kT_o", [128, 8 * tok], BF16, kind="ExternalOutput").ap()
    ktm_o = nc.dram_tensor("ktm_o", [tok, 1024], BF16, kind="ExternalOutput").ap()
    vtm_o = nc.dram_tensor("vtm_o", [tok, 1024], BF16, kind="ExternalOutput").ap()
    S_o = nc.dram_tensor("S_o", [128, 8 * 128], F32, kind="ExternalOutput").ap()
    Z_o = nc.dram_tensor("Z_o", [128, 8], F32, kind="ExternalOutput").ap()

    xr_ = x_in.rearrange("(n p) d -> p n d", p=128)
    ktm_r = ktm_o.rearrange("(n p) d -> p n d", p=128)
    vtm_r = vtm_o.rearrange("(n p) d -> p n d", p=128)

    with ExitStack() as stack:
        sb = lambda n, sh, dt: stack.enter_context(nc.sbuf_tensor(n, sh, dt))
        B = alloc_common(nc, stack, ntt, tpp)
        wqc = sb("wqc", [128, 8192], BF16)
        wkc = sb("wkc", [128, 8192], BF16)
        wvrc = sb("wvrc", [128, 8192], BF16)
        wkrc = sb("wkrc", [128, 8192], BF16)
        qT = sb("qT", [128, 8 * tpp], BF16)
        kT = sb("kT", [128, 8 * tpp], BF16)
        ktm = sb("ktm", [128, ntt * 1024], BF16)
        vtm = sb("vtm", [128, ntt * 1024], BF16)
        Ssb = sb("Ssb", [128, 8 * 128], F32)
        Zsb = sb("Zsb", [128, 8], F32)
        ones = sb("ones", [128, 1], BF16)

        p = Prog(nc)
        emit_identities(p, B.idb, B.idf, B.epsc)
        p.g.memset(ones[:, :], 1.0)
        p.v.memset(Ssb[:, :], 0.0)
        p.v.memset(Zsb[:, :], 0.0)
        for w_ap, wt in ((wq, wqc), (wk, wkc), (wvr, wvrc), (wkr, wkrc)):
            p.g.dma_start(out=wt[:, :], in_=w_ap[:, :])

        for pp in range(npass):
            emit_head(p, B, xr_, pp)
            for name, wt, dst, hbm in (("q", wqc, qT, qT_o), ("k", wkc, kT, kT_o)):
                for og in range(8):
                    ps = B.psmm[og % 2]
                    for kt in range(8):
                        p.t.matmul(ps[:, :], wt[:, (og * 8 + kt) * 128:(og * 8 + kt + 1) * 128],
                                   B.xlnT[:, kt * tpp:(kt + 1) * tpp],
                                   start=(kt == 0), stop=(kt == 7))
                    p.v.tensor_copy(dst[:, og * tpp:(og + 1) * tpp], ps[:, :])
                p.g.dma_start(out=hbm.rearrange("p (h t) -> p h t", h=8)[:, :, pp * tpp:(pp + 1) * tpp],
                              in_=dst.rearrange("p (h t) -> p h t", h=8))
            for name, wt, dst, hbm in (("kt", wkrc, ktm, ktm_r), ("vt", wvrc, vtm, vtm_r)):
                for k in range(ntt):
                    for oc in range(2):
                        ps = B.psmm[2 + oc]
                        for kt in range(8):
                            p.t.matmul(ps[:, :], B.xlnT[:, kt * tpp + k * 128:kt * tpp + (k + 1) * 128],
                                       wt[:, kt * 1024 + oc * 512:kt * 1024 + (oc + 1) * 512],
                                       start=(kt == 0), stop=(kt == 7))
                        p.v.tensor_copy(dst[:, k * 1024 + oc * 512:k * 1024 + (oc + 1) * 512], ps[:, :])
                    p.g.dma_start(out=hbm[:, pp * ntt + k, :], in_=dst[:, k * 1024:(k + 1) * 1024])
            for k in range(ntt):
                for h in range(8):
                    ps = B.psmm[h % 2]
                    p.t.matmul(ps[:, 0:128], ktm[:, k * 1024 + h * 128:k * 1024 + (h + 1) * 128],
                               vtm[:, k * 1024 + h * 128:k * 1024 + (h + 1) * 128],
                               start=True, stop=True)
                    p.v.tensor_tensor(Ssb[:, h * 128:(h + 1) * 128], Ssb[:, h * 128:(h + 1) * 128],
                                      ps[:, 0:128], OP.add)
                ps = B.psmm[2]
                for h in range(8):
                    p.t.matmul(ps[:, 2 * h:2 * h + 1],
                               ktm[:, k * 1024 + h * 128:k * 1024 + (h + 1) * 128],
                               ones[:, :], start=True, stop=True)
                p.v.tensor_tensor(Zsb[:, :], Zsb[:, :],
                                  ps.rearrange("p (a b) -> p a b", b=2)[:, 0:8, 0], OP.add)

        p.g.dma_start(out=S_o[:, :], in_=Ssb[:, :])
        p.g.dma_start(out=Z_o[:, :], in_=Zsb[:, :])

        with nc.Block() as block:
            p.emit(block, stack)
    return nc


def build_l1a_out(tok=2048, tpp=512):
    nc = bass.Bass("TRN2", target_bir_lowering=False)
    npass, ntt = tok // tpp, tpp // 128
    assert tpp == 512

    x_in = nc.dram_tensor("x_in", [tok, 1024], F32, kind="ExternalInput").ap()
    qT_i = nc.dram_tensor("qT_i", [128, 8 * tok], BF16, kind="ExternalInput").ap()
    kT_i = nc.dram_tensor("kT_i", [128, 8 * tok], BF16, kind="ExternalInput").ap()
    ktm_i = nc.dram_tensor("ktm_i", [tok, 1024], BF16, kind="ExternalInput").ap()
    vtm_i = nc.dram_tensor("vtm_i", [tok, 1024], BF16, kind="ExternalInput").ap()
    S_all = nc.dram_tensor("S_all", [8, 128, 8 * 128], F32, kind="ExternalInput").ap()
    Z_all = nc.dram_tensor("Z_all", [8, 128, 8], F32, kind="ExternalInput").ap()
    pmask = nc.dram_tensor("pmask", [128, 8], F32, kind="ExternalInput").ap()
    wo = nc.dram_tensor("wo", [128, 8192], F32, kind="ExternalInput").ap()
    nw = nc.dram_tensor("nw", [1, 128], F32, kind="ExternalInput").ap()
    x_out = nc.dram_tensor("x_out", [tok, 1024], F32, kind="ExternalOutput").ap()

    xr_ = x_in.rearrange("(n p) d -> p n d", p=128)
    xro = x_out.rearrange("(n p) d -> p n d", p=128)
    ktm_r = ktm_i.rearrange("(n p) d -> p n d", p=128)
    vtm_r = vtm_i.rearrange("(n p) d -> p n d", p=128)

    with ExitStack() as stack:
        sb = lambda n, sh, dt: stack.enter_context(nc.sbuf_tensor(n, sh, dt))
        B = alloc_common(nc, stack, ntt, tpp)
        woc = sb("woc", [128, 8192], BF16)
        Ssb = sb("Ssb", [128, 8 * 128], F32)
        Sbf = sb("Sbf", [128, 128], BF16)
        Zsb = sb("Zsb", [128, 8], F32)
        Zbf = sb("Zbf", [128, 1], BF16)
        Stmp = sb("Stmp", [128, 8 * 128], F32)
        Ztmp = sb("Ztmp", [128, 8], F32)
        msk = sb("msk", [128, 8], F32)
        triu = sb("triu", [128, 128], BF16)
        ones = sb("ones", [128, 1], BF16)
        nwb = sb("nwb", [128, 128], F32)
        nwrow = sb("nwrow", [1, 128], F32)
        onesrow = sb("onesrow", [1, 128], F32)
        qTl = sb("qTl", [128, 8 * tpp], BF16)
        kTl = sb("kTl", [128, 8 * tpp], BF16)
        ktm = sb("ktm", [128, ntt * 1024], BF16)
        vtm = sb("vtm", [128, ntt * 1024], BF16)
        sTm = sb("sTm", [128, 2 * 128], BF16)
        osb = sb("osb", [128, 2 * 128], F32)
        dstat = sb("dstat", [128, 16], F32)
        onb = sb("onb", [128, 2 * 128], BF16)
        oT = sb("oT", [128, 8 * tpp], BF16)
        outT = sb("outT", [128, 8 * tpp], F32)

        p = Prog(nc)
        emit_identities(p, B.idb, B.idf, B.epsc)
        p.g.memset(ones[:, :], 1.0)
        p.g.memset(triu[:, :], 1.0)
        p.g.affine_select(out=triu[:, :], in_=triu[:, :],
                          compare_op=OP.is_ge, fill=0.0, base=0,
                          pattern=[[1, 128]], channel_multiplier=-1)
        p.g.dma_start(out=woc[:, :], in_=wo[:, :])
        p.y.dma_start(out=msk[:, :], in_=pmask[:, :])
        p.y.dma_start(out=nwrow[:, :], in_=nw[:, :])
        p.g.memset(onesrow[:, :], 1.0)
        p.t.matmul(B.pstf[:, 0:128], onesrow[:, :], nwrow[:, :], start=True, stop=True)
        p.v.tensor_copy(nwb[:, :], B.pstf[:, 0:128])
        p.v.memset(Ssb[:, :], 0.0)
        p.v.memset(Zsb[:, :], 0.0)
        for r in range(8):
            p.g.dma_start(out=Stmp[:, :], in_=S_all[r])
            p.g.dma_start(out=Ztmp[:, :], in_=Z_all[r])
            p.v.scalar_tensor_tensor(Ssb[:, :], Stmp[:, :], msk[:, r:r + 1], Ssb[:, :],
                                     OP.mult, OP.add)
            p.v.scalar_tensor_tensor(Zsb[:, :], Ztmp[:, :], msk[:, r:r + 1], Zsb[:, :],
                                     OP.mult, OP.add)

        for pp in range(npass):
            emit_head(p, B, xr_, pp)
            p.y.dma_start(out=qTl.rearrange("p (h t) -> p h t", h=8),
                          in_=qT_i.rearrange("p (h t) -> p h t", h=8)[:, :, pp * tpp:(pp + 1) * tpp])
            p.y.dma_start(out=kTl.rearrange("p (h t) -> p h t", h=8),
                          in_=kT_i.rearrange("p (h t) -> p h t", h=8)[:, :, pp * tpp:(pp + 1) * tpp])
            p.y.dma_start(out=ktm.rearrange("p (n d) -> p n d", d=1024),
                          in_=ktm_r[:, pp * ntt:(pp + 1) * ntt, :])
            p.y.dma_start(out=vtm.rearrange("p (n d) -> p n d", d=1024),
                          in_=vtm_r[:, pp * ntt:(pp + 1) * ntt, :])
            for k in range(ntt):
                for h in range(8):
                    par = h % 2
                    qblk = qTl[:, h * tpp + k * 128:h * tpp + (k + 1) * 128]
                    kblk = kTl[:, h * tpp + k * 128:h * tpp + (k + 1) * 128]
                    ktb = ktm[:, k * 1024 + h * 128:k * 1024 + (h + 1) * 128]
                    vtb = vtm[:, k * 1024 + h * 128:k * 1024 + (h + 1) * 128]
                    Sh = Ssb[:, h * 128:(h + 1) * 128]
                    ps = B.psmm[par]
                    p.t.matmul(ps[:, 0:128], kblk, qblk, start=True, stop=True)
                    stm = sTm[:, par * 128:(par + 1) * 128]
                    p.v.tensor_tensor(stm, ps[:, 0:128], triu[:, :], OP.mult)
                    p.v.tensor_copy(Sbf[:, :], Sh)
                    po = B.psmm[2 + par]
                    p.t.matmul(po[:, 0:128], stm, vtb, start=True, stop=False)
                    p.t.matmul(po[:, 0:128], qblk, Sbf[:, :], start=False, stop=True)
                    p.v.tensor_copy(Zbf[:, :], Zsb[:, h:h + 1])
                    p.t.matmul(ps[:, 192:193], stm, ones[:, :], start=True, stop=False)
                    p.t.matmul(ps[:, 192:193], qblk, Zbf[:, :], start=False, stop=True)
                    p.v.tensor_scalar(dstat[:, 0:1], ps[:, 192:193], -1.0, None, OP.mult)
                    p.v.tensor_tensor(dstat[:, 1:2], ps[:, 192:193], dstat[:, 0:1], OP.max)
                    p.v.tensor_scalar(dstat[:, 2:3], dstat[:, 1:2], 1.0, None, OP.max)
                    p.v.reciprocal(dstat[:, 3:4], dstat[:, 2:3])
                    ob = osb[:, par * 128:(par + 1) * 128]
                    p.v.tensor_scalar(ob, po[:, 0:128], dstat[:, 3:4], None, OP.mult)
                    p.v.scalar_tensor_tensor(B.sq[:, 0:128], ob, 1.0, ob, OP.mult, OP.mult,
                                             accum_out=dstat[:, 4:5])
                    p.v.tensor_scalar(dstat[:, 5:6], dstat[:, 4:5], 1.0 / 128, None, OP.mult)
                    p.s.activation(dstat[:, 6:7], dstat[:, 5:6], AF.Ln, bias=B.epsc[:, 0:1])
                    p.s.activation(dstat[:, 7:8], dstat[:, 6:7], AF.Exp, scale=-0.5)
                    onbb = onb[:, par * 128:(par + 1) * 128]
                    p.v.tensor_scalar(B.sq[:, 128:256], ob, dstat[:, 7:8], None, OP.mult)
                    p.v.tensor_tensor(onbb, B.sq[:, 128:256], nwb[:, :], OP.mult)
                    p.t.transpose(B.pst[:, par * 128:(par + 1) * 128], onbb, B.idb[:, :])
                    p.v.tensor_copy(oT.rearrange("p (h t) -> p h t", h=8)[:, h, k * 128:(k + 1) * 128],
                                    B.pst[:, par * 128:(par + 1) * 128])
                    p.t.matmul(ps[:, 0:128], ktb, vtb, start=True, stop=True)
                    p.v.tensor_tensor(Sh, Sh, ps[:, 0:128], OP.add)
                    p.t.matmul(ps[:, 128:129], ktb, ones[:, :], start=True, stop=True)
                    p.v.tensor_tensor(Zsb[:, h:h + 1], Zsb[:, h:h + 1], ps[:, 128:129], OP.add)
            for og in range(8):
                ps = B.psmm[og % 2]
                for kt in range(8):
                    p.t.matmul(ps[:, :], woc[:, (og * 8 + kt) * 128:(og * 8 + kt + 1) * 128],
                               oT[:, kt * tpp:(kt + 1) * tpp],
                               start=(kt == 0), stop=(kt == 7))
                p.v.tensor_copy(outT[:, og * tpp:(og + 1) * tpp], ps[:, :])
            emit_tail(p, B, outT, xro, pp)

        with nc.Block() as block:
            p.emit(block, stack)
    return nc


def _rhs_resident(w):
    wT = np.ascontiguousarray(w.T)
    F_, O = wT.shape
    a = wT.reshape(F_ // 128, 128, O).transpose(1, 0, 2)
    return np.ascontiguousarray(a.reshape(128, (F_ // 128) * O))


def run_l1_attn(x_full, pm, tpp=512):
    xf = np.ascontiguousarray(x_full.reshape(16384, 1024).astype(np.float32))
    tok = xf.shape[0] // NCORES
    ncA = build_l1a_qkv(tok=tok, tpp=tpp)
    baseA = {
        "wq": _lhsT_resident(pm["Wq"]), "wk": _lhsT_resident(pm["Wk"]),
        "wvr": _rhs_resident(pm["Wv"]), "wkr": _rhs_resident(pm["Wk"]),
    }
    in_maps = []
    for c in range(NCORES):
        m = dict(baseA)
        m["x_in"] = xf[c * tok:(c + 1) * tok]
        in_maps.append(m)
    resA = run_bass_kernel_spmd(ncA, in_maps, core_ids=list(range(NCORES)))
    S_all = np.stack([resA.results[c]["S_o"] for c in range(NCORES)], axis=0)
    Z_all = np.stack([resA.results[c]["Z_o"] for c in range(NCORES)], axis=0)
    ncB = build_l1a_out(tok=tok, tpp=tpp)
    baseB = {"wo": _lhsT_resident(pm["Wo"]), "S_all": S_all, "Z_all": Z_all,
             "nw": pm["norm_w"].reshape(1, 128).astype(np.float32)}
    in_maps = []
    for c in range(NCORES):
        m = dict(baseB)
        m["x_in"] = xf[c * tok:(c + 1) * tok]
        m["qT_i"] = resA.results[c]["qT_o"]
        m["kT_i"] = resA.results[c]["kT_o"]
        m["ktm_i"] = resA.results[c]["ktm_o"]
        m["vtm_i"] = resA.results[c]["vtm_o"]
        pmask = np.zeros((128, 8), np.float32)
        for r in range((c // 4) * 4, c):
            pmask[:, r] = 1.0
        m["pmask"] = pmask
        in_maps.append(m)
    resB = run_bass_kernel_spmd(ncB, in_maps, core_ids=list(range(NCORES)))
    out = np.concatenate([resB.results[c]["x_out"] for c in range(NCORES)], axis=0)
    return out.reshape(2, 8192, 1024)




CH = 128
QS = 16


def emit_gla_state_update(p, B, S, kT, Lc, vtm_chunk, kfac, khf, khtm, gam, psg):
    for kb in range(8):
        Lkb = Lc[kb]
        p.v.tensor_scalar(kfac[:, :], Lkb, Lkb[:, CH - 1:CH], None, OP.subtract)
        p.s.activation(kfac[:, :], kfac[:, :], AF.Exp, scale=-1.0)
        p.v.tensor_tensor(khf[:, :], kT[kb], kfac[:, :], OP.mult)
        p.t.transpose(B.pst[:, 0:128], khf[:, :], B.idb[:, :])
        p.v.tensor_copy(khtm[:, kb * 128:(kb + 1) * 128], B.pst[:, 0:128])
        p.s.activation(gam[:, kb:kb + 1], Lkb[:, CH - 1:CH], AF.Exp)
    for kb in range(8):
        p.v.tensor_scalar(S[:, kb * 1024:(kb + 1) * 1024],
                          S[:, kb * 1024:(kb + 1) * 1024],
                          gam[:, kb:kb + 1], None, OP.mult)
        for vh in range(2):
            ps = psg[(kb * 2 + vh) % 2]
            p.t.matmul(ps[:, :], khtm[:, kb * 128:(kb + 1) * 128],
                       vtm_chunk[:, vh * 512:(vh + 1) * 512], start=True, stop=True)
            p.v.tensor_tensor(S[:, kb * 1024 + vh * 512:kb * 1024 + (vh + 1) * 512],
                              S[:, kb * 1024 + vh * 512:kb * 1024 + (vh + 1) * 512],
                              ps[:, :], OP.add)


def build_l2a_qkv(tok=2048, tpp=512):
    nc = bass.Bass("TRN2", target_bir_lowering=False)
    npass, ntt = tok // tpp, tpp // 128
    assert tpp == 512

    x_in = nc.dram_tensor("x_in", [tok, 1024], F32, kind="ExternalInput").ap()
    wq = nc.dram_tensor("wq", [128, 8192], F32, kind="ExternalInput").ap()
    wk = nc.dram_tensor("wk", [128, 8192], F32, kind="ExternalInput").ap()
    wg = nc.dram_tensor("wg", [128, 8192], F32, kind="ExternalInput").ap()
    wvr = nc.dram_tensor("wvr", [128, 8192], F32, kind="ExternalInput").ap()
    bqkg = nc.dram_tensor("bqkg", [128, 24], F32, kind="ExternalInput").ap()
    bvrow = nc.dram_tensor("bvrow", [1, 1024], F32, kind="ExternalInput").ap()
    qT_o = nc.dram_tensor("qT_o", [128, 8 * tok], BF16, kind="ExternalOutput").ap()
    kT_o = nc.dram_tensor("kT_o", [128, 8 * tok], BF16, kind="ExternalOutput").ap()
    vtm_o = nc.dram_tensor("vtm_o", [tok, 1024], BF16, kind="ExternalOutput").ap()
    L_o = nc.dram_tensor("L_o", [128, 8 * tok], F32, kind="ExternalOutput").ap()
    S_o = nc.dram_tensor("S_o", [128, 8 * 1024], F32, kind="ExternalOutput").ap()
    lg_o = nc.dram_tensor("lg_o", [128, 8], F32, kind="ExternalOutput").ap()

    xr_ = x_in.rearrange("(n p) d -> p n d", p=128)
    vtm_r = vtm_o.rearrange("(n p) d -> p n d", p=128)

    with ExitStack() as stack:
        sb = lambda n, sh, dt: stack.enter_context(nc.sbuf_tensor(n, sh, dt))
        B = alloc_common(nc, stack, ntt, tpp)
        wqc = sb("wqc", [128, 8192], BF16)
        wkc = sb("wkc", [128, 8192], BF16)
        wgc = sb("wgc", [128, 8192], BF16)
        wvrc = sb("wvrc", [128, 8192], BF16)
        bqk = sb("bqk", [128, 24], F32)
        bvb = sb("bvb", [128, 1024], F32)
        bvr = sb("bvr", [1, 1024], F32)
        onesrow = sb("onesrow", [1, 128], F32)
        qT = sb("qT", [128, 8 * tpp], BF16)
        kT = sb("kT", [128, 8 * tpp], BF16)
        vtm = sb("vtm", [128, ntt * 1024], BF16)
        Lb = sb("Lb", [128, 8 * tpp], F32)
        lgT = sb("lgT", [128, tpp], F32)
        Ssb = sb("Ssb", [128, 8 * 1024], F32)
        logG = sb("logG", [128, 8], F32)
        kfac = sb("kfac", [128, CH], F32)
        khf = sb("khf", [128, CH], BF16)
        khtm = sb("khtm", [128, 1024], BF16)
        gam = sb("gam", [128, 8], F32)
        onesc = sb("onesc", [128, CH], F32)

        p = Prog(nc)
        emit_identities(p, B.idb, B.idf, B.epsc)
        p.g.memset(onesc[:, :], 1.0)
        p.g.memset(onesrow[:, :], 1.0)
        p.v.memset(Ssb[:, :], 0.0)
        p.v.memset(logG[:, :], 0.0)
        for w_ap, wt in ((wq, wqc), (wk, wkc), (wg, wgc), (wvr, wvrc)):
            p.g.dma_start(out=wt[:, :], in_=w_ap[:, :])
        p.y.dma_start(out=bqk[:, :], in_=bqkg[:, :])
        p.y.dma_start(out=bvr[:, :], in_=bvrow[:, :])
        p.t.matmul(B.pstf[:, 0:512], onesrow[:, :], bvr[:, 0:512], start=True, stop=True)
        p.v.tensor_copy(bvb[:, 0:512], B.pstf[:, 0:512])
        p.t.matmul(B.pstf[:, 0:512], onesrow[:, :], bvr[:, 512:1024], start=True, stop=True)
        p.v.tensor_copy(bvb[:, 512:1024], B.pstf[:, 0:512])

        for pp in range(npass):
            emit_head(p, B, xr_, pp)
            for wt, dst, boff, fn in ((wqc, qT, 0, AF.Identity), (wkc, kT, 8, AF.Sigmoid)):
                for og in range(8):
                    ps = B.psmm[og % 2]
                    for kt in range(8):
                        p.t.matmul(ps[:, :], wt[:, (og * 8 + kt) * 128:(og * 8 + kt + 1) * 128],
                                   B.xlnT[:, kt * tpp:(kt + 1) * tpp],
                                   start=(kt == 0), stop=(kt == 7))
                    p.s.activation(dst[:, og * tpp:(og + 1) * tpp], ps[:, :],
                                   fn, bias=bqk[:, boff + og:boff + og + 1])
            for og in range(8):
                ps = B.psmm[og % 2]
                for kt in range(8):
                    p.t.matmul(ps[:, :], wgc[:, (og * 8 + kt) * 128:(og * 8 + kt + 1) * 128],
                               B.xlnT[:, kt * tpp:(kt + 1) * tpp],
                               start=(kt == 0), stop=(kt == 7))
                p.s.activation(lgT[:, :], ps[:, :], AF.Sigmoid, bias=bqk[:, 16 + og:16 + og + 1])
                p.s.activation(lgT[:, :], lgT[:, :], AF.Ln)
                for cc in range(tpp // CH):
                    p.v.tensor_tensor_scan(Lb[:, og * tpp + cc * CH:og * tpp + (cc + 1) * CH],
                                           onesc[:, :], lgT[:, cc * CH:(cc + 1) * CH],
                                           0.0, OP.mult, OP.add)
            for k in range(ntt):
                for oc in range(2):
                    ps = B.psmm[2 + oc]
                    for kt in range(8):
                        p.t.matmul(ps[:, :], B.xlnT[:, kt * tpp + k * 128:kt * tpp + (k + 1) * 128],
                                   wvrc[:, kt * 1024 + oc * 512:kt * 1024 + (oc + 1) * 512],
                                   start=(kt == 0), stop=(kt == 7))
                    p.v.tensor_tensor(vtm[:, k * 1024 + oc * 512:k * 1024 + (oc + 1) * 512],
                                      ps[:, :], bvb[:, oc * 512:(oc + 1) * 512], OP.add)
                p.g.dma_start(out=vtm_r[:, pp * ntt + k, :], in_=vtm[:, k * 1024:(k + 1) * 1024])
            p.g.dma_start(out=qT_o.rearrange("p (h t) -> p h t", h=8)[:, :, pp * tpp:(pp + 1) * tpp],
                          in_=qT.rearrange("p (h t) -> p h t", h=8))
            p.g.dma_start(out=kT_o.rearrange("p (h t) -> p h t", h=8)[:, :, pp * tpp:(pp + 1) * tpp],
                          in_=kT.rearrange("p (h t) -> p h t", h=8))
            p.g.dma_start(out=L_o.rearrange("p (h t) -> p h t", h=8)[:, :, pp * tpp:(pp + 1) * tpp],
                          in_=Lb.rearrange("p (h t) -> p h t", h=8))
            for cc in range(tpp // CH):
                Lc = [Lb[:, kb * tpp + cc * CH:kb * tpp + (cc + 1) * CH] for kb in range(8)]
                kTc = [kT[:, kb * tpp + cc * CH:kb * tpp + (cc + 1) * CH] for kb in range(8)]
                vtmc = vtm[:, cc * 1024:(cc + 1) * 1024]
                emit_gla_state_update(p, B, Ssb, kTc, Lc, vtmc, kfac, khf, khtm, gam,
                                      (B.psmm[2], B.psmm[3]))
                for kb in range(8):
                    p.v.tensor_tensor(logG[:, kb:kb + 1], logG[:, kb:kb + 1],
                                      Lc[kb][:, CH - 1:CH], OP.add)

        p.g.dma_start(out=S_o[:, :], in_=Ssb[:, :])
        p.g.dma_start(out=lg_o[:, :], in_=logG[:, :])
        with nc.Block() as block:
            p.emit(block, stack)
    return nc


def build_l2a_out(tok=2048, tpp=512):
    nc = bass.Bass("TRN2", target_bir_lowering=False)
    npass, ntt = tok // tpp, tpp // 128
    assert tpp == 512

    x_in = nc.dram_tensor("x_in", [tok, 1024], F32, kind="ExternalInput").ap()
    qT_i = nc.dram_tensor("qT_i", [128, 8 * tok], BF16, kind="ExternalInput").ap()
    kT_i = nc.dram_tensor("kT_i", [128, 8 * tok], BF16, kind="ExternalInput").ap()
    vtm_i = nc.dram_tensor("vtm_i", [tok, 1024], BF16, kind="ExternalInput").ap()
    L_i = nc.dram_tensor("L_i", [128, 8 * tok], F32, kind="ExternalInput").ap()
    S_all = nc.dram_tensor("S_all", [8, 128, 8 * 1024], F32, kind="ExternalInput").ap()
    lg_all = nc.dram_tensor("lg_all", [8, 128, 8], F32, kind="ExternalInput").ap()
    pmask = nc.dram_tensor("pmask", [128, 8], F32, kind="ExternalInput").ap()
    m2 = nc.dram_tensor("m2", [128, 64], F32, kind="ExternalInput").ap()
    wo = nc.dram_tensor("wo", [128, 8192], F32, kind="ExternalInput").ap()
    borow = nc.dram_tensor("borow", [128, 8], F32, kind="ExternalInput").ap()
    x_out = nc.dram_tensor("x_out", [tok, 1024], F32, kind="ExternalOutput").ap()

    xr_ = x_in.rearrange("(n p) d -> p n d", p=128)
    xro = x_out.rearrange("(n p) d -> p n d", p=128)
    vtm_r = vtm_i.rearrange("(n p) d -> p n d", p=128)

    with ExitStack() as stack:
        sb = lambda n, sh, dt: stack.enter_context(nc.sbuf_tensor(n, sh, dt))
        B = alloc_common(nc, stack, ntt, tpp)
        woc = sb("woc", [128, 8192], BF16)
        bo = sb("bo", [128, 8], F32)
        Ssb = sb("Ssb", [128, 8 * 1024], F32)
        Sbf = sb("Sbf", [128, 8 * 1024], BF16)
        Stmp = sb("Stmp", [128, 1024], F32)
        lgtmp = sb("lgtmp", [128, 8], F32)
        wr_ = sb("wr_", [128, 8], F32)
        msk = sb("msk", [128, 8], F32)
        m2c = sb("m2c", [128, 64], F32)
        qTl = sb("qTl", [128, 8 * tpp], BF16)
        kTl = sb("kTl", [128, 8 * tpp], BF16)
        Ll = sb("Ll", [128, 8 * tpp], F32)
        vtm = sb("vtm", [128, ntt * 1024], BF16)
        qt_ = sb("qt_", [128, 8 * CH], BF16)
        efac = sb("efac", [128, CH], F32)
        qs_ = sb("qs_", [128, 8 * QS], BF16)
        kt_ = sb("kt_", [128, CH], BF16)
        sTb = sb("sTb", [128, QS], BF16)
        maskall = sb("maskall", [128, (CH // QS) * QS], BF16)
        kfac = sb("kfac", [128, CH], F32)
        khf = sb("khf", [128, CH], BF16)
        khtm = sb("khtm", [128, 1024], BF16)
        gam = sb("gam", [128, 8], F32)
        oT = sb("oT", [128, 8 * tpp], BF16)
        outT = sb("outT", [128, 8 * tpp], F32)

        p = Prog(nc)
        emit_identities(p, B.idb, B.idf, B.epsc)
        p.g.memset(maskall[:, :], 1.0)
        p.g.affine_select(out=maskall.rearrange("p (i q) -> p i q", q=QS),
                          in_=maskall.rearrange("p (i q) -> p i q", q=QS),
                          compare_op=OP.is_ge, fill=0.0, base=0,
                          pattern=[[QS, CH // QS], [1, QS]], channel_multiplier=-1)
        p.g.dma_start(out=woc[:, :], in_=wo[:, :])
        p.y.dma_start(out=bo[:, :], in_=borow[:, :])
        p.y.dma_start(out=msk[:, :], in_=pmask[:, :])
        p.y.dma_start(out=m2c[:, :], in_=m2[:, :])
        p.v.memset(Ssb[:, :], 0.0)
        for r in range(8):
            p.v.memset(wr_[:, :], 0.0)
            for j in range(8):
                p.g.dma_start(out=lgtmp[:, :], in_=lg_all[j])
                p.v.scalar_tensor_tensor(wr_[:, :], lgtmp[:, :], m2c[:, r * 8 + j:r * 8 + j + 1],
                                         wr_[:, :], OP.mult, OP.add)
            p.s.activation(wr_[:, :], wr_[:, :], AF.Exp)
            p.v.tensor_scalar(wr_[:, :], wr_[:, :], msk[:, r:r + 1], None, OP.mult)
            for kb in range(8):
                p.g.dma_start(out=Stmp[:, :], in_=S_all[r][:, kb * 1024:(kb + 1) * 1024])
                p.v.scalar_tensor_tensor(Ssb[:, kb * 1024:(kb + 1) * 1024], Stmp[:, :],
                                         wr_[:, kb:kb + 1], Ssb[:, kb * 1024:(kb + 1) * 1024],
                                         OP.mult, OP.add)

        for pp in range(npass):
            emit_head(p, B, xr_, pp)
            p.y.dma_start(out=qTl.rearrange("p (h t) -> p h t", h=8),
                          in_=qT_i.rearrange("p (h t) -> p h t", h=8)[:, :, pp * tpp:(pp + 1) * tpp])
            p.y.dma_start(out=kTl.rearrange("p (h t) -> p h t", h=8),
                          in_=kT_i.rearrange("p (h t) -> p h t", h=8)[:, :, pp * tpp:(pp + 1) * tpp])
            p.y.dma_start(out=Ll.rearrange("p (h t) -> p h t", h=8),
                          in_=L_i.rearrange("p (h t) -> p h t", h=8)[:, :, pp * tpp:(pp + 1) * tpp])
            p.y.dma_start(out=vtm.rearrange("p (n d) -> p n d", d=1024),
                          in_=vtm_r[:, pp * ntt:(pp + 1) * ntt, :])
            for cc in range(tpp // CH):
                Lc = [Ll[:, kb * tpp + cc * CH:kb * tpp + (cc + 1) * CH] for kb in range(8)]
                kTc = [kTl[:, kb * tpp + cc * CH:kb * tpp + (cc + 1) * CH] for kb in range(8)]
                qTc = [qTl[:, kb * tpp + cc * CH:kb * tpp + (cc + 1) * CH] for kb in range(8)]
                vtmc = vtm[:, cc * 1024:(cc + 1) * 1024]
                for kb in range(8):
                    p.v.tensor_copy(Sbf[:, kb * 1024:(kb + 1) * 1024],
                                    Ssb[:, kb * 1024:(kb + 1) * 1024])
                for kb in range(8):
                    p.s.activation(efac[:, :], Lc[kb], AF.Exp)
                    p.v.tensor_tensor(qt_[:, kb * CH:(kb + 1) * CH], qTc[kb], efac[:, :], OP.mult)
                for half in range(2):
                    po = B.psmm[half]
                    for vb4 in range(4):
                        vb = half * 4 + vb4
                        for kb in range(8):
                            p.t.matmul(po[:, vb4 * 128:(vb4 + 1) * 128],
                                       Sbf[:, kb * 1024 + vb * 128:kb * 1024 + (vb + 1) * 128],
                                       qt_[:, kb * CH:(kb + 1) * CH],
                                       start=(vb4 == 0 and kb == 0), stop=False)
                for i in range(CH // QS):
                    i0 = i * QS
                    for kb in range(8):
                        p.v.tensor_scalar(efac[:, 0:QS], Lc[kb][:, i0:i0 + QS],
                                          Lc[kb][:, i0:i0 + 1], None, OP.subtract)
                        p.s.activation(efac[:, 0:QS], efac[:, 0:QS], AF.Exp)
                        p.v.tensor_tensor(qs_[:, kb * QS:(kb + 1) * QS],
                                          qTc[kb][:, i0:i0 + QS], efac[:, 0:QS], OP.mult)
                    ns = i0 + QS
                    pss = B.pstf
                    for kb in range(8):
                        p.v.tensor_scalar(kfac[:, 0:ns], Lc[kb][:, 0:ns],
                                          Lc[kb][:, i0:i0 + 1], None, OP.subtract)
                        p.s.activation(kfac[:, 0:ns], kfac[:, 0:ns], AF.Exp, scale=-1.0)
                        p.v.tensor_tensor(kt_[:, 0:ns], kTc[kb][:, 0:ns], kfac[:, 0:ns], OP.mult)
                        p.t.matmul(pss[0:ns, 0:QS], kt_[:, 0:ns], qs_[:, kb * QS:(kb + 1) * QS],
                                   start=(kb == 0), stop=(kb == 7))
                    p.v.tensor_tensor(sTb[0:ns, :], pss[0:ns, 0:QS],
                                      maskall[0:ns, i * QS:(i + 1) * QS], OP.mult)
                    for half in range(2):
                        po = B.psmm[half]
                        for vb4 in range(4):
                            vb = half * 4 + vb4
                            stop = (i == CH // QS - 1 and vb4 == 3)
                            p.t.matmul(po[:, vb4 * 128 + i * QS:vb4 * 128 + (i + 1) * QS],
                                       vtmc[0:ns, vb * 128:(vb + 1) * 128], sTb[0:ns, :],
                                       start=False, stop=stop)
                for half in range(2):
                    po = B.psmm[half]
                    for vb4 in range(4):
                        vb = half * 4 + vb4
                        p.v.tensor_copy(oT[:, vb * tpp + cc * CH:vb * tpp + (cc + 1) * CH],
                                        po[:, vb4 * 128:(vb4 + 1) * 128])
                emit_gla_state_update(p, B, Ssb, kTc, Lc, vtmc, kfac, khf, khtm, gam,
                                      (B.psmm[2], B.psmm[3]))
            for og in range(8):
                ps = B.psmm[2 + og % 2]
                for kt in range(8):
                    p.t.matmul(ps[:, :], woc[:, (og * 8 + kt) * 128:(og * 8 + kt + 1) * 128],
                               oT[:, kt * tpp:(kt + 1) * tpp],
                               start=(kt == 0), stop=(kt == 7))
                p.s.activation(outT[:, og * tpp:(og + 1) * tpp], ps[:, :],
                               AF.Identity, bias=bo[:, og:og + 1])
            emit_tail(p, B, outT, xro, pp)

        with nc.Block() as block:
            p.emit(block, stack)
    return nc


def l2_inputs_A(pm):
    return {
        "wq": _lhsT_resident(pm["Wq"].T), "wk": _lhsT_resident(pm["Wk"].T),
        "wg": _lhsT_resident(pm["Wg"].T), "wvr": _rhs_resident(pm["Wv"].T),
        "bqkg": np.concatenate([pm["bq"].reshape(8, 128).T, pm["bk"].reshape(8, 128).T,
                                pm["bg"].reshape(8, 128).T], axis=1).astype(np.float32),
        "bvrow": pm["bv"].reshape(1, 1024).astype(np.float32),
    }


def run_l2_attn(x_full, pm, tpp=512):
    xf = np.ascontiguousarray(x_full.reshape(16384, 1024).astype(np.float32))
    tok = xf.shape[0] // NCORES
    ncA = build_l2a_qkv(tok=tok, tpp=tpp)
    baseA = l2_inputs_A(pm)
    in_maps = []
    for c in range(NCORES):
        m = dict(baseA)
        m["x_in"] = xf[c * tok:(c + 1) * tok]
        in_maps.append(m)
    resA = run_bass_kernel_spmd(ncA, in_maps, core_ids=list(range(NCORES)))
    S_all = np.stack([resA.results[c]["S_o"] for c in range(NCORES)], axis=0)
    lg_all = np.stack([resA.results[c]["lg_o"] for c in range(NCORES)], axis=0)
    ncB = build_l2a_out(tok=tok, tpp=tpp)
    baseB = {"wo": _lhsT_resident(pm["Wo"].T), "S_all": S_all, "lg_all": lg_all,
             "borow": pm["bo"].reshape(8, 128).T.astype(np.float32).copy()}
    in_maps = []
    for c in range(NCORES):
        m = dict(baseB)
        m["x_in"] = xf[c * tok:(c + 1) * tok]
        for nm in ("qT_o", "kT_o", "vtm_o", "L_o"):
            m[nm.replace("_o", "_i")] = resA.results[c][nm]
        pmask = np.zeros((128, 8), np.float32)
        m2 = np.zeros((128, 64), np.float32)
        for r in range((c // 4) * 4, c):
            pmask[:, r] = 1.0
            for j in range(r + 1, c):
                m2[:, r * 8 + j] = 1.0
        m["pmask"] = pmask
        m["m2"] = m2
        in_maps.append(m)
    resB = run_bass_kernel_spmd(ncB, in_maps, core_ids=list(range(NCORES)))
    out = np.concatenate([resB.results[c]["x_out"] for c in range(NCORES)], axis=0)
    return out.reshape(2, 8192, 1024)




def build_l3a_qkv(tok=2048, tpp=512):
    nc = bass.Bass("TRN2", target_bir_lowering=False)
    npass, ntt = tok // tpp, tpp // 128
    assert tpp == 512

    x_in = nc.dram_tensor("x_in", [tok, 1024], F32, kind="ExternalInput").ap()
    wq = nc.dram_tensor("wq", [128, 8192], F32, kind="ExternalInput").ap()
    wk = nc.dram_tensor("wk", [128, 8192], F32, kind="ExternalInput").ap()
    wvr = nc.dram_tensor("wvr", [128, 8192], F32, kind="ExternalInput").ap()
    qT_o = nc.dram_tensor("qT_o", [128, 8 * tok], BF16, kind="ExternalOutput").ap()
    kT_o = nc.dram_tensor("kT_o", [128, 8 * tok], BF16, kind="ExternalOutput").ap()
    vtm_o = nc.dram_tensor("vtm_o", [tok, 1024], BF16, kind="ExternalOutput").ap()

    xr_ = x_in.rearrange("(n p) d -> p n d", p=128)
    vtm_r = vtm_o.rearrange("(n p) d -> p n d", p=128)

    with ExitStack() as stack:
        sb = lambda n, sh, dt: stack.enter_context(nc.sbuf_tensor(n, sh, dt))
        B = alloc_common(nc, stack, ntt, tpp)
        wqc = sb("wqc", [128, 8192], BF16)
        wkc = sb("wkc", [128, 8192], BF16)
        wvrc = sb("wvrc", [128, 8192], BF16)
        qT = sb("qT", [128, 8 * tpp], BF16)
        kT = sb("kT", [128, 8 * tpp], BF16)
        vtm = sb("vtm", [128, ntt * 1024], BF16)

        p = Prog(nc)
        emit_identities(p, B.idb, B.idf, B.epsc)
        for w_ap, wt in ((wq, wqc), (wk, wkc), (wvr, wvrc)):
            p.g.dma_start(out=wt[:, :], in_=w_ap[:, :])
        for pp in range(npass):
            emit_head(p, B, xr_, pp)
            for wt, dst, hbm in ((wqc, qT, qT_o), (wkc, kT, kT_o)):
                for og in range(8):
                    ps = B.psmm[og % 2]
                    for kt in range(8):
                        p.t.matmul(ps[:, :], wt[:, (og * 8 + kt) * 128:(og * 8 + kt + 1) * 128],
                                   B.xlnT[:, kt * tpp:(kt + 1) * tpp],
                                   start=(kt == 0), stop=(kt == 7))
                    p.v.tensor_copy(dst[:, og * tpp:(og + 1) * tpp], ps[:, :])
                p.g.dma_start(out=hbm.rearrange("p (h t) -> p h t", h=8)[:, :, pp * tpp:(pp + 1) * tpp],
                              in_=dst.rearrange("p (h t) -> p h t", h=8))
            for k in range(ntt):
                for oc in range(2):
                    ps = B.psmm[2 + oc]
                    for kt in range(8):
                        p.t.matmul(ps[:, :], B.xlnT[:, kt * tpp + k * 128:kt * tpp + (k + 1) * 128],
                                   wvrc[:, kt * 1024 + oc * 512:kt * 1024 + (oc + 1) * 512],
                                   start=(kt == 0), stop=(kt == 7))
                    p.v.tensor_copy(vtm[:, k * 1024 + oc * 512:k * 1024 + (oc + 1) * 512], ps[:, :])
                p.g.dma_start(out=vtm_r[:, pp * ntt + k, :], in_=vtm[:, k * 1024:(k + 1) * 1024])
        with nc.Block() as block:
            p.emit(block, stack)
    return nc


L3_SLOTS = (2048, 4096, 6144, 8192)


def build_l3a_out(tok=2048, tpp=512, slots=L3_SLOTS):
    nc = bass.Bass("TRN2", target_bir_lowering=False)
    npass, ntt = tok // tpp, tpp // 128
    KVQ = 1024
    KB = 128
    KVTOT = sum(slots)
    assert tpp == 512 and ntt == 4 and npass == len(slots)

    x_in = nc.dram_tensor("x_in", [tok, 1024], F32, kind="ExternalInput").ap()
    qT_i = nc.dram_tensor("qT_i", [128, 16 * tok], BF16, kind="ExternalInput").ap()
    kT_f = nc.dram_tensor("kT_f", [128, 8 * KVTOT], BF16, kind="ExternalInput").ap()
    vtm_f = nc.dram_tensor("vtm_f", [KVTOT, 1040], BF16, kind="ExternalInput").ap()
    wo = nc.dram_tensor("wo", [128, 8192], F32, kind="ExternalInput").ap()
    x_out = nc.dram_tensor("x_out", [tok, 1024], F32, kind="ExternalOutput").ap()

    xr_ = x_in.rearrange("(n p) d -> p n d", p=128)
    xro = x_out.rearrange("(n p) d -> p n d", p=128)
    vtm_fr = vtm_f.rearrange("(n p) d -> p n d", p=128)

    with ExitStack() as stack:
        sb = lambda n, sh, dt: stack.enter_context(nc.sbuf_tensor(n, sh, dt))
        B = alloc_common(nc, stack, ntt, tpp, ln_bufs=False, psmm_n=2, psum_tp=False)
        pss2 = [stack.enter_context(nc.psum_tensor(f"pss2_{i}", [128, 1024], F32))
                for i in range(3)]
        B.pstf = pss2[0][:, 0:512]
        woc = sb("woc", [128, 8192], BF16)
        qTp = sb("qTp", [128, 16 * tpp], BF16)
        kq = sb("kq", [128, 2 * 8 * KVQ], BF16)
        vq = sb("vq", [128, 2 * (KVQ // 128) * 1040], BF16)
        cmask = sb("cmask", [128, 4 * 512], BF16)
        pexp = sb("pexp", [128, 3 * 1024], BF16)
        oacc = sb("oacc", [128, 16 * ntt * 65], F32)
        lrec = sb("lrec", [128, 64], F32)
        onb = sb("onb", [128, 4 * 64], F32)
        oT = sb("oT", [128, 8 * tpp], BF16)
        outT = sb("outT", [128, 8 * tpp], F32)
        neg8 = sb("neg8", [128, 1], F32)

        p = Prog(nc)
        emit_identities(p, B.idb, B.idf, B.epsc)
        p.g.memset(neg8[:, :], -8.0)
        p.g.memset(cmask[:, :], 1.0)
        p.g.affine_select(out=cmask.rearrange("p (r c) -> p r c", r=4),
                          in_=cmask.rearrange("p (r c) -> p r c", r=4),
                          compare_op=OP.is_ge, fill=0.0, base=0,
                          pattern=[[-128, 4], [1, 512]], channel_multiplier=-1)
        p.g.dma_start(out=woc[:, :], in_=wo[:, :])

        gpair = [0]
        for pp in range(npass):
            S = slots[pp]
            off = sum(slots[:pp])
            p.y.dma_start(out=B.xt.rearrange("p (n d) -> p n d", d=1024),
                          in_=xr_[:, pp * ntt:(pp + 1) * ntt, :])
            p.y.dma_start(out=qTp.rearrange("p (h t) -> p h t", h=16),
                          in_=qT_i.rearrange("p (h t) -> p h t", h=16)[:, :, pp * tpp:(pp + 1) * tpp])
            qpos0 = S - tpp
            frontier = S
            nq = S // KVQ
            for qq in range(nq):
                kb0 = qq * KVQ
                nkc = KVQ // KB
                buf = qq % 2
                kqb = kq.rearrange("p (b h t) -> p b h t", b=2, h=8)
                p.y.dma_start(out=kqb[:, buf, :, :],
                              in_=kT_f.rearrange("p (h t) -> p h t", h=8)[:, :, off + kb0:off + kb0 + KVQ])
                p.y.dma_start(out=vq.rearrange("p (b n d) -> p b n d", b=2, d=1040)[:, buf, :, :],
                              in_=vtm_fr[:, (off + kb0) // 128:(off + kb0) // 128 + nkc, :])
                for h in range(16):
                    og = h // 2
                    po = B.psmm[h % 2]
                    qs = qTp[:, h * tpp:(h + 1) * tpp]
                    npair = nkc // 2
                    base = gpair[0]
                    gpair[0] += npair

                    def score_pair(j):
                        for side in range(2):
                            c = 2 * j + side
                            ks = kq[:, (buf * 8 + og) * KVQ + c * KB:(buf * 8 + og) * KVQ + (c + 1) * KB]
                            p.t.matmul(pss2[(base + j) % 3][:, side * 512:(side + 1) * 512],
                                       ks, qs, start=True, stop=True)

                    score_pair(0)
                    score_pair(1)
                    iomm = 0
                    for j in range(npair):
                        pexj = pexp[:, ((base + j) % 3) * 1024:((base + j) % 3 + 1) * 1024]
                        p.s.activation(pexj, pss2[(base + j) % 3][:, :], AF.Exp,
                                       scale=0.125, bias=neg8[:, 0:1])
                        if j + 2 < npair:
                            score_pair(j + 2)
                        for side in range(2):
                            c = 2 * j + side
                            kidx = kb0 + c * KB
                            pex = pexj[:, side * 512:(side + 1) * 512]
                            r = (kidx - qpos0) // 128
                            if 0 <= r <= 3:
                                p.v.tensor_tensor(pex, pex,
                                                  cmask[:, r * 512:(r + 1) * 512], OP.mult)
                            vs = vq[:, (buf * (KVQ // 128) + c) * 1040 + h * 65:
                                    (buf * (KVQ // 128) + c) * 1040 + (h + 1) * 65]
                            for k in range(ntt):
                                if kidx < qpos0 + (k + 1) * 128:
                                    p.t.matmul(po[:, k * 65:(k + 1) * 65],
                                               pex[:, k * 128:(k + 1) * 128], vs,
                                               start=(c == 0 and k == 0),
                                               stop=(c == nkc - 1 and k == ntt - 1))
                    sl = oacc[:, h * ntt * 65:(h * ntt + 4) * 65]
                    if qq == 0:
                        p.v.tensor_copy(sl, po[:, 0:4 * 65])
                    else:
                        p.v.tensor_tensor(sl, sl, po[:, 0:4 * 65], OP.add)
            p.v.reciprocal(lrec.rearrange("p (j c) -> p j c", c=1),
                           oacc.rearrange("p (j c) -> p j c", c=65)[:, :, 64:65])
            for og in range(8):
                for half in range(2):
                    h = 2 * og + half
                    for k in range(ntt):
                        j = h * ntt + k
                        ob = onb[:, k * 64:(k + 1) * 64]
                        p.v.tensor_scalar(ob, oacc[:, j * 65:j * 65 + 64],
                                          lrec[:, j:j + 1], None, OP.mult)
                        p.t.transpose(B.pstf[0:64, k * 128:(k + 1) * 128], ob, B.idf[:, :])
                    p.v.tensor_copy(oT[half * 64:half * 64 + 64, og * tpp:(og + 1) * tpp],
                                    B.pstf[0:64, :])
            for og in range(8):
                ps = B.psmm[og % 2]
                for kt in range(8):
                    p.t.matmul(ps[:, :], woc[:, (og * 8 + kt) * 128:(og * 8 + kt + 1) * 128],
                               oT[:, kt * tpp:(kt + 1) * tpp],
                               start=(kt == 0), stop=(kt == 7))
                p.v.tensor_copy(outT[:, og * tpp:(og + 1) * tpp], ps[:, :])
            emit_tail(p, B, outT, xro, pp)

        with nc.Block() as block:
            p.emit(block, stack)
    return nc


def run_l3_attn(x_full, pm, tpp=512):
    xf = np.ascontiguousarray(x_full.reshape(16384, 1024).astype(np.float32))
    tok = xf.shape[0] // NCORES
    seq = 8192
    ncA = build_l3a_qkv(tok=tok, tpp=tpp)
    WQ = pm["W_Q"].reshape(1024, 1024)
    WK = pm["W_K"].reshape(1024, 1024)
    WV = pm["W_V"].reshape(1024, 1024)
    baseA = {"wq": _lhsT_resident(WQ), "wk": _lhsT_resident(WK), "wvr": _rhs_resident(WV)}
    in_maps = []
    for c in range(NCORES):
        m = dict(baseA)
        m["x_in"] = xf[c * tok:(c + 1) * tok]
        in_maps.append(m)
    resA = run_bass_kernel_spmd(ncA, in_maps, core_ids=list(range(NCORES)))
    slots = L3_SLOTS
    KVTOT = sum(slots)
    ncB = build_l3a_out(tok=tok, tpp=tpp, slots=slots)
    baseB = {"wo": _lhsT_resident(pm["W_O"])}
    kT_b = {}
    v_b = {}
    qT_b = {}
    for b in range(2):
        kT_b[b] = np.concatenate([resA.results[b * 4 + i]["kT_o"].reshape(128, 8, tok)
                                  for i in range(4)], axis=2)
        qT_b[b] = np.concatenate([resA.results[b * 4 + i]["qT_o"].reshape(128, 8, tok)
                                  for i in range(4)], axis=2)
        vr = np.concatenate([resA.results[b * 4 + i]["vtm_o"] for i in range(4)],
                            axis=0).reshape(seq, 16, 64)
        v_b[b] = np.concatenate(
            [np.concatenate([vr[:, i, :], np.ones((seq, 1), vr.dtype)], axis=1)
             for i in range(16)], axis=1)
    in_maps = []
    for c in range(NCORES):
        b, posq = c // 4, (c % 4)
        blocks = [posq, 7 - posq, 8 + posq, 15 - posq]
        m = dict(baseB)
        xb = xf[b * seq:(b + 1) * seq]
        m["x_in"] = np.ascontiguousarray(
            np.concatenate([xb[j * tpp:(j + 1) * tpp] for j in blocks], axis=0))
        qg = np.concatenate([qT_b[b][:, :, j * tpp:(j + 1) * tpp] for j in blocks],
                            axis=2)
        qz = np.zeros((128, 16, tok), dtype=qg.dtype)
        for h in range(16):
            rows = slice(0, 64) if h % 2 == 0 else slice(64, 128)
            qz[rows, h] = qg[rows, h // 2]
        m["qT_i"] = np.ascontiguousarray(qz.reshape(128, 16 * tok))
        kT = np.zeros((128, 8, KVTOT), dtype=kT_b[b].dtype)
        v = np.zeros((KVTOT, 1040), dtype=v_b[b].dtype)
        off = 0
        for pp, j in enumerate(blocks):
            S = slots[pp]
            L = (j + 1) * tpp
            kT[:, :, off + S - L:off + S] = kT_b[b][:, :, :L]
            v[off + S - L:off + S] = v_b[b][:L]
            off += S
        m["kT_f"] = np.ascontiguousarray(kT.reshape(128, 8 * KVTOT))
        m["vtm_f"] = v
        in_maps.append(m)
    resB = run_bass_kernel_spmd(ncB, in_maps, core_ids=list(range(NCORES)))
    out = np.empty((2, seq, 1024), np.float32)
    for c in range(NCORES):
        b, posq = c // 4, (c % 4)
        blocks = [posq, 7 - posq, 8 + posq, 15 - posq]
        xo = resB.results[c]["x_out"]
        for pp, j in enumerate(blocks):
            out[b, j * tpp:(j + 1) * tpp] = xo[pp * tpp:(pp + 1) * tpp]
    return out




def build_m1(tok=2048):
    nc = bass.Bass("TRN2", target_bir_lowering=False)
    ntt = tok // 128
    x_in = nc.dram_tensor("x_in", [tok, 1024], F32, kind="ExternalInput").ap()
    xln_o = nc.dram_tensor("xln_o", [tok, 1024], BF16, kind="ExternalOutput").ap()
    xr_ = x_in.rearrange("(n p) d -> p n d", p=128)
    xlo = xln_o.rearrange("(n p) d -> p n d", p=128)
    with ExitStack() as stack:
        sb = lambda n, sh, dt: stack.enter_context(nc.sbuf_tensor(n, sh, dt))
        xt = sb("xt", [128, 4 * 1024], F32)
        xl = sb("xl", [128, 4 * 1024], BF16)
        stat = sb("stat", [128, 64], F32)
        sq = sb("sq", [128, 1024], F32)
        epsc = sb("epsc", [128, 1], F32)
        p = Prog(nc)
        p.g.memset(epsc[:, :], LNEPS)
        for pp in range(tok // 512):
            p.y.dma_start(out=xt.rearrange("p (n d) -> p n d", d=1024),
                          in_=xr_[:, pp * 4:(pp + 1) * 4, :])
            xts = [xt[:, k * 1024:(k + 1) * 1024] for k in range(4)]
            xls = [xl[:, k * 1024:(k + 1) * 1024] for k in range(4)]
            emit_ln(p, xts, xls, stat, sq, epsc)
            for k in range(4):
                p.g.dma_start(out=xlo[:, pp * 4 + k, :], in_=xl[:, k * 1024:(k + 1) * 1024])
        with nc.Block() as block:
            p.emit(block, stack)
    return nc


def build_m2(T=16384, tc=512):
    nc = bass.Bass("TRN2", target_bir_lowering=False)
    nch = T // tc
    BCH = 8192 // tc
    win = nc.dram_tensor("win", [2, 128, 4096], F32, kind="ExternalInput").ap()
    xlnT = nc.dram_tensor("xlnT", [1024, T], BF16, kind="ExternalInput").ap()
    xlnTr = nc.dram_tensor("xlnTr", [1024, T], BF16, kind="ExternalInput").ap()
    convw = nc.dram_tensor("convw", [128, 16], F32, kind="ExternalInput").ap()
    convb = nc.dram_tensor("convb", [128, 4], F32, kind="ExternalInput").ap()
    wx = nc.dram_tensor("wx", [2, 128, 192], F32, kind="ExternalInput").ap()
    u_o = nc.dram_tensor("u_o", [2, 256, T], BF16, kind="ExternalOutput").ap()
    sz_o = nc.dram_tensor("sz_o", [2, 256, T], BF16, kind="ExternalOutput").ap()
    px_o = nc.dram_tensor("px_o", [2, 96, T], F32, kind="ExternalOutput").ap()

    with ExitStack() as stack:
        sb = lambda n, sh, dt: stack.enter_context(nc.sbuf_tensor(n, sh, dt))
        psal = lambda n, sh, dt: stack.enter_context(nc.psum_tensor(n, sh, dt))
        wc = [sb(f"wc{s}", [128, 4096], BF16) for s in range(2)]
        wxc = [sb(f"wxc{s}", [128, 192], BF16) for s in range(2)]
        cwc = sb("cwc", [128, 16], F32)
        cbc = sb("cbc", [128, 4], F32)
        xb = sb("xb", [128, 8 * tc], BF16)
        xs = sb("xs", [128, 2 * 2 * (tc + 3)], F32)
        cacc = sb("cacc", [128, 2 * tc], F32)
        ub = sb("ub", [128, 2 * tc], BF16)
        zb = sb("zb", [128, 2 * tc], BF16)
        pxs = sb("pxs", [128, tc], F32)
        ps = [psal(f"ps{i}", [128, 512], F32) for i in range(4)]
        psx = psal("psx", [128, 512], F32)

        p = Prog(nc)
        for s in range(2):
            p.g.dma_start(out=wc[s][:, :], in_=win[s])
            p.g.dma_start(out=wxc[s][:, :], in_=wx[s])
        p.y.dma_start(out=cwc[:, :], in_=convw[:, :])
        p.y.dma_start(out=cbc[:, :], in_=convb[:, :])

        for s in range(2):
            xsr = (xlnT if s == 0 else xlnTr).rearrange("(kt p) t -> p kt t", p=128)
            for c in range(nch):
                p.y.dma_start(out=xb.rearrange("p (kt t) -> p kt t", kt=8),
                              in_=xsr[:, :, c * tc:(c + 1) * tc])
                for og in range(4):
                    for kt in range(8):
                        p.t.matmul(ps[og][:, :], wc[s][:, (og * 8 + kt) * 128:(og * 8 + kt + 1) * 128],
                                   xb[:, kt * tc:(kt + 1) * tc],
                                   start=(kt == 0), stop=(kt == 7))
                for g in range(2):
                    hbuf = xs[:, (s * 2 + g) * (tc + 3):(s * 2 + g) * (tc + 3) + tc + 3]
                    if c % BCH == 0:
                        p.v.memset(hbuf[:, 0:3], 0.0)
                    else:
                        p.v.tensor_copy(hbuf[:, 0:3], hbuf[:, tc:tc + 3])
                    p.v.tensor_copy(hbuf[:, 3:3 + tc], ps[g][:, :])
                    ca = cacc[:, g * tc:(g + 1) * tc]
                    p.v.tensor_scalar(ca, hbuf[:, 0:tc], cwc[:, (s * 2 + g) * 4:(s * 2 + g) * 4 + 1],
                                      None, OP.mult)
                    for k in range(1, 4):
                        p.v.scalar_tensor_tensor(ca, hbuf[:, k:k + tc],
                                                 cwc[:, (s * 2 + g) * 4 + k:(s * 2 + g) * 4 + k + 1],
                                                 ca, OP.mult, OP.add)
                    p.s.activation(ub[:, g * tc:(g + 1) * tc], ca, AF.Silu,
                                   bias=cbc[:, s * 2 + g:s * 2 + g + 1])
                    p.s.activation(zb[:, g * tc:(g + 1) * tc], ps[2 + g][:, :], AF.Silu)
                p.g.dma_start(out=u_o[s].rearrange("(g p) t -> p g t", p=128)[:, :, c * tc:(c + 1) * tc],
                              in_=ub.rearrange("p (g t) -> p g t", g=2))
                p.g.dma_start(out=sz_o[s].rearrange("(g p) t -> p g t", p=128)[:, :, c * tc:(c + 1) * tc],
                              in_=zb.rearrange("p (g t) -> p g t", g=2))
                for g in range(2):
                    p.t.matmul(psx[0:96, 0:tc], wxc[s][:, g * 96:(g + 1) * 96],
                               ub[:, g * tc:(g + 1) * tc], start=(g == 0), stop=(g == 1))
                p.v.tensor_copy(pxs[0:96, :], psx[0:96, 0:tc])
                p.g.dma_start(out=px_o[s][:, c * tc:(c + 1) * tc], in_=pxs[0:96, :])

        with nc.Block() as block:
            p.emit(block, stack)
    return nc


def build_m3(T=16384, tc=1024):
    nc = bass.Bass("TRN2", target_bir_lowering=False)
    nch = T // tc
    BCH = 8192 // tc
    NH = tc // 512
    u_i = nc.dram_tensor("u_i", [2, 256, T], BF16, kind="ExternalInput").ap()
    sz_i = nc.dram_tensor("sz_i", [2, 256, T], BF16, kind="ExternalInput").ap()
    px_all = nc.dram_tensor("px_all", [8, 2, 96, T], F32, kind="ExternalInput").ap()
    wdt = nc.dram_tensor("wdt", [2, 64, 256], F32, kind="ExternalInput").ap()
    bdt = nc.dram_tensor("bdt", [128, 4], F32, kind="ExternalInput").ap()
    alog = nc.dram_tensor("alog", [128, 64], F32, kind="ExternalInput").ap()
    dvec = nc.dram_tensor("dvec", [128, 4], F32, kind="ExternalInput").ap()
    y_o = nc.dram_tensor("y_o", [2, 256, T], BF16, kind="ExternalOutput").ap()
    bcscr = [nc.dram_tensor(f"bcscr{i}", [32, tc], BF16) for i in range(2)]

    with ExitStack() as stack:
        sb = lambda n, sh, dt: stack.enter_context(nc.sbuf_tensor(n, sh, dt))
        psal = lambda n, sh, dt: stack.enter_context(nc.psum_tensor(n, sh, dt))
        wdtc = [sb(f"wdtc{s}", [64, 256], F32) for s in range(2)]
        bdtc = sb("bdtc", [128, 4], F32)
        acol = sb("acol", [128, 64], F32)
        dcol = sb("dcol", [128, 4], F32)
        pxsum = sb("pxsum", [128, tc], F32)
        pxtmp = sb("pxtmp", [128, 4 * tc], F32)
        bc32 = sb("bc32", [32, tc], BF16)
        dtb = sb("dtb", [128, 2 * tc], F32)
        dtt = sb("dtt", [128, tc], F32)
        Blo = sb("Blo", [128, 8 * tc], BF16)
        Bhi = sb("Bhi", [128, 8 * tc], BF16)
        Clo = sb("Clo", [128, 8 * tc], BF16)
        Chi = sb("Chi", [128, 8 * tc], BF16)
        ubb = sb("ubb", [128, 2 * tc], BF16)
        szb = sb("szb", [128, 2 * tc], BF16)
        dub = sb("dub", [128, 2 * tc], F32)
        dab = sb("dab", [128, 2 * tc], F32)
        bbb = sb("bbb", [128, 2 * tc], F32)
        hbb = sb("hbb", [128, 3 * tc], F32)
        hcb = sb("hcb", [128, 4 * tc], F32)
        yac = sb("yac", [128, 2 * tc], F32)
        yacg = sb("yacg", [128, 2 * tc], F32)
        ybf = sb("ybf", [128, 2 * tc], BF16)
        carry = sb("carry", [128, 64], F32)
        psA = psal("psA", [128, 512], F32)

        p = Prog(nc)
        for s in range(2):
            p.y.dma_start(out=wdtc[s][:, :], in_=wdt[s])
        p.y.dma_start(out=bdtc[:, :], in_=bdt[:, :])
        p.y.dma_start(out=dcol[:, :], in_=dvec[:, :])
        p.y.dma_start(out=acol[:, :], in_=alog[:, :])
        p.s.activation(acol[:, :], acol[:, :], AF.Exp)
        p.v.tensor_scalar(acol[:, :], acol[:, :], -1.0, None, OP.mult)

        def Bn(n):
            return (Blo if n < 8 else Bhi)[:, (n % 8) * tc:(n % 8 + 1) * tc]

        def Cn(n):
            return (Clo if n < 8 else Chi)[:, (n % 8) * tc:(n % 8 + 1) * tc]

        def prep(s, c):
            scr = bcscr[c % 2].ap()
            p.y.dma_start(out=pxsum[0:96, :], in_=px_all[0][s][:, c * tc:(c + 1) * tc])
            for r in range(1, 5):
                j = (r - 1) % 4
                p.y.dma_start(out=pxtmp[0:96, j * tc:(j + 1) * tc],
                              in_=px_all[r][s][:, c * tc:(c + 1) * tc])
            for r in range(1, 8):
                j = (r - 1) % 4
                p.v.tensor_tensor(pxsum[0:96, :], pxsum[0:96, :],
                                  pxtmp[0:96, j * tc:(j + 1) * tc], OP.add)
                if r + 4 < 8:
                    j2 = (r + 3) % 4
                    p.y.dma_start(out=pxtmp[0:96, j2 * tc:(j2 + 1) * tc],
                                  in_=px_all[r + 4][s][:, c * tc:(c + 1) * tc])
            p.v.tensor_copy(bc32[:, :], pxsum[64:96, :])
            p.g.dma_start(out=scr[:, :], in_=bc32[:, :])
            flat = scr.rearrange("n t -> (n t)")
            p.y.dma_start(out=Blo.rearrange("p (n t) -> p n t", n=8),
                          in_=flat[0:8 * tc].rearrange("(a n t) -> a n t", a=1, n=8).broadcast_to([128, 8, tc]))
            p.y.dma_start(out=Clo.rearrange("p (n t) -> p n t", n=8),
                          in_=flat[16 * tc:24 * tc].rearrange("(a n t) -> a n t", a=1, n=8).broadcast_to([128, 8, tc]))
            p.y.dma_start(out=Bhi.rearrange("p (n t) -> p n t", n=8),
                          in_=flat[8 * tc:16 * tc].rearrange("(a n t) -> a n t", a=1, n=8).broadcast_to([128, 8, tc]))
            p.y.dma_start(out=Chi.rearrange("p (n t) -> p n t", n=8),
                          in_=flat[24 * tc:32 * tc].rearrange("(a n t) -> a n t", a=1, n=8).broadcast_to([128, 8, tc]))
            for g in range(2):
                dgt = dtb[:, g * tc:(g + 1) * tc]
                bcol = bdtc[:, s * 2 + g:s * 2 + g + 1]
                for hh in range(NH):
                    cols = slice(hh * 512, (hh + 1) * 512)
                    p.t.matmul(psA[:, :], wdtc[s][:, g * 128:(g + 1) * 128],
                               pxsum[0:64, cols], start=True, stop=True)
                    p.v.tensor_scalar(dgt[:, cols], psA[:, :], bcol, 0.0, OP.add, OP.max)
                    p.s.activation(dtt[:, cols], psA[:, :], AF.Abs, bias=bcol)
                p.s.activation(dtt[:, :], dtt[:, :], AF.Exp, scale=-1.0)
                p.s.activation(dtt[:, :], dtt[:, :], AF.Ln, bias=1.0)
                p.v.tensor_tensor(dgt, dgt, dtt[:, :], OP.add)
            p.y.dma_start(out=ubb.rearrange("p (g t) -> p g t", g=2),
                          in_=u_i[s].rearrange("(g p) t -> p g t", p=128)[:, :, c * tc:(c + 1) * tc])
            p.y.dma_start(out=szb.rearrange("p (g t) -> p g t", g=2),
                          in_=sz_i[s].rearrange("(g p) t -> p g t", p=128)[:, :, c * tc:(c + 1) * tc])

        for s in range(2):
            for c in range(nch):
                prep(s, c)
                for g in range(2):
                    dgt = dtb[:, g * tc:(g + 1) * tc]
                    du = dub[:, g * tc:(g + 1) * tc]
                    p.v.tensor_tensor(du, dgt, ubb[:, g * tc:(g + 1) * tc], OP.mult)
                    ya = yac[:, g * tc:(g + 1) * tc]
                    yag = yacg[:, g * tc:(g + 1) * tc]
                    pool_ns = {2, 5, 8, 11, 14}

                    def emit_dA(n):
                        ac = acol[:, (s * 2 + g) * 16 + n:(s * 2 + g) * 16 + n + 1]
                        p.s.activation(dab[:, (n % 2) * tc:(n % 2 + 1) * tc], dgt,
                                       AF.Exp, scale=ac)

                    def emit_b(n):
                        p.g.tensor_tensor(bbb[:, (n % 2) * tc:(n % 2 + 1) * tc], du,
                                          Bn(n), OP.mult)

                    def emit_scan(n):
                        cidx = (s * 2 + g) * 16 + n
                        init = 0.0 if c % BCH == 0 else carry[:, cidx:cidx + 1]
                        p.v.tensor_tensor_scan(hbb[:, (n % 3) * tc:(n % 3 + 1) * tc],
                                               dab[:, (n % 2) * tc:(n % 2 + 1) * tc],
                                               bbb[:, (n % 2) * tc:(n % 2 + 1) * tc],
                                               init, OP.mult, OP.add)
                        p.v.tensor_copy(carry[:, cidx:cidx + 1],
                                        hbb[:, (n % 3) * tc + tc - 1:(n % 3) * tc + tc])

                    def emit_hc(n):
                        eng = p.g if n in pool_ns else p.v
                        eng.tensor_tensor(hcb[:, (n % 4) * tc:(n % 4 + 1) * tc],
                                          hbb[:, (n % 3) * tc:(n % 3 + 1) * tc],
                                          Cn(n), OP.mult)

                    def emit_acc(n):
                        if n in pool_ns:
                            if n == 2:
                                p.g.tensor_copy(yag, hcb[:, (n % 4) * tc:(n % 4 + 1) * tc])
                            else:
                                p.g.tensor_tensor(yag, yag, hcb[:, (n % 4) * tc:(n % 4 + 1) * tc], OP.add)
                        elif n == 0:
                            p.v.tensor_copy(ya, hcb[:, 0:tc])
                        else:
                            p.v.tensor_tensor(ya, ya, hcb[:, (n % 4) * tc:(n % 4 + 1) * tc], OP.add)

                    emit_dA(0); emit_b(0); emit_scan(0)
                    for n in range(1, 16):
                        emit_dA(n)
                        emit_b(n)
                        emit_hc(n - 1)
                        emit_scan(n)
                        emit_acc(n - 1)
                    emit_hc(15)
                    emit_acc(15)
                    p.v.tensor_tensor(ya, ya, yag, OP.add)
                    p.v.scalar_tensor_tensor(ya, ubb[:, g * tc:(g + 1) * tc],
                                             dcol[:, s * 2 + g:s * 2 + g + 1], ya,
                                             OP.mult, OP.add)
                    p.v.tensor_tensor(ybf[:, g * tc:(g + 1) * tc], ya,
                                      szb[:, g * tc:(g + 1) * tc], OP.mult)
                p.y.dma_start(out=y_o[s].rearrange("(g p) t -> p g t", p=128)[:, :, c * tc:(c + 1) * tc],
                              in_=ybf.rearrange("p (g t) -> p g t", g=2))

        with nc.Block() as block:
            p.emit(block, stack)
    return nc


def build_m4(tok=2048, tpp=512):
    nc = bass.Bass("TRN2", target_bir_lowering=False)
    npass, ntt = tok // tpp, tpp // 128
    assert tpp == 512
    x_in = nc.dram_tensor("x_in", [tok, 1024], F32, kind="ExternalInput").ap()
    yfw = nc.dram_tensor("yfw", [2048, tok], BF16, kind="ExternalInput").ap()
    ybw = nc.dram_tensor("ybw", [2048, tok], BF16, kind="ExternalInput").ap()
    wof = nc.dram_tensor("wof", [128, 16384], F32, kind="ExternalInput").ap()
    wob = nc.dram_tensor("wob", [128, 16384], F32, kind="ExternalInput").ap()
    wfu = nc.dram_tensor("wfu", [128, 16384], F32, kind="ExternalInput").ap()
    fbr = nc.dram_tensor("fbr", [128, 8], F32, kind="ExternalInput").ap()
    x_out = nc.dram_tensor("x_out", [tok, 1024], F32, kind="ExternalOutput").ap()

    xr_ = x_in.rearrange("(n p) d -> p n d", p=128)
    xro = x_out.rearrange("(n p) d -> p n d", p=128)

    with ExitStack() as stack:
        sb = lambda n, sh, dt: stack.enter_context(nc.sbuf_tensor(n, sh, dt))
        B = alloc_common(nc, stack, ntt, tpp, ln_bufs=False)
        wofc = sb("wofc", [128, 16384], BF16)
        wobc = sb("wobc", [128, 16384], BF16)
        wfuc = sb("wfuc", [128, 16384], BF16)
        fbc = sb("fbc", [128, 8], F32)
        yfb = sb("yfb", [128, 16 * tpp], BF16)
        ybb = sb("ybb", [128, 16 * tpp], BF16)
        foT = sb("foT", [128, 8 * tpp], BF16)
        boT = sb("boT", [128, 8 * tpp], BF16)
        outT = sb("outT", [128, 8 * tpp], F32)

        p = Prog(nc)
        emit_identities(p, B.idb, B.idf, B.epsc)
        p.g.dma_start(out=wofc[:, :], in_=wof[:, :])
        p.g.dma_start(out=wobc[:, :], in_=wob[:, :])
        p.g.dma_start(out=wfuc[:, :], in_=wfu[:, :])
        p.y.dma_start(out=fbc[:, :], in_=fbr[:, :])

        for pp in range(npass):
            p.y.dma_start(out=B.xt.rearrange("p (n d) -> p n d", d=1024),
                          in_=xr_[:, pp * ntt:(pp + 1) * ntt, :])
            p.y.dma_start(out=yfb.rearrange("p (kt t) -> p kt t", kt=16),
                          in_=yfw.rearrange("(kt p) t -> p kt t", p=128)[:, :, pp * tpp:(pp + 1) * tpp])
            p.y.dma_start(out=ybb.rearrange("p (kt t) -> p kt t", kt=16),
                          in_=ybw.rearrange("(kt p) t -> p kt t", p=128)[:, :, pp * tpp:(pp + 1) * tpp])
            for wsrc, ysrc, dst in ((wofc, yfb, foT), (wobc, ybb, boT)):
                for og in range(8):
                    ps = B.psmm[og % 2]
                    for kt in range(16):
                        p.t.matmul(ps[:, :], wsrc[:, (og * 16 + kt) * 128:(og * 16 + kt + 1) * 128],
                                   ysrc[:, kt * tpp:(kt + 1) * tpp],
                                   start=(kt == 0), stop=(kt == 15))
                    p.v.tensor_copy(dst[:, og * tpp:(og + 1) * tpp], ps[:, :])
            for og in range(8):
                ps = B.psmm[2 + og % 2]
                for kt in range(16):
                    rhs = foT[:, kt * tpp:(kt + 1) * tpp] if kt < 8 else boT[:, (kt - 8) * tpp:(kt - 7) * tpp]
                    p.t.matmul(ps[:, :], wfuc[:, (og * 16 + kt) * 128:(og * 16 + kt + 1) * 128],
                               rhs, start=(kt == 0), stop=(kt == 15))
                p.s.activation(outT[:, og * tpp:(og + 1) * tpp], ps[:, :],
                               AF.Identity, bias=fbc[:, og:og + 1])
            emit_tail(p, B, outT, xro, pp)

        with nc.Block() as block:
            p.emit(block, stack)
    return nc


def run_l0_attn(x_full, pm, T=16384):
    xf = np.ascontiguousarray(x_full.reshape(T, 1024).astype(np.float32))
    tok = T // NCORES
    nc1 = build_m1(tok=tok)
    res1 = run_bass_kernel_spmd(nc1, [{"x_in": xf[c * tok:(c + 1) * tok]} for c in range(NCORES)],
                                core_ids=list(range(NCORES)))
    xln = np.concatenate([res1.results[c]["xln_o"] for c in range(NCORES)], axis=0)
    xlnT = np.ascontiguousarray(xln.T)
    xlnr = xln.reshape(2, T // 2, 1024)[:, ::-1, :].reshape(T, 1024)
    xlnTr = np.ascontiguousarray(xlnr.T)
    nc2 = build_m2(T=T)
    in_maps = []
    for c in range(NCORES):
        sl = slice(c * 256, (c + 1) * 256)
        m = {"xlnT": xlnT, "xlnTr": xlnTr}
        wrows = {}
        for s, pref in ((0, "fw"), (1, "bw")):
            Win = pm[pref + "_Win"]
            rows = np.concatenate([Win[sl], Win[2048 + (c * 256):(2048 + (c + 1) * 256)]], axis=0)
            wrows[s] = _lhsT_resident(rows)
        m["win"] = np.stack([wrows[0], wrows[1]], axis=0)
        cw = np.zeros((128, 16), np.float32)
        cb = np.zeros((128, 4), np.float32)
        wxs = []
        for s, pref in ((0, "fw"), (1, "bw")):
            cwf = pm[pref + "_convw"][sl, 0, :]
            cw[:, s * 8:s * 8 + 4] = cwf[:128]
            cw[:, s * 8 + 4:s * 8 + 8] = cwf[128:]
            cbf = pm[pref + "_convb"][sl]
            cb[:, s * 2] = cbf[:128]
            cb[:, s * 2 + 1] = cbf[128:]
            Wx = pm[pref + "_Wx"]
            wxr = Wx[:, sl]
            a = np.zeros((128, 192), np.float32)
            a[:, 0:96] = wxr.T[:128]
            a[:, 96:192] = wxr.T[128:]
            wxs.append(a)
        m["convw"] = np.ascontiguousarray(cw[:, [0,1,2,3,8,9,10,11,4,5,6,7,12,13,14,15]]) if False else cw
        m["convb"] = cb
        m["wx"] = np.stack(wxs, axis=0)
        in_maps.append(m)
    res2 = run_bass_kernel_spmd(nc2, in_maps, core_ids=list(range(NCORES)))
    px_all = np.stack([res2.results[c]["px_o"] for c in range(NCORES)], axis=0)
    nc3 = build_m3(T=T)
    in_maps = []
    for c in range(NCORES):
        sl = slice(c * 256, (c + 1) * 256)
        m = {"px_all": px_all,
             "u_i": res2.results[c]["u_o"], "sz_i": res2.results[c]["sz_o"]}
        wdts, bdt_, alog_, dv_ = [], np.zeros((128, 4), np.float32), np.zeros((128, 64), np.float32), np.zeros((128, 4), np.float32)
        for s, pref in ((0, "fw"), (1, "bw")):
            Wdt = pm[pref + "_Wdt"][sl]
            wdts.append(np.ascontiguousarray(Wdt.T))
            b = pm[pref + "_bdt"][sl]
            bdt_[:, s * 2] = b[:128]
            bdt_[:, s * 2 + 1] = b[128:]
            Al = pm[pref + "_Alog"][sl]
            alog_[:, (s * 2) * 16:(s * 2 + 1) * 16] = Al[:128]
            alog_[:, (s * 2 + 1) * 16:(s * 2 + 2) * 16] = Al[128:]
            D = pm[pref + "_D"][sl]
            dv_[:, s * 2] = D[:128]
            dv_[:, s * 2 + 1] = D[128:]
        m["wdt"] = np.stack(wdts, axis=0)
        m["bdt"] = bdt_
        m["alog"] = alog_
        m["dvec"] = dv_
        in_maps.append(m)
    res3 = run_bass_kernel_spmd(nc3, in_maps, core_ids=list(range(NCORES)))
    yfw = np.concatenate([res3.results[c]["y_o"][0] for c in range(NCORES)], axis=0)
    ybw_r = np.concatenate([res3.results[c]["y_o"][1] for c in range(NCORES)], axis=0)
    ybw = ybw_r.reshape(2048, 2, T // 2)[:, :, ::-1].reshape(2048, T)
    nc4 = build_m4(tok=tok)
    base = {
        "wof": _lhsT_resident(pm["fw_Wout"]), "wob": _lhsT_resident(pm["bw_Wout"]),
        "wfu": _lhsT_resident(pm["fuse_W"]),
        "fbr": np.ascontiguousarray(pm["fuse_b"].reshape(8, 128).T),
    }
    in_maps = []
    for c in range(NCORES):
        m = dict(base)
        m["x_in"] = xf[c * tok:(c + 1) * tok]
        m["yfw"] = np.ascontiguousarray(yfw[:, c * tok:(c + 1) * tok])
        m["ybw"] = np.ascontiguousarray(ybw[:, c * tok:(c + 1) * tok])
        in_maps.append(m)
    res4 = run_bass_kernel_spmd(nc4, in_maps, core_ids=list(range(NCORES)))
    out = np.concatenate([res4.results[c]["x_out"] for c in range(NCORES)], axis=0)
    return out.reshape(2, 8192, 1024)


def _grp(inputs, pref):
    return {k[len(pref):]: np.asarray(v) for k, v in inputs.items() if k.startswith(pref)}


def kernel(**inputs):
    x = np.asarray(inputs["L0_attn_x"]).astype(np.float32)
    x = run_l0_attn(x, _grp(inputs, "L0_attn_"))
    x = run_ffn0(x, _grp(inputs, "L0_ffn_"))
    x = run_l1_attn(x, _grp(inputs, "L1_attn_"))
    x = run_rwkv(x, _grp(inputs, "L1_ffn_"))
    x = run_l2_attn(x, _grp(inputs, "L2_attn_"))
    x = run_moe(x, _grp(inputs, "L2_ffn_"))
    x = run_l3_attn(x, _grp(inputs, "L3_attn_"))
    x = run_rwkv(x, _grp(inputs, "L3_ffn_"))
    return np.ascontiguousarray(x.astype(np.float32))


# version 05 (final)
# speedup vs baseline: 1.2463x; 1.0066x over previous
import numpy as np
import ml_dtypes
from contextlib import ExitStack
import concourse.bass as bass
import concourse.mybir as mybir
from concourse.bass_utils import run_bass_kernel_spmd

F32 = mybir.dt.float32
BF16 = mybir.dt.bfloat16
AF = mybir.ActivationFunctionType
OP = mybir.AluOpType
AX = mybir.AxisListType

NCORES = 8
LNEPS = 1e-5

_WRITE_KW = {"out", "accum_out", "out_max", "out_indices", "dest", "ap"}

ENGINES = ("t", "v", "s", "g", "y")
DMA_QUEUES = {"g": "qg", "y": "qy"}


def _ap_span(ap):
    try:
        name = ap.tensor.name
    except AttributeError:
        return None
    off = int(ap.offset)
    reach = 1
    for step, cnt in ap.ap:
        reach += abs(int(step)) * (int(cnt) - 1)
    is_psum = "PSum" in type(ap.tensor).__name__
    return (name, off, off + reach, is_psum)


class Op:
    __slots__ = ("eng", "fn", "reads", "writes", "idx", "need_inc", "is_dma")

    def __init__(self, eng, fn, reads, writes, idx, is_dma):
        self.eng, self.fn, self.reads, self.writes = eng, fn, reads, writes
        self.idx, self.is_dma = idx, is_dma
        self.need_inc = False


class _EngProxy:
    def __init__(self, prog, eng):
        self._p = prog
        self._e = eng

    def __getattr__(self, fname):
        def call(*args, _reads=(), _writes=(), **kw):
            return self._p.record(self._e, fname, args, kw, _reads, _writes)
        return call


class Prog:

    def __init__(self, nc):
        self.nc = nc
        self.ops = {e: [] for e in ENGINES}
        self.order = []
        self.segs = {}
        self.deps = {}
        for e in ENGINES:
            setattr(self, e, _EngProxy(self, e))

    def record(self, eng, fname, args, kw, extra_reads, extra_writes):
        reads, writes = [], []
        if fname.startswith("dma_start"):
            writes.append(kw["out"] if "out" in kw else args[0])
            reads.append(kw["in_"] if "in_" in kw else args[1])
        elif fname in ("matmul", "transpose"):
            writes.append(args[0])
            reads.extend(a for a in args[1:] if hasattr(a, "tensor"))
        elif fname in ("memset", "iota"):
            writes.append(args[0])
        elif fname == "affine_select":
            writes.append(kw.get("out", args[0] if args else None))
            if "in_" in kw:
                reads.append(kw["in_"])
        else:
            for i, a in enumerate(args):
                if hasattr(a, "tensor"):
                    (writes if i == 0 else reads).append(a)
            for k, v in kw.items():
                if hasattr(v, "tensor"):
                    (writes if k in _WRITE_KW else reads).append(v)
        reads.extend(extra_reads)
        writes.extend(extra_writes)
        rs = [x for x in (_ap_span(a) for a in reads if a is not None) if x]
        ws = [x for x in (_ap_span(a) for a in writes if a is not None) if x]
        is_dma = fname.startswith("dma_start")
        op = Op(eng, (fname, args, kw), rs, ws, len(self.ops[eng]), is_dma)
        self.ops[eng].append(op)
        self.order.append(op)
        self._resolve(op)
        return op

    def _resolve(self, op):
        deps = set()

        def add(producer):
            if producer is None or producer is op:
                return
            if (producer.eng == op.eng == "t"
                    and not producer.is_dma and not op.is_dma):
                return
            deps.add(producer)

        for (name, a, b, is_psum) in op.reads:
            for seg in self.segs.get(name, []):
                if seg[0] < b and a < seg[1]:
                    add(seg[2])
                    if is_psum:
                        for r in seg[3]:
                            if r.eng != op.eng:
                                add(r)
                    seg[3].append(op)
        for (name, a, b, is_psum) in op.writes:
            lst = self.segs.setdefault(name, [])
            keep = []
            for seg in lst:
                if seg[0] < b and a < seg[1]:
                    add(seg[2])
                    for r in seg[3]:
                        add(r)
                    if seg[0] < a:
                        keep.append([seg[0], a, seg[2], list(seg[3])])
                    if seg[1] > b:
                        keep.append([b, seg[1], seg[2], list(seg[3])])
                else:
                    keep.append(seg)
            keep.append([a, b, op, []])
            self.segs[name] = keep
        self.deps[op] = deps

    def emit(self, block, stack, dma_pool=16):
        nc = self.nc
        sems = {}
        for e in ENGINES:
            sems[e] = stack.enter_context(nc.semaphore(f"ss_{e}"))
        qsems = {}
        for q in DMA_QUEUES.values():
            for j in range(dma_pool):
                qsems[f"{q}{j}"] = stack.enter_context(nc.semaphore(f"ss_{q}{j}"))
        sems.update(qsems)

        for deps in self.deps.values():
            for d in deps:
                d.need_inc = True

        semval = {}
        counters = {k: 0 for k in ENGINES}
        qcount = {q: 0 for q in DMA_QUEUES.values()}
        dma_sem_of = {}
        for op in self.order:
            if op.is_dma:
                q = DMA_QUEUES[op.eng]
                i = qcount[q]
                qcount[q] += 1
                dom = f"{q}{i % dma_pool}"
                dma_sem_of[op] = dom
                semval[op] = (dom, 16 * (i // dma_pool + 1))
            elif op.need_inc:
                counters[op.eng] += 1
                semval[op] = (op.eng, counters[op.eng])

        clocks = {e: {} for e in ENGINES}
        snapshots = {}
        emitted = {e: [] for e in ENGINES}
        for op in self.order:
            ck = clocks[op.eng]
            waits = []
            deps = sorted(self.deps[op], key=lambda o: (o.eng, o.idx))
            for d in deps:
                dom, val = semval[d]
                if ck.get(dom, 0) >= val:
                    continue
                waits.append((dom, val))
                ck[dom] = val
                sn = snapshots.get((dom, val))
                if sn:
                    for d2, v2 in sn.items():
                        if ck.get(d2, 0) < v2:
                            ck[d2] = v2
            mv = semval.get(op)
            if op.is_dma and mv is not None and mv[1] > 16:
                dom, val = mv
                if ck.get(dom, 0) < val - 16:
                    waits.append((dom, val - 16))
                    ck[dom] = val - 16
                    sn = snapshots.get((dom, val - 16))
                    if sn:
                        for d2, v2 in sn.items():
                            if ck.get(d2, 0) < v2:
                                ck[d2] = v2
            if mv is not None:
                snapshots[mv] = dict(ck)
            emitted[op.eng].append((op, waits, mv))

        eng_map = {"t": block.tensor, "v": block.vector, "s": block.scalar,
                   "g": block.gpsimd, "y": block.sync}
        for e in ENGINES:
            if not emitted[e]:
                continue

            @eng_map[e]
            def _(eh, e=e):
                for op, waits, mv in emitted[e]:
                    for dom, val in waits:
                        eh.wait_ge(sems[dom], val)
                    fname, args, kw = op.fn
                    instr = getattr(eh, fname)(*args, **kw)
                    if op.is_dma:
                        instr.then_inc(sems[dma_sem_of[op]], 16)
                    elif mv is not None:
                        instr.then_inc(sems[e], 1)
                if e in DMA_QUEUES:
                    q = DMA_QUEUES[e]
                    n = qcount[q]
                    for j in range(min(n, dma_pool)):
                        cnt = len([i for i in range(n) if i % dma_pool == j])
                        eh.wait_ge(sems[f"{q}{j}"], 16 * cnt)
        return sems


def emit_ln(p, x_tiles, out_tiles, stat, sq, eps_ap):
    for k, xk in enumerate(x_tiles):
        d = xk.shape[-1]
        P = xk.shape[0]
        st = stat[0:P, 16 * k:16 * k + 16]
        p.v.tensor_reduce(st[:, 4:5], xk, AX.X, OP.add)
        p.v.scalar_tensor_tensor(sq[0:P, 0:d], xk, 1.0, xk, OP.mult, OP.mult,
                                 accum_out=st[:, 5:6])
        p.v.tensor_scalar(st[:, 0:1], st[:, 4:5], 1.0 / d, None, OP.mult)
        p.v.tensor_scalar(st[:, 6:7], st[:, 5:6], 1.0 / d, None, OP.mult)
        p.v.tensor_scalar(st[:, 7:8], st[:, 0:1], st[:, 0:1], None, OP.mult)
        p.v.tensor_tensor(st[:, 1:2], st[:, 6:7], st[:, 7:8], OP.subtract)
        p.s.activation(st[:, 2:3], st[:, 1:2], AF.Ln, bias=eps_ap[0:P, 0:1])
        p.s.activation(st[:, 3:4], st[:, 2:3], AF.Exp, scale=-0.5)
        p.v.tensor_scalar(out_tiles[k], xk, st[:, 0:1], st[:, 3:4],
                          OP.subtract, OP.mult)


def emit_transpose(p, src_tiles, dst_fn, pst, idn, fgroups=8):
    for k, src in enumerate(src_tiles):
        for fg in range(fgroups):
            p.t.transpose(pst[:, fg * 128:(fg + 1) * 128],
                          src[:, fg * 128:(fg + 1) * 128], idn[:, :])
        p.v.tensor_copy(dst_fn(k), pst.rearrange("p (f t) -> p f t", f=fgroups))


def emit_identities(p, idb, idf, epsc):
    p.g.memset(epsc[:, :], LNEPS)
    for idn in (idb, idf):
        p.g.memset(idn[:, :], 0.0)
        p.g.affine_select(out=idn[:, :], in_=idn[:, :],
                          compare_op=OP.not_equal, fill=1.0, base=0,
                          pattern=[[-1, 128]], channel_multiplier=1)




class NS:
    pass


def alloc_common(nc, stack, ntt, tpp, ln_bufs=True, psmm_n=4, psum_tp=True):
    sb = lambda n, sh, dt: stack.enter_context(nc.sbuf_tensor(n, sh, dt))
    psal = lambda n, sh, dt: stack.enter_context(nc.psum_tensor(n, sh, dt))
    B = NS()
    B.ntt, B.tpp = ntt, tpp
    B.xt = sb("xt", [128, ntt * 1024], F32)
    if ln_bufs:
        B.xln = sb("xln", [128, ntt * 1024], BF16)
        B.xlnT = sb("xlnT", [128, 8 * tpp], BF16)
    B.stat = sb("stat", [128, 16 * ntt + 16], F32)
    B.sq = sb("sq", [128, 1024], F32)
    B.otm = sb("otm", [128, 2 * 1024], F32)
    B.oln = sb("oln", [128, 2 * 1024], F32)
    B.epsc = sb("epsc", [128, 1], F32)
    B.idb = sb("idb", [128, 128], BF16)
    B.idf = sb("idf", [128, 128], F32)
    if psum_tp:
        B.pst = psal("pst", [128, 1024], BF16)
        B.pstf = psal("pstf", [128, 512], F32)
    B.psmm = [psal(f"psmm{i}", [128, 512], F32) for i in range(psmm_n)]
    return B


def emit_head(p, B, xr, pp):
    ntt, tpp = B.ntt, B.tpp
    p.y.dma_start(out=B.xt.rearrange("p (n d) -> p n d", d=1024),
                  in_=xr[:, pp * ntt:(pp + 1) * ntt, :])
    x_tiles = [B.xt[:, k * 1024:(k + 1) * 1024] for k in range(ntt)]
    ln_tiles = [B.xln[:, k * 1024:(k + 1) * 1024] for k in range(ntt)]
    emit_ln(p, x_tiles, ln_tiles, B.stat, B.sq, B.epsc)
    emit_transpose(p, ln_tiles,
                   lambda k: B.xlnT.rearrange("p (f t) -> p f t", f=8)[:, :, k * 128:(k + 1) * 128],
                   B.pst, B.idb)
    return x_tiles


def emit_tail(p, B, outT, xro, pp):
    ntt, tpp = B.ntt, B.tpp
    for k in range(ntt):
        par = k % 2
        om = B.otm[:, par * 1024:(par + 1) * 1024]
        ol = B.oln[:, par * 1024:(par + 1) * 1024]
        for half in range(2):
            for og4 in range(4):
                og = half * 4 + og4
                p.t.transpose(B.pstf[:, og4 * 128:(og4 + 1) * 128],
                              outT[:, og * tpp + k * 128:og * tpp + (k + 1) * 128],
                              B.idf[:, :])
            p.v.tensor_copy(
                om.rearrange("p (g t) -> p g t", g=8)[:, half * 4:(half + 1) * 4, :],
                B.pstf.rearrange("p (g t) -> p g t", g=4))
        emit_ln(p, [om], [ol], B.stat[:, 16 * ntt:], B.sq, B.epsc)
        p.v.tensor_tensor(ol, ol, B.xt[:, k * 1024:(k + 1) * 1024], OP.add)
        p.g.dma_start(out=xro[:, pp * ntt + k, :], in_=ol)


def emit_square_matmul(p, wT, rhs, psum, out_fn, kt_n=8, og_n=8, tpp=512,
                       evac=None):
    for og in range(og_n):
        ps = psum[og % 2]
        for kt in range(kt_n):
            p.t.matmul(ps[:, :tpp], wT[:, (og * kt_n + kt) * 128:(og * kt_n + kt + 1) * 128],
                       rhs[:, kt * tpp:(kt + 1) * tpp],
                       start=(kt == 0), stop=(kt == kt_n - 1))
        evac(og, ps)


def build_ffn0(tok=2048, tpp=512):
    nc = bass.Bass("TRN2", target_bir_lowering=False)
    DH = 4096
    OG1, KT1, KT2, OG3 = DH // 128, 1024 // 128, DH // 128, 1024 // 128
    npass, ntt = tok // tpp, tpp // 128
    assert tpp == 512

    x_in = nc.dram_tensor("x_in", [tok, 1024], F32, kind="ExternalInput").ap()
    w1 = nc.dram_tensor("w1", [OG1, 1024, 128], BF16, kind="ExternalInput").ap()
    w2 = nc.dram_tensor("w2", [OG1, 1024, 128], BF16, kind="ExternalInput").ap()
    h1 = nc.dram_tensor("h1", [OG1, DH, 128], BF16, kind="ExternalInput").ap()
    h2 = nc.dram_tensor("h2", [OG1, DH, 128], BF16, kind="ExternalInput").ap()
    w3 = nc.dram_tensor("w3", [OG3, DH, 128], BF16, kind="ExternalInput").ap()
    scales = nc.dram_tensor("scales", [128, 4 * OG1 + OG3], F32, kind="ExternalInput").ap()
    x_out = nc.dram_tensor("x_out", [tok, 1024], F32, kind="ExternalOutput").ap()

    xr = x_in.rearrange("(n p) d -> p n d", p=128)
    xro = x_out.rearrange("(n p) d -> p n d", p=128)

    with ExitStack() as stack:
        sb = lambda n, sh, dt: stack.enter_context(nc.sbuf_tensor(n, sh, dt))
        psal = lambda n, sh, dt: stack.enter_context(nc.psum_tensor(n, sh, dt))
        xt = sb("xt", [128, ntt * 1024], F32)
        xln = sb("xln", [128, ntt * 1024], BF16)
        xlnT = sb("xlnT", [128, KT1 * tpp], BF16)
        hA = sb("hA", [128, OG1 * tpp], BF16)
        hB = sb("hB", [128, OG1 * tpp], BF16)
        tmpA = sb("tmpA", [128, 2 * tpp], F32)
        wsl = [sb(f"wsl{i}", [128, DH], BF16) for i in range(4)]
        scl = sb("scl", [128, 4 * OG1 + OG3], F32)
        stat = sb("stat", [128, 16 * ntt + 16], F32)
        sq = sb("sq", [128, 1024], F32)
        outT = sb("outT", [128, OG3 * tpp], F32)
        otm = sb("otm", [128, 2 * 1024], F32)
        oln = sb("oln", [128, 2 * 1024], F32)
        epsc = sb("epsc", [128, 1], F32)
        idb = sb("idb", [128, 128], BF16)
        idf = sb("idf", [128, 128], F32)
        pst = psal("pst", [128, 1024], BF16)
        pstf = psal("pstf", [128, 512], F32)
        psmm = [psal(f"psmm{i}", [128, 512], F32) for i in range(4)]

        p = Prog(nc)
        emit_identities(p, idb, idf, epsc)
        p.y.dma_start(out=scl[:, :], in_=scales[:, :])

        for pp in range(npass):
            p.y.dma_start(out=xt.rearrange("p (n d) -> p n d", d=1024),
                          in_=xr[:, pp * ntt:(pp + 1) * ntt, :])
            x_tiles = [xt[:, k * 1024:(k + 1) * 1024] for k in range(ntt)]
            ln_tiles = [xln[:, k * 1024:(k + 1) * 1024] for k in range(ntt)]
            emit_ln(p, x_tiles, ln_tiles, stat, sq, epsc)
            emit_transpose(p, ln_tiles,
                           lambda k: xlnT.rearrange("p (f t) -> p f t", f=8)[:, :, k * 128:(k + 1) * 128],
                           pst, idb)

            for og in range(OG1):
                pa, pb = psmm[2 * (og % 2)], psmm[2 * (og % 2) + 1]
                wa, wb = wsl[2 * (og % 2)], wsl[2 * (og % 2) + 1]
                p.g.dma_start(out=wa.rearrange("p (kt o) -> p kt o", o=128)[:, 0:8, :],
                              in_=w1[og].rearrange("(kt p) o -> p kt o", p=128))
                p.y.dma_start(out=wb.rearrange("p (kt o) -> p kt o", o=128)[:, 0:8, :],
                              in_=w2[og].rearrange("(kt p) o -> p kt o", p=128))
                for kt in range(KT1):
                    p.t.matmul(pa[:, :], wa[:, kt * 128:(kt + 1) * 128],
                               xlnT[:, kt * tpp:(kt + 1) * tpp],
                               start=(kt == 0), stop=(kt == KT1 - 1))
                for kt in range(KT1):
                    p.t.matmul(pb[:, :], wb[:, kt * 128:(kt + 1) * 128],
                               xlnT[:, kt * tpp:(kt + 1) * tpp],
                               start=(kt == 0), stop=(kt == KT1 - 1))
                tA = tmpA[:, (og % 2) * tpp:(og % 2 + 1) * tpp]
                p.s.activation(tA, pa[:, :], AF.Silu, scale=scl[:, og:og + 1])
                p.v.scalar_tensor_tensor(hA[:, og * tpp:(og + 1) * tpp], pb[:, :],
                                         scl[:, OG1 + og:OG1 + og + 1], tA,
                                         OP.mult, OP.mult)

            for stage, (wsrc, hin, hout, soff) in enumerate((
                    (h1, hA, hB, 2 * OG1), (h2, hB, hA, 3 * OG1))):
                for og in range(OG1):
                    pa, wa = psmm[og % 2], wsl[og % 2]
                    eng = p.g if og % 2 == 0 else p.y
                    eng.dma_start(out=wa.rearrange("p (kt o) -> p kt o", o=128),
                                  in_=wsrc[og].rearrange("(kt p) o -> p kt o", p=128))
                    for kt in range(KT2):
                        p.t.matmul(pa[:, :], wa[:, kt * 128:(kt + 1) * 128],
                                   hin[:, kt * tpp:(kt + 1) * tpp],
                                   start=(kt == 0), stop=(kt == KT2 - 1))
                    p.s.activation(hout[:, og * tpp:(og + 1) * tpp], pa[:, :],
                                   AF.Gelu, scale=scl[:, soff + og:soff + og + 1])

            for og in range(OG3):
                pa, wa = psmm[og % 2], wsl[og % 2]
                eng = p.g if og % 2 == 0 else p.y
                eng.dma_start(out=wa.rearrange("p (kt o) -> p kt o", o=128),
                              in_=w3[og].rearrange("(kt p) o -> p kt o", p=128))
                for kt in range(KT2):
                    p.t.matmul(pa[:, :], wa[:, kt * 128:(kt + 1) * 128],
                               hA[:, kt * tpp:(kt + 1) * tpp],
                               start=(kt == 0), stop=(kt == KT2 - 1))
                p.v.tensor_scalar(outT[:, og * tpp:(og + 1) * tpp], pa[:, :],
                                  scl[:, 4 * OG1 + og:4 * OG1 + og + 1], None, OP.mult)

            for k in range(ntt):
                par = k % 2
                om = otm[:, par * 1024:(par + 1) * 1024]
                ol = oln[:, par * 1024:(par + 1) * 1024]
                for half in range(2):
                    for og4 in range(4):
                        og = half * 4 + og4
                        p.t.transpose(pstf[:, og4 * 128:(og4 + 1) * 128],
                                      outT[:, og * tpp + k * 128:og * tpp + (k + 1) * 128],
                                      idf[:, :])
                    p.v.tensor_copy(
                        om.rearrange("p (g t) -> p g t", g=8)[:, half * 4:(half + 1) * 4, :],
                        pstf.rearrange("p (g t) -> p g t", g=4))
                emit_ln(p, [om], [ol], stat[:, 16 * ntt:], sq, epsc)
                p.v.tensor_tensor(ol, ol, xt[:, k * 1024:(k + 1) * 1024], OP.add)
                p.g.dma_start(out=xro[:, pp * ntt + k, :], in_=ol)

        with nc.Block() as block:
            p.emit(block, stack)

    return nc


def _lhsT_tiles(w):
    O, K = w.shape
    return np.ascontiguousarray(w.reshape(O // 128, 128, K).transpose(0, 2, 1))


def _pack_scales(*scales):
    cols = [np.ascontiguousarray(s.reshape(-1, 128).T) for s in scales]
    return np.ascontiguousarray(np.concatenate(cols, axis=1))


def _bf16(a):
    return np.ascontiguousarray(np.asarray(a, dtype=ml_dtypes.bfloat16))


def ffn0_inputs(p):
    return {
        "w1": _bf16(_lhsT_tiles(p["w1_w"])), "w2": _bf16(_lhsT_tiles(p["w2_w"])),
        "h1": _bf16(_lhsT_tiles(p["h1_w"])), "h2": _bf16(_lhsT_tiles(p["h2_w"])),
        "w3": _bf16(_lhsT_tiles(p["w3_w"])),
        "scales": _pack_scales(p["w1_s"], p["w2_s"], p["h1_s"], p["h2_s"], p["w3_s"]),
    }


def run_ffn0(x_full, p, tpp=512):
    xf = np.ascontiguousarray(x_full.reshape(16384, 1024).astype(np.float32))
    tok = xf.shape[0] // NCORES
    nc = build_ffn0(tok=tok, tpp=tpp)
    base = ffn0_inputs(p)
    in_maps = []
    for c in range(NCORES):
        m = dict(base)
        m["x_in"] = xf[c * tok:(c + 1) * tok]
        in_maps.append(m)
    res = run_bass_kernel_spmd(nc, in_maps, core_ids=list(range(NCORES)))
    out = np.concatenate([res.results[c]["x_out"] for c in range(NCORES)], axis=0)
    return out.reshape(2, 8192, 1024)




def build_rwkv(tok=2048, tpp=512):
    nc = bass.Bass("TRN2", target_bir_lowering=False)
    npass, ntt = tok // tpp, tpp // 128
    assert tpp == 512

    x_in = nc.dram_tensor("x_in", [tok, 1024], F32, kind="ExternalInput").ap()
    halo = nc.dram_tensor("halo", [1, 1024], F32, kind="ExternalInput").ap()
    wr = nc.dram_tensor("wr", [128, 8192], F32, kind="ExternalInput").ap()
    wk = nc.dram_tensor("wk", [128, 8192], F32, kind="ExternalInput").ap()
    wv = nc.dram_tensor("wv", [128, 8192], F32, kind="ExternalInput").ap()
    mus = nc.dram_tensor("mus", [128, 16], F32, kind="ExternalInput").ap()
    x_out = nc.dram_tensor("x_out", [tok, 1024], F32, kind="ExternalOutput").ap()

    xr_ = x_in.rearrange("(n p) d -> p n d", p=128)
    xro = x_out.rearrange("(n p) d -> p n d", p=128)

    with ExitStack() as stack:
        sb = lambda n, sh, dt: stack.enter_context(nc.sbuf_tensor(n, sh, dt))
        B = alloc_common(nc, stack, ntt, tpp)
        wrT = sb("wrT", [128, 8192], BF16)
        wkT = sb("wkT", [128, 8192], BF16)
        wvT = sb("wvT", [128, 8192], BF16)
        muc = sb("muc", [128, 16], F32)
        xprevT = sb("xprevT", [128, 8 * tpp], F32)
        mixr = sb("mixr", [128, 8 * tpp], BF16)
        mixk = sb("mixk", [128, 8 * tpp], BF16)
        dT = sb("dT", [128, tpp], F32)
        rT = sb("rT", [128, 8 * tpp], F32)
        kact = sb("kact", [128, 8 * tpp], BF16)
        relu_t = sb("relu_t", [128, 2 * tpp], F32)
        outT = sb("outT", [128, 8 * tpp], F32)
        carry = sb("carry", [128, 8], F32)
        hx = sb("hx", [1, 1024], F32)
        hln = sb("hln", [1, 1024], BF16)

        p = Prog(nc)
        emit_identities(p, B.idb, B.idf, B.epsc)
        for w_ap, wt in ((wr, wrT), (wk, wkT), (wv, wvT)):
            p.g.dma_start(out=wt[:, :], in_=w_ap[:, :])
        p.y.dma_start(out=muc[:, :], in_=mus[:, :])
        p.y.dma_start(out=hx[:, :], in_=halo[:, :])
        emit_ln(p, [hx[:, :]], [hln[:, :]], B.stat[:, 16 * ntt:], B.sq, B.epsc)
        for fg in range(8):
            p.t.transpose(B.pst[:, 2 * fg:2 * fg + 1], hln[:, fg * 128:(fg + 1) * 128],
                          B.idb[0:1, 0:1])
        p.v.tensor_copy(carry[:, :], B.pst.rearrange("p (a b) -> p a b", b=2)[:, 0:8, 0])

        for pp in range(npass):
            emit_head(p, B, xr_, pp)
            for fg in range(8):
                blk = slice(fg * tpp, (fg + 1) * tpp)
                p.v.tensor_copy(xprevT[:, fg * tpp + 1:(fg + 1) * tpp],
                                B.xlnT[:, fg * tpp:(fg + 1) * tpp - 1])
                p.v.tensor_copy(xprevT[:, fg * tpp:fg * tpp + 1], carry[:, fg:fg + 1])
                p.v.tensor_copy(carry[:, fg:fg + 1], B.xlnT[:, (fg + 1) * tpp - 1:(fg + 1) * tpp])
                p.v.tensor_tensor(dT[:, :], B.xlnT[:, blk], xprevT[:, blk], OP.subtract)
                p.v.scalar_tensor_tensor(mixr[:, blk], dT[:, :], muc[:, fg:fg + 1],
                                         xprevT[:, blk], OP.mult, OP.add)
                p.v.scalar_tensor_tensor(mixk[:, blk], dT[:, :], muc[:, 8 + fg:9 + fg],
                                         xprevT[:, blk], OP.mult, OP.add)

            def evac_r(og, ps):
                p.s.activation(rT[:, og * tpp:(og + 1) * tpp], ps[:, :tpp], AF.Sigmoid)
            emit_square_matmul(p, wrT, mixr, B.psmm, None, evac=evac_r, tpp=tpp)

            def evac_k(og, ps):
                rl = relu_t[:, (og % 2) * tpp:(og % 2 + 1) * tpp]
                p.s.activation(rl, ps[:, :tpp], AF.Relu)
                p.s.activation(kact[:, og * tpp:(og + 1) * tpp], rl, AF.Square)
            emit_square_matmul(p, wkT, mixk, B.psmm, None, evac=evac_k, tpp=tpp)

            def evac_v(og, ps):
                p.v.tensor_tensor(outT[:, og * tpp:(og + 1) * tpp], ps[:, :tpp],
                                  rT[:, og * tpp:(og + 1) * tpp], OP.mult)
            emit_square_matmul(p, wvT, kact, B.psmm, None, evac=evac_v, tpp=tpp)

            emit_tail(p, B, outT, xro, pp)

        with nc.Block() as block:
            p.emit(block, stack)
    return nc


def _lhsT_resident(w):
    O, K = w.shape
    wT = np.ascontiguousarray(w.T)
    a = wT.reshape(K // 128, 128, O // 128, 128).transpose(1, 2, 0, 3)
    return np.ascontiguousarray(a.reshape(128, (K // 128) * (O // 128) * 128))


def rwkv_inputs(pm):
    return {
        "wr": _lhsT_resident(pm["W_r"]), "wk": _lhsT_resident(pm["W_k"]),
        "wv": _lhsT_resident(pm["W_v"]),
        "mus": np.concatenate([pm["mu_r"].reshape(8, 128).T, pm["mu_k"].reshape(8, 128).T], axis=1).astype(np.float32),
    }


def run_rwkv(x_full, pm, tpp=512):
    xf = np.ascontiguousarray(x_full.reshape(16384, 1024).astype(np.float32))
    tok = xf.shape[0] // NCORES
    nc = build_rwkv(tok=tok, tpp=tpp)
    base = rwkv_inputs(pm)
    in_maps = []
    for c in range(NCORES):
        m = dict(base)
        m["x_in"] = xf[c * tok:(c + 1) * tok]
        if c % 4 == 0:
            m["halo"] = np.zeros((1, 1024), np.float32)
        else:
            m["halo"] = xf[c * tok - 1:c * tok]
        in_maps.append(m)
    res = run_bass_kernel_spmd(nc, in_maps, core_ids=list(range(NCORES)))
    out = np.concatenate([res.results[c]["x_out"] for c in range(NCORES)], axis=0)
    return out.reshape(2, 8192, 1024)




def build_moe(tok=2048, tpp=512):
    nc = bass.Bass("TRN2", target_bir_lowering=False)
    npass, ntt = tok // tpp, tpp // 128
    NE, HID = 16, 512
    assert tpp == 512

    x_in = nc.dram_tensor("x_in", [tok, 1024], F32, kind="ExternalInput").ap()
    gw = nc.dram_tensor("gw", [128, 8 * 16], F32, kind="ExternalInput").ap()
    w1 = nc.dram_tensor("w1", [NE, 128, 4096], BF16, kind="ExternalInput").ap()
    w3 = nc.dram_tensor("w3", [NE, 128, 4096], BF16, kind="ExternalInput").ap()
    w2 = nc.dram_tensor("w2", [NE, 128, 4096], BF16, kind="ExternalInput").ap()
    sw1 = nc.dram_tensor("sw1", [128, 4096], F32, kind="ExternalInput").ap()
    sw3 = nc.dram_tensor("sw3", [128, 4096], F32, kind="ExternalInput").ap()
    sw2 = nc.dram_tensor("sw2", [128, 4096], F32, kind="ExternalInput").ap()
    x_out = nc.dram_tensor("x_out", [tok, 1024], F32, kind="ExternalOutput").ap()

    xr_ = x_in.rearrange("(n p) d -> p n d", p=128)
    xro = x_out.rearrange("(n p) d -> p n d", p=128)

    with ExitStack() as stack:
        sb = lambda n, sh, dt: stack.enter_context(nc.sbuf_tensor(n, sh, dt))
        B = alloc_common(nc, stack, ntt, tpp)
        gwc = sb("gwc", [128, 8 * 16], BF16)
        esel = sb("esel", [16, 16 * 128], F32)
        logit = sb("logit", [128, ntt * 16], F32)
        gstat = sb("gstat", [128, ntt * 8], F32)
        rem = sb("rem", [128, ntt * 16], F32)
        selm = sb("selm", [128, ntt * 16], F32)
        tmpm = sb("tmpm", [128, ntt * 16], F32)
        cb = sb("cb", [128, ntt * 16], F32)
        cbT = sb("cbT", [16, ntt * 128], F32)
        cbB = sb("cbB", [128, tpp], F32)
        wsa = [sb(f"wsa{i}", [128, 4096], BF16) for i in range(2)]
        wsb = [sb(f"wsb{i}", [128, 4096], BF16) for i in range(2)]
        wsc = [sb(f"wsc{i}", [128, 4096], BF16) for i in range(2)]
        swc = [sb(f"swc{i}", [128, 4096], BF16) for i in range(3)]
        act = sb("act", [128, 8 * 4 * tpp], BF16)
        acts = sb("acts", [128, 4 * tpp], BF16)
        tmpa = sb("tmpa", [128, 2 * tpp], F32)
        tmpu = sb("tmpu", [128, 2 * tpp], F32)
        outT = sb("outT", [128, 8 * tpp], F32)

        p = Prog(nc)
        emit_identities(p, B.idb, B.idf, B.epsc)
        p.g.dma_start(out=gwc[:, :], in_=gw[:, :])
        for s_ap, st in ((sw1, swc[0]), (sw3, swc[1]), (sw2, swc[2])):
            p.g.dma_start(out=st[:, :], in_=s_ap[:, :])
        p.g.memset(esel[:, :], 0.0)
        p.g.affine_select(out=esel.rearrange("p (e m) -> p e m", e=16),
                          in_=esel.rearrange("p (e m) -> p e m", e=16),
                          compare_op=OP.not_equal, fill=1.0, base=0,
                          pattern=[[-1, 16], [0, 128]], channel_multiplier=1)

        for pp in range(npass):
            emit_head(p, B, xr_, pp)

            for k in range(ntt):
                ps = B.psmm[2]
                for kt in range(8):
                    p.t.matmul(ps[:, 0:16],
                               B.xlnT[:, kt * tpp + k * 128:kt * tpp + (k + 1) * 128],
                               gwc[:, kt * 16:(kt + 1) * 16],
                               start=(kt == 0), stop=(kt == 7))
                p.v.tensor_copy(logit[:, k * 16:(k + 1) * 16], ps[:, 0:16])
            for k in range(ntt):
                lg = logit[:, k * 16:(k + 1) * 16]
                st = gstat[:, k * 8:(k + 1) * 8]
                p.v.tensor_reduce(st[:, 0:1], lg, AX.X, OP.max)
                p.v.tensor_scalar(st[:, 1:2], st[:, 0:1], -1.0, None, OP.mult)
                p.s.activation(lg, lg, AF.Exp, bias=st[:, 1:2])
                p.v.tensor_reduce(st[:, 2:3], lg, AX.X, OP.add)
                p.v.reciprocal(st[:, 3:4], st[:, 2:3])
                p.v.tensor_scalar(lg, lg, st[:, 3:4], None, OP.mult)
                rm = rem[:, k * 16:(k + 1) * 16]
                sl = selm[:, k * 16:(k + 1) * 16]
                tp = tmpm[:, k * 16:(k + 1) * 16]
                p.v.tensor_copy(rm, lg)
                p.v.memset(sl, 0.0)
                for it in range(4):
                    p.v.tensor_reduce(st[:, 4:5], rm, AX.X, OP.max)
                    p.v.tensor_scalar(tp, rm, st[:, 4:5], None, OP.is_equal)
                    p.v.tensor_tensor(sl, sl, tp, OP.add)
                    if it < 3:
                        p.v.tensor_tensor(tp, tp, rm, OP.mult)
                        p.v.tensor_tensor(rm, rm, tp, OP.subtract)
                cbk = cb[:, k * 16:(k + 1) * 16]
                p.v.tensor_tensor(cbk, lg, sl, OP.mult)
                p.v.tensor_reduce(st[:, 5:6], cbk, AX.X, OP.add)
                p.v.reciprocal(st[:, 6:7], st[:, 5:6])
                p.v.tensor_scalar(cbk, cbk, st[:, 6:7], None, OP.mult)
                p.t.transpose(B.pstf[0:16, 0:128], cbk, B.idf[:, :])
                p.v.tensor_copy(cbT[:, k * 128:(k + 1) * 128], B.pstf[0:16, 0:128])

            for og in range(4):
                psA, psB = B.psmm[0], B.psmm[1]
                for kt in range(8):
                    p.t.matmul(psA[:, :], swc[0][:, (og * 8 + kt) * 128:(og * 8 + kt + 1) * 128],
                               B.xlnT[:, kt * tpp:(kt + 1) * tpp],
                               start=(kt == 0), stop=(kt == 7))
                for kt in range(8):
                    p.t.matmul(psB[:, :], swc[1][:, (og * 8 + kt) * 128:(og * 8 + kt + 1) * 128],
                               B.xlnT[:, kt * tpp:(kt + 1) * tpp],
                               start=(kt == 0), stop=(kt == 7))
                ta = tmpa[:, (og % 2) * tpp:(og % 2 + 1) * tpp]
                p.s.activation(ta, psA[:, :], AF.Silu)
                p.v.tensor_tensor(acts[:, og * tpp:(og + 1) * tpp], psB[:, :], ta, OP.mult)

            for half in range(2):
                for ee in range(8):
                    e = half * 8 + ee
                    wa, wb = wsa[e % 2], wsb[e % 2]
                    p.g.dma_start(out=wa[:, :], in_=w1[e])
                    p.y.dma_start(out=wb[:, :], in_=w3[e])
                    p.t.matmul(B.pstf[:, 0:tpp], esel[:, e * 128:(e + 1) * 128],
                               cbT[:, 0:tpp], start=True, stop=True)
                    p.v.tensor_copy(cbB[:, :], B.pstf[:, 0:tpp])
                    for og in range(4):
                        psA, psB = B.psmm[0], B.psmm[1]
                        for kt in range(8):
                            p.t.matmul(psA[:, :], wa[:, (og * 8 + kt) * 128:(og * 8 + kt + 1) * 128],
                                       B.xlnT[:, kt * tpp:(kt + 1) * tpp],
                                       start=(kt == 0), stop=(kt == 7))
                        for kt in range(8):
                            p.t.matmul(psB[:, :], wb[:, (og * 8 + kt) * 128:(og * 8 + kt + 1) * 128],
                                       B.xlnT[:, kt * tpp:(kt + 1) * tpp],
                                       start=(kt == 0), stop=(kt == 7))
                        ta = tmpa[:, (og % 2) * tpp:(og % 2 + 1) * tpp]
                        tu = tmpu[:, (og % 2) * tpp:(og % 2 + 1) * tpp]
                        p.s.activation(ta, psA[:, :], AF.Silu)
                        p.v.tensor_tensor(tu, psB[:, :], ta, OP.mult)
                        p.v.tensor_tensor(act[:, (ee * 4 + og) * tpp:(ee * 4 + og + 1) * tpp],
                                          tu, cbB[:, :], OP.mult)

                for ogg in range(2):
                    for ee in range(8):
                        e = half * 8 + ee
                        wc = wsc[ee % 2]
                        eng = p.g if ee % 2 == 0 else p.y
                        eng.dma_start(out=wc[:, 0:2048],
                                      in_=w2[e][:, ogg * 2048:(ogg + 1) * 2048])
                        for og4 in range(4):
                            ps = B.psmm[og4]
                            for kt in range(4):
                                stop = (half == 0 and ee == 7 and kt == 3)
                                p.t.matmul(ps[:, :], wc[:, (og4 * 4 + kt) * 128:(og4 * 4 + kt + 1) * 128],
                                           act[:, (ee * 4 + kt) * tpp:(ee * 4 + kt + 1) * tpp],
                                           start=(ee == 0 and kt == 0), stop=stop)
                    for og4 in range(4):
                        og = ogg * 4 + og4
                        ps = B.psmm[og4]
                        if half == 0:
                            p.v.tensor_copy(outT[:, og * tpp:(og + 1) * tpp], ps[:, :])
                        else:
                            for kt in range(4):
                                p.t.matmul(ps[:, :], swc[2][:, (og * 4 + kt) * 128:(og * 4 + kt + 1) * 128],
                                           acts[:, kt * tpp:(kt + 1) * tpp],
                                           start=False, stop=(kt == 3))
                            p.v.tensor_tensor(outT[:, og * tpp:(og + 1) * tpp],
                                              outT[:, og * tpp:(og + 1) * tpp], ps[:, :], OP.add)

            emit_tail(p, B, outT, xro, pp)

        with nc.Block() as block:
            p.emit(block, stack)
    return nc


def moe_inputs(pm):
    def stack_lhsT(ws):
        return _bf16(np.stack([_lhsT_resident(w) for w in ws], axis=0))
    gwT = pm["gate_w"].T
    gpack = np.ascontiguousarray(gwT.reshape(8, 128, 16).transpose(1, 0, 2).reshape(128, 128))
    return {
        "gw": gpack.astype(np.float32),
        "w1": stack_lhsT(pm["w1"]), "w3": stack_lhsT(pm["w3"]), "w2": stack_lhsT(pm["w2"]),
        "sw1": _lhsT_resident(pm["sw1"]), "sw3": _lhsT_resident(pm["sw3"]),
        "sw2": _lhsT_resident(pm["sw2"]),
    }


def run_moe(x_full, pm, tpp=512):
    xf = np.ascontiguousarray(x_full.reshape(16384, 1024).astype(np.float32))
    tok = xf.shape[0] // NCORES
    nc = build_moe(tok=tok, tpp=tpp)
    base = moe_inputs(pm)
    in_maps = []
    for c in range(NCORES):
        m = dict(base)
        m["x_in"] = xf[c * tok:(c + 1) * tok]
        in_maps.append(m)
    res = run_bass_kernel_spmd(nc, in_maps, core_ids=list(range(NCORES)))
    out = np.concatenate([res.results[c]["x_out"] for c in range(NCORES)], axis=0)
    return out.reshape(2, 8192, 1024)




def build_l1a_qkv(tok=2048, tpp=512):
    nc = bass.Bass("TRN2", target_bir_lowering=False)
    npass, ntt = tok // tpp, tpp // 128
    assert tpp == 512

    x_in = nc.dram_tensor("x_in", [tok, 1024], F32, kind="ExternalInput").ap()
    wq = nc.dram_tensor("wq", [128, 8192], F32, kind="ExternalInput").ap()
    wk = nc.dram_tensor("wk", [128, 8192], F32, kind="ExternalInput").ap()
    wvr = nc.dram_tensor("wvr", [128, 8192], F32, kind="ExternalInput").ap()
    wkr = nc.dram_tensor("wkr", [128, 8192], F32, kind="ExternalInput").ap()
    qT_o = nc.dram_tensor("qT_o", [128, 8 * tok], BF16, kind="ExternalOutput").ap()
    kT_o = nc.dram_tensor("kT_o", [128, 8 * tok], BF16, kind="ExternalOutput").ap()
    ktm_o = nc.dram_tensor("ktm_o", [tok, 1024], BF16, kind="ExternalOutput").ap()
    vtm_o = nc.dram_tensor("vtm_o", [tok, 1024], BF16, kind="ExternalOutput").ap()
    S_o = nc.dram_tensor("S_o", [128, 8 * 128], F32, kind="ExternalOutput").ap()
    Z_o = nc.dram_tensor("Z_o", [128, 8], F32, kind="ExternalOutput").ap()

    xr_ = x_in.rearrange("(n p) d -> p n d", p=128)
    ktm_r = ktm_o.rearrange("(n p) d -> p n d", p=128)
    vtm_r = vtm_o.rearrange("(n p) d -> p n d", p=128)

    with ExitStack() as stack:
        sb = lambda n, sh, dt: stack.enter_context(nc.sbuf_tensor(n, sh, dt))
        B = alloc_common(nc, stack, ntt, tpp)
        wqc = sb("wqc", [128, 8192], BF16)
        wkc = sb("wkc", [128, 8192], BF16)
        wvrc = sb("wvrc", [128, 8192], BF16)
        wkrc = sb("wkrc", [128, 8192], BF16)
        qT = sb("qT", [128, 8 * tpp], BF16)
        kT = sb("kT", [128, 8 * tpp], BF16)
        ktm = sb("ktm", [128, ntt * 1024], BF16)
        vtm = sb("vtm", [128, ntt * 1024], BF16)
        Ssb = sb("Ssb", [128, 8 * 128], F32)
        Zsb = sb("Zsb", [128, 8], F32)
        ones = sb("ones", [128, 1], BF16)

        p = Prog(nc)
        emit_identities(p, B.idb, B.idf, B.epsc)
        p.g.memset(ones[:, :], 1.0)
        p.v.memset(Ssb[:, :], 0.0)
        p.v.memset(Zsb[:, :], 0.0)
        for w_ap, wt in ((wq, wqc), (wk, wkc), (wvr, wvrc), (wkr, wkrc)):
            p.g.dma_start(out=wt[:, :], in_=w_ap[:, :])

        for pp in range(npass):
            emit_head(p, B, xr_, pp)
            for name, wt, dst, hbm in (("q", wqc, qT, qT_o), ("k", wkc, kT, kT_o)):
                for og in range(8):
                    ps = B.psmm[og % 2]
                    for kt in range(8):
                        p.t.matmul(ps[:, :], wt[:, (og * 8 + kt) * 128:(og * 8 + kt + 1) * 128],
                                   B.xlnT[:, kt * tpp:(kt + 1) * tpp],
                                   start=(kt == 0), stop=(kt == 7))
                    p.v.tensor_copy(dst[:, og * tpp:(og + 1) * tpp], ps[:, :])
                p.g.dma_start(out=hbm.rearrange("p (h t) -> p h t", h=8)[:, :, pp * tpp:(pp + 1) * tpp],
                              in_=dst.rearrange("p (h t) -> p h t", h=8))
            for name, wt, dst, hbm in (("kt", wkrc, ktm, ktm_r), ("vt", wvrc, vtm, vtm_r)):
                for k in range(ntt):
                    for oc in range(2):
                        ps = B.psmm[2 + oc]
                        for kt in range(8):
                            p.t.matmul(ps[:, :], B.xlnT[:, kt * tpp + k * 128:kt * tpp + (k + 1) * 128],
                                       wt[:, kt * 1024 + oc * 512:kt * 1024 + (oc + 1) * 512],
                                       start=(kt == 0), stop=(kt == 7))
                        p.v.tensor_copy(dst[:, k * 1024 + oc * 512:k * 1024 + (oc + 1) * 512], ps[:, :])
                    p.g.dma_start(out=hbm[:, pp * ntt + k, :], in_=dst[:, k * 1024:(k + 1) * 1024])
            for k in range(ntt):
                for h in range(8):
                    ps = B.psmm[h % 2]
                    p.t.matmul(ps[:, 0:128], ktm[:, k * 1024 + h * 128:k * 1024 + (h + 1) * 128],
                               vtm[:, k * 1024 + h * 128:k * 1024 + (h + 1) * 128],
                               start=True, stop=True)
                    p.v.tensor_tensor(Ssb[:, h * 128:(h + 1) * 128], Ssb[:, h * 128:(h + 1) * 128],
                                      ps[:, 0:128], OP.add)
                ps = B.psmm[2]
                for h in range(8):
                    p.t.matmul(ps[:, 2 * h:2 * h + 1],
                               ktm[:, k * 1024 + h * 128:k * 1024 + (h + 1) * 128],
                               ones[:, :], start=True, stop=True)
                p.v.tensor_tensor(Zsb[:, :], Zsb[:, :],
                                  ps.rearrange("p (a b) -> p a b", b=2)[:, 0:8, 0], OP.add)

        p.g.dma_start(out=S_o[:, :], in_=Ssb[:, :])
        p.g.dma_start(out=Z_o[:, :], in_=Zsb[:, :])

        with nc.Block() as block:
            p.emit(block, stack)
    return nc


def build_l1a_out(tok=2048, tpp=512):
    nc = bass.Bass("TRN2", target_bir_lowering=False)
    npass, ntt = tok // tpp, tpp // 128
    assert tpp == 512

    x_in = nc.dram_tensor("x_in", [tok, 1024], F32, kind="ExternalInput").ap()
    qT_i = nc.dram_tensor("qT_i", [128, 8 * tok], BF16, kind="ExternalInput").ap()
    kT_i = nc.dram_tensor("kT_i", [128, 8 * tok], BF16, kind="ExternalInput").ap()
    ktm_i = nc.dram_tensor("ktm_i", [tok, 1024], BF16, kind="ExternalInput").ap()
    vtm_i = nc.dram_tensor("vtm_i", [tok, 1024], BF16, kind="ExternalInput").ap()
    S_all = nc.dram_tensor("S_all", [8, 128, 8 * 128], F32, kind="ExternalInput").ap()
    Z_all = nc.dram_tensor("Z_all", [8, 128, 8], F32, kind="ExternalInput").ap()
    pmask = nc.dram_tensor("pmask", [128, 8], F32, kind="ExternalInput").ap()
    wo = nc.dram_tensor("wo", [128, 8192], F32, kind="ExternalInput").ap()
    nw = nc.dram_tensor("nw", [1, 128], F32, kind="ExternalInput").ap()
    x_out = nc.dram_tensor("x_out", [tok, 1024], F32, kind="ExternalOutput").ap()

    xr_ = x_in.rearrange("(n p) d -> p n d", p=128)
    xro = x_out.rearrange("(n p) d -> p n d", p=128)
    ktm_r = ktm_i.rearrange("(n p) d -> p n d", p=128)
    vtm_r = vtm_i.rearrange("(n p) d -> p n d", p=128)

    with ExitStack() as stack:
        sb = lambda n, sh, dt: stack.enter_context(nc.sbuf_tensor(n, sh, dt))
        B = alloc_common(nc, stack, ntt, tpp)
        woc = sb("woc", [128, 8192], BF16)
        Ssb = sb("Ssb", [128, 8 * 128], F32)
        Sbf = sb("Sbf", [128, 128], BF16)
        Zsb = sb("Zsb", [128, 8], F32)
        Zbf = sb("Zbf", [128, 1], BF16)
        Stmp = sb("Stmp", [128, 8 * 128], F32)
        Ztmp = sb("Ztmp", [128, 8], F32)
        msk = sb("msk", [128, 8], F32)
        triu = sb("triu", [128, 128], BF16)
        ones = sb("ones", [128, 1], BF16)
        nwb = sb("nwb", [128, 128], F32)
        nwrow = sb("nwrow", [1, 128], F32)
        onesrow = sb("onesrow", [1, 128], F32)
        qTl = sb("qTl", [128, 8 * tpp], BF16)
        kTl = sb("kTl", [128, 8 * tpp], BF16)
        ktm = sb("ktm", [128, ntt * 1024], BF16)
        vtm = sb("vtm", [128, ntt * 1024], BF16)
        sTm = sb("sTm", [128, 2 * 128], BF16)
        osb = sb("osb", [128, 2 * 128], F32)
        dstat = sb("dstat", [128, 16], F32)
        onb = sb("onb", [128, 2 * 128], BF16)
        oT = sb("oT", [128, 8 * tpp], BF16)
        outT = sb("outT", [128, 8 * tpp], F32)

        p = Prog(nc)
        emit_identities(p, B.idb, B.idf, B.epsc)
        p.g.memset(ones[:, :], 1.0)
        p.g.memset(triu[:, :], 1.0)
        p.g.affine_select(out=triu[:, :], in_=triu[:, :],
                          compare_op=OP.is_ge, fill=0.0, base=0,
                          pattern=[[1, 128]], channel_multiplier=-1)
        p.g.dma_start(out=woc[:, :], in_=wo[:, :])
        p.y.dma_start(out=msk[:, :], in_=pmask[:, :])
        p.y.dma_start(out=nwrow[:, :], in_=nw[:, :])
        p.g.memset(onesrow[:, :], 1.0)
        p.t.matmul(B.pstf[:, 0:128], onesrow[:, :], nwrow[:, :], start=True, stop=True)
        p.v.tensor_copy(nwb[:, :], B.pstf[:, 0:128])
        p.v.memset(Ssb[:, :], 0.0)
        p.v.memset(Zsb[:, :], 0.0)
        for r in range(8):
            p.g.dma_start(out=Stmp[:, :], in_=S_all[r])
            p.g.dma_start(out=Ztmp[:, :], in_=Z_all[r])
            p.v.scalar_tensor_tensor(Ssb[:, :], Stmp[:, :], msk[:, r:r + 1], Ssb[:, :],
                                     OP.mult, OP.add)
            p.v.scalar_tensor_tensor(Zsb[:, :], Ztmp[:, :], msk[:, r:r + 1], Zsb[:, :],
                                     OP.mult, OP.add)

        for pp in range(npass):
            emit_head(p, B, xr_, pp)
            p.y.dma_start(out=qTl.rearrange("p (h t) -> p h t", h=8),
                          in_=qT_i.rearrange("p (h t) -> p h t", h=8)[:, :, pp * tpp:(pp + 1) * tpp])
            p.y.dma_start(out=kTl.rearrange("p (h t) -> p h t", h=8),
                          in_=kT_i.rearrange("p (h t) -> p h t", h=8)[:, :, pp * tpp:(pp + 1) * tpp])
            p.y.dma_start(out=ktm.rearrange("p (n d) -> p n d", d=1024),
                          in_=ktm_r[:, pp * ntt:(pp + 1) * ntt, :])
            p.y.dma_start(out=vtm.rearrange("p (n d) -> p n d", d=1024),
                          in_=vtm_r[:, pp * ntt:(pp + 1) * ntt, :])
            for k in range(ntt):
                for h in range(8):
                    par = h % 2
                    qblk = qTl[:, h * tpp + k * 128:h * tpp + (k + 1) * 128]
                    kblk = kTl[:, h * tpp + k * 128:h * tpp + (k + 1) * 128]
                    ktb = ktm[:, k * 1024 + h * 128:k * 1024 + (h + 1) * 128]
                    vtb = vtm[:, k * 1024 + h * 128:k * 1024 + (h + 1) * 128]
                    Sh = Ssb[:, h * 128:(h + 1) * 128]
                    ps = B.psmm[par]
                    p.t.matmul(ps[:, 0:128], kblk, qblk, start=True, stop=True)
                    stm = sTm[:, par * 128:(par + 1) * 128]
                    p.v.tensor_tensor(stm, ps[:, 0:128], triu[:, :], OP.mult)
                    p.v.tensor_copy(Sbf[:, :], Sh)
                    po = B.psmm[2 + par]
                    p.t.matmul(po[:, 0:128], stm, vtb, start=True, stop=False)
                    p.t.matmul(po[:, 0:128], qblk, Sbf[:, :], start=False, stop=True)
                    p.v.tensor_copy(Zbf[:, :], Zsb[:, h:h + 1])
                    p.t.matmul(ps[:, 192:193], stm, ones[:, :], start=True, stop=False)
                    p.t.matmul(ps[:, 192:193], qblk, Zbf[:, :], start=False, stop=True)
                    p.v.tensor_scalar(dstat[:, 0:1], ps[:, 192:193], -1.0, None, OP.mult)
                    p.v.tensor_tensor(dstat[:, 1:2], ps[:, 192:193], dstat[:, 0:1], OP.max)
                    p.v.tensor_scalar(dstat[:, 2:3], dstat[:, 1:2], 1.0, None, OP.max)
                    p.v.reciprocal(dstat[:, 3:4], dstat[:, 2:3])
                    ob = osb[:, par * 128:(par + 1) * 128]
                    p.v.tensor_scalar(ob, po[:, 0:128], dstat[:, 3:4], None, OP.mult)
                    p.v.scalar_tensor_tensor(B.sq[:, 0:128], ob, 1.0, ob, OP.mult, OP.mult,
                                             accum_out=dstat[:, 4:5])
                    p.v.tensor_scalar(dstat[:, 5:6], dstat[:, 4:5], 1.0 / 128, None, OP.mult)
                    p.s.activation(dstat[:, 6:7], dstat[:, 5:6], AF.Ln, bias=B.epsc[:, 0:1])
                    p.s.activation(dstat[:, 7:8], dstat[:, 6:7], AF.Exp, scale=-0.5)
                    onbb = onb[:, par * 128:(par + 1) * 128]
                    p.v.tensor_scalar(B.sq[:, 128:256], ob, dstat[:, 7:8], None, OP.mult)
                    p.v.tensor_tensor(onbb, B.sq[:, 128:256], nwb[:, :], OP.mult)
                    p.t.transpose(B.pst[:, par * 128:(par + 1) * 128], onbb, B.idb[:, :])
                    p.v.tensor_copy(oT.rearrange("p (h t) -> p h t", h=8)[:, h, k * 128:(k + 1) * 128],
                                    B.pst[:, par * 128:(par + 1) * 128])
                    p.t.matmul(ps[:, 0:128], ktb, vtb, start=True, stop=True)
                    p.v.tensor_tensor(Sh, Sh, ps[:, 0:128], OP.add)
                    p.t.matmul(ps[:, 128:129], ktb, ones[:, :], start=True, stop=True)
                    p.v.tensor_tensor(Zsb[:, h:h + 1], Zsb[:, h:h + 1], ps[:, 128:129], OP.add)
            for og in range(8):
                ps = B.psmm[og % 2]
                for kt in range(8):
                    p.t.matmul(ps[:, :], woc[:, (og * 8 + kt) * 128:(og * 8 + kt + 1) * 128],
                               oT[:, kt * tpp:(kt + 1) * tpp],
                               start=(kt == 0), stop=(kt == 7))
                p.v.tensor_copy(outT[:, og * tpp:(og + 1) * tpp], ps[:, :])
            emit_tail(p, B, outT, xro, pp)

        with nc.Block() as block:
            p.emit(block, stack)
    return nc


def _rhs_resident(w):
    wT = np.ascontiguousarray(w.T)
    F_, O = wT.shape
    a = wT.reshape(F_ // 128, 128, O).transpose(1, 0, 2)
    return np.ascontiguousarray(a.reshape(128, (F_ // 128) * O))


def run_l1_attn(x_full, pm, tpp=512):
    xf = np.ascontiguousarray(x_full.reshape(16384, 1024).astype(np.float32))
    tok = xf.shape[0] // NCORES
    ncA = build_l1a_qkv(tok=tok, tpp=tpp)
    baseA = {
        "wq": _lhsT_resident(pm["Wq"]), "wk": _lhsT_resident(pm["Wk"]),
        "wvr": _rhs_resident(pm["Wv"]), "wkr": _rhs_resident(pm["Wk"]),
    }
    in_maps = []
    for c in range(NCORES):
        m = dict(baseA)
        m["x_in"] = xf[c * tok:(c + 1) * tok]
        in_maps.append(m)
    resA = run_bass_kernel_spmd(ncA, in_maps, core_ids=list(range(NCORES)))
    S_all = np.stack([resA.results[c]["S_o"] for c in range(NCORES)], axis=0)
    Z_all = np.stack([resA.results[c]["Z_o"] for c in range(NCORES)], axis=0)
    ncB = build_l1a_out(tok=tok, tpp=tpp)
    baseB = {"wo": _lhsT_resident(pm["Wo"]), "S_all": S_all, "Z_all": Z_all,
             "nw": pm["norm_w"].reshape(1, 128).astype(np.float32)}
    in_maps = []
    for c in range(NCORES):
        m = dict(baseB)
        m["x_in"] = xf[c * tok:(c + 1) * tok]
        m["qT_i"] = resA.results[c]["qT_o"]
        m["kT_i"] = resA.results[c]["kT_o"]
        m["ktm_i"] = resA.results[c]["ktm_o"]
        m["vtm_i"] = resA.results[c]["vtm_o"]
        pmask = np.zeros((128, 8), np.float32)
        for r in range((c // 4) * 4, c):
            pmask[:, r] = 1.0
        m["pmask"] = pmask
        in_maps.append(m)
    resB = run_bass_kernel_spmd(ncB, in_maps, core_ids=list(range(NCORES)))
    out = np.concatenate([resB.results[c]["x_out"] for c in range(NCORES)], axis=0)
    return out.reshape(2, 8192, 1024)




CH = 128
QS = 16


def emit_gla_state_update(p, B, S, kT, Lc, vtm_chunk, kfac, khf, khtm, gam, psg):
    for kb in range(8):
        Lkb = Lc[kb]
        p.v.tensor_scalar(kfac[:, :], Lkb, Lkb[:, CH - 1:CH], None, OP.subtract)
        p.s.activation(kfac[:, :], kfac[:, :], AF.Exp, scale=-1.0)
        p.v.tensor_tensor(khf[:, :], kT[kb], kfac[:, :], OP.mult)
        p.t.transpose(B.pst[:, 0:128], khf[:, :], B.idb[:, :])
        p.v.tensor_copy(khtm[:, kb * 128:(kb + 1) * 128], B.pst[:, 0:128])
        p.s.activation(gam[:, kb:kb + 1], Lkb[:, CH - 1:CH], AF.Exp)
    for kb in range(8):
        p.v.tensor_scalar(S[:, kb * 1024:(kb + 1) * 1024],
                          S[:, kb * 1024:(kb + 1) * 1024],
                          gam[:, kb:kb + 1], None, OP.mult)
        for vh in range(2):
            ps = psg[(kb * 2 + vh) % 2]
            p.t.matmul(ps[:, :], khtm[:, kb * 128:(kb + 1) * 128],
                       vtm_chunk[:, vh * 512:(vh + 1) * 512], start=True, stop=True)
            p.v.tensor_tensor(S[:, kb * 1024 + vh * 512:kb * 1024 + (vh + 1) * 512],
                              S[:, kb * 1024 + vh * 512:kb * 1024 + (vh + 1) * 512],
                              ps[:, :], OP.add)


def build_l2a_qkv(tok=2048, tpp=512):
    nc = bass.Bass("TRN2", target_bir_lowering=False)
    npass, ntt = tok // tpp, tpp // 128
    assert tpp == 512

    x_in = nc.dram_tensor("x_in", [tok, 1024], F32, kind="ExternalInput").ap()
    wq = nc.dram_tensor("wq", [128, 8192], F32, kind="ExternalInput").ap()
    wk = nc.dram_tensor("wk", [128, 8192], F32, kind="ExternalInput").ap()
    wg = nc.dram_tensor("wg", [128, 8192], F32, kind="ExternalInput").ap()
    wvr = nc.dram_tensor("wvr", [128, 8192], F32, kind="ExternalInput").ap()
    bqkg = nc.dram_tensor("bqkg", [128, 24], F32, kind="ExternalInput").ap()
    bvrow = nc.dram_tensor("bvrow", [1, 1024], F32, kind="ExternalInput").ap()
    qT_o = nc.dram_tensor("qT_o", [128, 8 * tok], BF16, kind="ExternalOutput").ap()
    kT_o = nc.dram_tensor("kT_o", [128, 8 * tok], BF16, kind="ExternalOutput").ap()
    vtm_o = nc.dram_tensor("vtm_o", [tok, 1024], BF16, kind="ExternalOutput").ap()
    L_o = nc.dram_tensor("L_o", [128, 8 * tok], F32, kind="ExternalOutput").ap()
    S_o = nc.dram_tensor("S_o", [128, 8 * 1024], F32, kind="ExternalOutput").ap()
    lg_o = nc.dram_tensor("lg_o", [128, 8], F32, kind="ExternalOutput").ap()

    xr_ = x_in.rearrange("(n p) d -> p n d", p=128)
    vtm_r = vtm_o.rearrange("(n p) d -> p n d", p=128)

    with ExitStack() as stack:
        sb = lambda n, sh, dt: stack.enter_context(nc.sbuf_tensor(n, sh, dt))
        B = alloc_common(nc, stack, ntt, tpp)
        wqc = sb("wqc", [128, 8192], BF16)
        wkc = sb("wkc", [128, 8192], BF16)
        wgc = sb("wgc", [128, 8192], BF16)
        wvrc = sb("wvrc", [128, 8192], BF16)
        bqk = sb("bqk", [128, 24], F32)
        bvb = sb("bvb", [128, 1024], F32)
        bvr = sb("bvr", [1, 1024], F32)
        onesrow = sb("onesrow", [1, 128], F32)
        qT = sb("qT", [128, 8 * tpp], BF16)
        kT = sb("kT", [128, 8 * tpp], BF16)
        vtm = sb("vtm", [128, ntt * 1024], BF16)
        Lb = sb("Lb", [128, 8 * tpp], F32)
        lgT = sb("lgT", [128, tpp], F32)
        Ssb = sb("Ssb", [128, 8 * 1024], F32)
        logG = sb("logG", [128, 8], F32)
        kfac = sb("kfac", [128, CH], F32)
        khf = sb("khf", [128, CH], BF16)
        khtm = sb("khtm", [128, 1024], BF16)
        gam = sb("gam", [128, 8], F32)
        onesc = sb("onesc", [128, CH], F32)

        p = Prog(nc)
        emit_identities(p, B.idb, B.idf, B.epsc)
        p.g.memset(onesc[:, :], 1.0)
        p.g.memset(onesrow[:, :], 1.0)
        p.v.memset(Ssb[:, :], 0.0)
        p.v.memset(logG[:, :], 0.0)
        for w_ap, wt in ((wq, wqc), (wk, wkc), (wg, wgc), (wvr, wvrc)):
            p.g.dma_start(out=wt[:, :], in_=w_ap[:, :])
        p.y.dma_start(out=bqk[:, :], in_=bqkg[:, :])
        p.y.dma_start(out=bvr[:, :], in_=bvrow[:, :])
        p.t.matmul(B.pstf[:, 0:512], onesrow[:, :], bvr[:, 0:512], start=True, stop=True)
        p.v.tensor_copy(bvb[:, 0:512], B.pstf[:, 0:512])
        p.t.matmul(B.pstf[:, 0:512], onesrow[:, :], bvr[:, 512:1024], start=True, stop=True)
        p.v.tensor_copy(bvb[:, 512:1024], B.pstf[:, 0:512])

        for pp in range(npass):
            emit_head(p, B, xr_, pp)
            for wt, dst, boff, fn in ((wqc, qT, 0, AF.Identity), (wkc, kT, 8, AF.Sigmoid)):
                for og in range(8):
                    ps = B.psmm[og % 2]
                    for kt in range(8):
                        p.t.matmul(ps[:, :], wt[:, (og * 8 + kt) * 128:(og * 8 + kt + 1) * 128],
                                   B.xlnT[:, kt * tpp:(kt + 1) * tpp],
                                   start=(kt == 0), stop=(kt == 7))
                    p.s.activation(dst[:, og * tpp:(og + 1) * tpp], ps[:, :],
                                   fn, bias=bqk[:, boff + og:boff + og + 1])
            for og in range(8):
                ps = B.psmm[og % 2]
                for kt in range(8):
                    p.t.matmul(ps[:, :], wgc[:, (og * 8 + kt) * 128:(og * 8 + kt + 1) * 128],
                               B.xlnT[:, kt * tpp:(kt + 1) * tpp],
                               start=(kt == 0), stop=(kt == 7))
                p.s.activation(lgT[:, :], ps[:, :], AF.Sigmoid, bias=bqk[:, 16 + og:16 + og + 1])
                p.s.activation(lgT[:, :], lgT[:, :], AF.Ln)
                for cc in range(tpp // CH):
                    p.v.tensor_tensor_scan(Lb[:, og * tpp + cc * CH:og * tpp + (cc + 1) * CH],
                                           onesc[:, :], lgT[:, cc * CH:(cc + 1) * CH],
                                           0.0, OP.mult, OP.add)
            for k in range(ntt):
                for oc in range(2):
                    ps = B.psmm[2 + oc]
                    for kt in range(8):
                        p.t.matmul(ps[:, :], B.xlnT[:, kt * tpp + k * 128:kt * tpp + (k + 1) * 128],
                                   wvrc[:, kt * 1024 + oc * 512:kt * 1024 + (oc + 1) * 512],
                                   start=(kt == 0), stop=(kt == 7))
                    p.v.tensor_tensor(vtm[:, k * 1024 + oc * 512:k * 1024 + (oc + 1) * 512],
                                      ps[:, :], bvb[:, oc * 512:(oc + 1) * 512], OP.add)
                p.g.dma_start(out=vtm_r[:, pp * ntt + k, :], in_=vtm[:, k * 1024:(k + 1) * 1024])
            p.g.dma_start(out=qT_o.rearrange("p (h t) -> p h t", h=8)[:, :, pp * tpp:(pp + 1) * tpp],
                          in_=qT.rearrange("p (h t) -> p h t", h=8))
            p.g.dma_start(out=kT_o.rearrange("p (h t) -> p h t", h=8)[:, :, pp * tpp:(pp + 1) * tpp],
                          in_=kT.rearrange("p (h t) -> p h t", h=8))
            p.g.dma_start(out=L_o.rearrange("p (h t) -> p h t", h=8)[:, :, pp * tpp:(pp + 1) * tpp],
                          in_=Lb.rearrange("p (h t) -> p h t", h=8))
            for cc in range(tpp // CH):
                Lc = [Lb[:, kb * tpp + cc * CH:kb * tpp + (cc + 1) * CH] for kb in range(8)]
                kTc = [kT[:, kb * tpp + cc * CH:kb * tpp + (cc + 1) * CH] for kb in range(8)]
                vtmc = vtm[:, cc * 1024:(cc + 1) * 1024]
                emit_gla_state_update(p, B, Ssb, kTc, Lc, vtmc, kfac, khf, khtm, gam,
                                      (B.psmm[2], B.psmm[3]))
                for kb in range(8):
                    p.v.tensor_tensor(logG[:, kb:kb + 1], logG[:, kb:kb + 1],
                                      Lc[kb][:, CH - 1:CH], OP.add)

        p.g.dma_start(out=S_o[:, :], in_=Ssb[:, :])
        p.g.dma_start(out=lg_o[:, :], in_=logG[:, :])
        with nc.Block() as block:
            p.emit(block, stack)
    return nc


def build_l2a_out(tok=2048, tpp=512):
    nc = bass.Bass("TRN2", target_bir_lowering=False)
    npass, ntt = tok // tpp, tpp // 128
    assert tpp == 512

    x_in = nc.dram_tensor("x_in", [tok, 1024], F32, kind="ExternalInput").ap()
    qT_i = nc.dram_tensor("qT_i", [128, 8 * tok], BF16, kind="ExternalInput").ap()
    kT_i = nc.dram_tensor("kT_i", [128, 8 * tok], BF16, kind="ExternalInput").ap()
    vtm_i = nc.dram_tensor("vtm_i", [tok, 1024], BF16, kind="ExternalInput").ap()
    L_i = nc.dram_tensor("L_i", [128, 8 * tok], F32, kind="ExternalInput").ap()
    S_all = nc.dram_tensor("S_all", [8, 128, 8 * 1024], F32, kind="ExternalInput").ap()
    lg_all = nc.dram_tensor("lg_all", [8, 128, 8], F32, kind="ExternalInput").ap()
    pmask = nc.dram_tensor("pmask", [128, 8], F32, kind="ExternalInput").ap()
    m2 = nc.dram_tensor("m2", [128, 64], F32, kind="ExternalInput").ap()
    wo = nc.dram_tensor("wo", [128, 8192], F32, kind="ExternalInput").ap()
    borow = nc.dram_tensor("borow", [128, 8], F32, kind="ExternalInput").ap()
    x_out = nc.dram_tensor("x_out", [tok, 1024], F32, kind="ExternalOutput").ap()

    xr_ = x_in.rearrange("(n p) d -> p n d", p=128)
    xro = x_out.rearrange("(n p) d -> p n d", p=128)
    vtm_r = vtm_i.rearrange("(n p) d -> p n d", p=128)

    with ExitStack() as stack:
        sb = lambda n, sh, dt: stack.enter_context(nc.sbuf_tensor(n, sh, dt))
        B = alloc_common(nc, stack, ntt, tpp)
        woc = sb("woc", [128, 8192], BF16)
        bo = sb("bo", [128, 8], F32)
        Ssb = sb("Ssb", [128, 8 * 1024], F32)
        Sbf = sb("Sbf", [128, 8 * 1024], BF16)
        Stmp = sb("Stmp", [128, 1024], F32)
        lgtmp = sb("lgtmp", [128, 8], F32)
        wr_ = sb("wr_", [128, 8], F32)
        msk = sb("msk", [128, 8], F32)
        m2c = sb("m2c", [128, 64], F32)
        qTl = sb("qTl", [128, 8 * tpp], BF16)
        kTl = sb("kTl", [128, 8 * tpp], BF16)
        Ll = sb("Ll", [128, 8 * tpp], F32)
        vtm = sb("vtm", [128, ntt * 1024], BF16)
        qt_ = sb("qt_", [128, 8 * CH], BF16)
        efac = sb("efac", [128, CH], F32)
        qs_ = sb("qs_", [128, 8 * QS], BF16)
        kt_ = sb("kt_", [128, CH], BF16)
        sTb = sb("sTb", [128, QS], BF16)
        maskall = sb("maskall", [128, (CH // QS) * QS], BF16)
        kfac = sb("kfac", [128, CH], F32)
        khf = sb("khf", [128, CH], BF16)
        khtm = sb("khtm", [128, 1024], BF16)
        gam = sb("gam", [128, 8], F32)
        oT = sb("oT", [128, 8 * tpp], BF16)
        outT = sb("outT", [128, 8 * tpp], F32)

        p = Prog(nc)
        emit_identities(p, B.idb, B.idf, B.epsc)
        p.g.memset(maskall[:, :], 1.0)
        p.g.affine_select(out=maskall.rearrange("p (i q) -> p i q", q=QS),
                          in_=maskall.rearrange("p (i q) -> p i q", q=QS),
                          compare_op=OP.is_ge, fill=0.0, base=0,
                          pattern=[[QS, CH // QS], [1, QS]], channel_multiplier=-1)
        p.g.dma_start(out=woc[:, :], in_=wo[:, :])
        p.y.dma_start(out=bo[:, :], in_=borow[:, :])
        p.y.dma_start(out=msk[:, :], in_=pmask[:, :])
        p.y.dma_start(out=m2c[:, :], in_=m2[:, :])
        p.v.memset(Ssb[:, :], 0.0)
        for r in range(8):
            p.v.memset(wr_[:, :], 0.0)
            for j in range(8):
                p.g.dma_start(out=lgtmp[:, :], in_=lg_all[j])
                p.v.scalar_tensor_tensor(wr_[:, :], lgtmp[:, :], m2c[:, r * 8 + j:r * 8 + j + 1],
                                         wr_[:, :], OP.mult, OP.add)
            p.s.activation(wr_[:, :], wr_[:, :], AF.Exp)
            p.v.tensor_scalar(wr_[:, :], wr_[:, :], msk[:, r:r + 1], None, OP.mult)
            for kb in range(8):
                p.g.dma_start(out=Stmp[:, :], in_=S_all[r][:, kb * 1024:(kb + 1) * 1024])
                p.v.scalar_tensor_tensor(Ssb[:, kb * 1024:(kb + 1) * 1024], Stmp[:, :],
                                         wr_[:, kb:kb + 1], Ssb[:, kb * 1024:(kb + 1) * 1024],
                                         OP.mult, OP.add)

        for pp in range(npass):
            emit_head(p, B, xr_, pp)
            p.y.dma_start(out=qTl.rearrange("p (h t) -> p h t", h=8),
                          in_=qT_i.rearrange("p (h t) -> p h t", h=8)[:, :, pp * tpp:(pp + 1) * tpp])
            p.y.dma_start(out=kTl.rearrange("p (h t) -> p h t", h=8),
                          in_=kT_i.rearrange("p (h t) -> p h t", h=8)[:, :, pp * tpp:(pp + 1) * tpp])
            p.y.dma_start(out=Ll.rearrange("p (h t) -> p h t", h=8),
                          in_=L_i.rearrange("p (h t) -> p h t", h=8)[:, :, pp * tpp:(pp + 1) * tpp])
            p.y.dma_start(out=vtm.rearrange("p (n d) -> p n d", d=1024),
                          in_=vtm_r[:, pp * ntt:(pp + 1) * ntt, :])
            for cc in range(tpp // CH):
                Lc = [Ll[:, kb * tpp + cc * CH:kb * tpp + (cc + 1) * CH] for kb in range(8)]
                kTc = [kTl[:, kb * tpp + cc * CH:kb * tpp + (cc + 1) * CH] for kb in range(8)]
                qTc = [qTl[:, kb * tpp + cc * CH:kb * tpp + (cc + 1) * CH] for kb in range(8)]
                vtmc = vtm[:, cc * 1024:(cc + 1) * 1024]
                for kb in range(8):
                    p.v.tensor_copy(Sbf[:, kb * 1024:(kb + 1) * 1024],
                                    Ssb[:, kb * 1024:(kb + 1) * 1024])
                for kb in range(8):
                    p.s.activation(efac[:, :], Lc[kb], AF.Exp)
                    p.v.tensor_tensor(qt_[:, kb * CH:(kb + 1) * CH], qTc[kb], efac[:, :], OP.mult)
                for half in range(2):
                    po = B.psmm[half]
                    for vb4 in range(4):
                        vb = half * 4 + vb4
                        for kb in range(8):
                            p.t.matmul(po[:, vb4 * 128:(vb4 + 1) * 128],
                                       Sbf[:, kb * 1024 + vb * 128:kb * 1024 + (vb + 1) * 128],
                                       qt_[:, kb * CH:(kb + 1) * CH],
                                       start=(vb4 == 0 and kb == 0), stop=False)
                for i in range(CH // QS):
                    i0 = i * QS
                    for kb in range(8):
                        p.v.tensor_scalar(efac[:, 0:QS], Lc[kb][:, i0:i0 + QS],
                                          Lc[kb][:, i0:i0 + 1], None, OP.subtract)
                        p.s.activation(efac[:, 0:QS], efac[:, 0:QS], AF.Exp)
                        p.v.tensor_tensor(qs_[:, kb * QS:(kb + 1) * QS],
                                          qTc[kb][:, i0:i0 + QS], efac[:, 0:QS], OP.mult)
                    ns = i0 + QS
                    pss = B.pstf
                    for kb in range(8):
                        p.v.tensor_scalar(kfac[:, 0:ns], Lc[kb][:, 0:ns],
                                          Lc[kb][:, i0:i0 + 1], None, OP.subtract)
                        p.s.activation(kfac[:, 0:ns], kfac[:, 0:ns], AF.Exp, scale=-1.0)
                        p.v.tensor_tensor(kt_[:, 0:ns], kTc[kb][:, 0:ns], kfac[:, 0:ns], OP.mult)
                        p.t.matmul(pss[0:ns, 0:QS], kt_[:, 0:ns], qs_[:, kb * QS:(kb + 1) * QS],
                                   start=(kb == 0), stop=(kb == 7))
                    p.v.tensor_tensor(sTb[0:ns, :], pss[0:ns, 0:QS],
                                      maskall[0:ns, i * QS:(i + 1) * QS], OP.mult)
                    for half in range(2):
                        po = B.psmm[half]
                        for vb4 in range(4):
                            vb = half * 4 + vb4
                            stop = (i == CH // QS - 1 and vb4 == 3)
                            p.t.matmul(po[:, vb4 * 128 + i * QS:vb4 * 128 + (i + 1) * QS],
                                       vtmc[0:ns, vb * 128:(vb + 1) * 128], sTb[0:ns, :],
                                       start=False, stop=stop)
                for half in range(2):
                    po = B.psmm[half]
                    for vb4 in range(4):
                        vb = half * 4 + vb4
                        p.v.tensor_copy(oT[:, vb * tpp + cc * CH:vb * tpp + (cc + 1) * CH],
                                        po[:, vb4 * 128:(vb4 + 1) * 128])
                emit_gla_state_update(p, B, Ssb, kTc, Lc, vtmc, kfac, khf, khtm, gam,
                                      (B.psmm[2], B.psmm[3]))
            for og in range(8):
                ps = B.psmm[2 + og % 2]
                for kt in range(8):
                    p.t.matmul(ps[:, :], woc[:, (og * 8 + kt) * 128:(og * 8 + kt + 1) * 128],
                               oT[:, kt * tpp:(kt + 1) * tpp],
                               start=(kt == 0), stop=(kt == 7))
                p.s.activation(outT[:, og * tpp:(og + 1) * tpp], ps[:, :],
                               AF.Identity, bias=bo[:, og:og + 1])
            emit_tail(p, B, outT, xro, pp)

        with nc.Block() as block:
            p.emit(block, stack)
    return nc


def l2_inputs_A(pm):
    return {
        "wq": _lhsT_resident(pm["Wq"].T), "wk": _lhsT_resident(pm["Wk"].T),
        "wg": _lhsT_resident(pm["Wg"].T), "wvr": _rhs_resident(pm["Wv"].T),
        "bqkg": np.concatenate([pm["bq"].reshape(8, 128).T, pm["bk"].reshape(8, 128).T,
                                pm["bg"].reshape(8, 128).T], axis=1).astype(np.float32),
        "bvrow": pm["bv"].reshape(1, 1024).astype(np.float32),
    }


def run_l2_attn(x_full, pm, tpp=512):
    xf = np.ascontiguousarray(x_full.reshape(16384, 1024).astype(np.float32))
    tok = xf.shape[0] // NCORES
    ncA = build_l2a_qkv(tok=tok, tpp=tpp)
    baseA = l2_inputs_A(pm)
    in_maps = []
    for c in range(NCORES):
        m = dict(baseA)
        m["x_in"] = xf[c * tok:(c + 1) * tok]
        in_maps.append(m)
    resA = run_bass_kernel_spmd(ncA, in_maps, core_ids=list(range(NCORES)))
    S_all = np.stack([resA.results[c]["S_o"] for c in range(NCORES)], axis=0)
    lg_all = np.stack([resA.results[c]["lg_o"] for c in range(NCORES)], axis=0)
    ncB = build_l2a_out(tok=tok, tpp=tpp)
    baseB = {"wo": _lhsT_resident(pm["Wo"].T), "S_all": S_all, "lg_all": lg_all,
             "borow": pm["bo"].reshape(8, 128).T.astype(np.float32).copy()}
    in_maps = []
    for c in range(NCORES):
        m = dict(baseB)
        m["x_in"] = xf[c * tok:(c + 1) * tok]
        for nm in ("qT_o", "kT_o", "vtm_o", "L_o"):
            m[nm.replace("_o", "_i")] = resA.results[c][nm]
        pmask = np.zeros((128, 8), np.float32)
        m2 = np.zeros((128, 64), np.float32)
        for r in range((c // 4) * 4, c):
            pmask[:, r] = 1.0
            for j in range(r + 1, c):
                m2[:, r * 8 + j] = 1.0
        m["pmask"] = pmask
        m["m2"] = m2
        in_maps.append(m)
    resB = run_bass_kernel_spmd(ncB, in_maps, core_ids=list(range(NCORES)))
    out = np.concatenate([resB.results[c]["x_out"] for c in range(NCORES)], axis=0)
    return out.reshape(2, 8192, 1024)




def build_l3a_qkv(tok=2048, tpp=512):
    nc = bass.Bass("TRN2", target_bir_lowering=False)
    npass, ntt = tok // tpp, tpp // 128
    assert tpp == 512

    x_in = nc.dram_tensor("x_in", [tok, 1024], F32, kind="ExternalInput").ap()
    wq = nc.dram_tensor("wq", [128, 8192], F32, kind="ExternalInput").ap()
    wk = nc.dram_tensor("wk", [128, 8192], F32, kind="ExternalInput").ap()
    wvr = nc.dram_tensor("wvr", [128, 8192], F32, kind="ExternalInput").ap()
    qT_o = nc.dram_tensor("qT_o", [128, 8 * tok], BF16, kind="ExternalOutput").ap()
    kT_o = nc.dram_tensor("kT_o", [128, 8 * tok], BF16, kind="ExternalOutput").ap()
    vtm_o = nc.dram_tensor("vtm_o", [tok, 1024], BF16, kind="ExternalOutput").ap()

    xr_ = x_in.rearrange("(n p) d -> p n d", p=128)
    vtm_r = vtm_o.rearrange("(n p) d -> p n d", p=128)

    with ExitStack() as stack:
        sb = lambda n, sh, dt: stack.enter_context(nc.sbuf_tensor(n, sh, dt))
        B = alloc_common(nc, stack, ntt, tpp)
        wqc = sb("wqc", [128, 8192], BF16)
        wkc = sb("wkc", [128, 8192], BF16)
        wvrc = sb("wvrc", [128, 8192], BF16)
        qT = sb("qT", [128, 8 * tpp], BF16)
        kT = sb("kT", [128, 8 * tpp], BF16)
        vtm = sb("vtm", [128, ntt * 1024], BF16)

        p = Prog(nc)
        emit_identities(p, B.idb, B.idf, B.epsc)
        for w_ap, wt in ((wq, wqc), (wk, wkc), (wvr, wvrc)):
            p.g.dma_start(out=wt[:, :], in_=w_ap[:, :])
        for pp in range(npass):
            emit_head(p, B, xr_, pp)
            for wt, dst, hbm in ((wqc, qT, qT_o), (wkc, kT, kT_o)):
                for og in range(8):
                    ps = B.psmm[og % 2]
                    for kt in range(8):
                        p.t.matmul(ps[:, :], wt[:, (og * 8 + kt) * 128:(og * 8 + kt + 1) * 128],
                                   B.xlnT[:, kt * tpp:(kt + 1) * tpp],
                                   start=(kt == 0), stop=(kt == 7))
                    p.v.tensor_copy(dst[:, og * tpp:(og + 1) * tpp], ps[:, :])
                p.g.dma_start(out=hbm.rearrange("p (h t) -> p h t", h=8)[:, :, pp * tpp:(pp + 1) * tpp],
                              in_=dst.rearrange("p (h t) -> p h t", h=8))
            for k in range(ntt):
                for oc in range(2):
                    ps = B.psmm[2 + oc]
                    for kt in range(8):
                        p.t.matmul(ps[:, :], B.xlnT[:, kt * tpp + k * 128:kt * tpp + (k + 1) * 128],
                                   wvrc[:, kt * 1024 + oc * 512:kt * 1024 + (oc + 1) * 512],
                                   start=(kt == 0), stop=(kt == 7))
                    p.v.tensor_copy(vtm[:, k * 1024 + oc * 512:k * 1024 + (oc + 1) * 512], ps[:, :])
                p.g.dma_start(out=vtm_r[:, pp * ntt + k, :], in_=vtm[:, k * 1024:(k + 1) * 1024])
        with nc.Block() as block:
            p.emit(block, stack)
    return nc


L3_SLOTS = (2048, 4096, 6144, 8192)


def build_l3a_out(tok=2048, tpp=512, slots=L3_SLOTS):
    nc = bass.Bass("TRN2", target_bir_lowering=False)
    npass, ntt = tok // tpp, tpp // 128
    KVQ = 1024
    KB = 128
    KVTOT = sum(slots)
    assert tpp == 512 and ntt == 4 and npass == len(slots)

    x_in = nc.dram_tensor("x_in", [tok, 1024], F32, kind="ExternalInput").ap()
    qT_i = nc.dram_tensor("qT_i", [128, 16 * tok], BF16, kind="ExternalInput").ap()
    kT_f = nc.dram_tensor("kT_f", [128, 8 * KVTOT], BF16, kind="ExternalInput").ap()
    vtm_f = nc.dram_tensor("vtm_f", [KVTOT, 1040], BF16, kind="ExternalInput").ap()
    wo = nc.dram_tensor("wo", [128, 8192], F32, kind="ExternalInput").ap()
    x_out = nc.dram_tensor("x_out", [tok, 1024], F32, kind="ExternalOutput").ap()

    xr_ = x_in.rearrange("(n p) d -> p n d", p=128)
    xro = x_out.rearrange("(n p) d -> p n d", p=128)
    vtm_fr = vtm_f.rearrange("(n p) d -> p n d", p=128)

    with ExitStack() as stack:
        sb = lambda n, sh, dt: stack.enter_context(nc.sbuf_tensor(n, sh, dt))
        B = alloc_common(nc, stack, ntt, tpp, ln_bufs=False, psmm_n=2, psum_tp=False)
        pss2 = [stack.enter_context(nc.psum_tensor(f"pss2_{i}", [128, 1024], F32))
                for i in range(3)]
        B.pstf = pss2[0][:, 0:512]
        woc = sb("woc", [128, 8192], BF16)
        qTp = sb("qTp", [128, 16 * tpp], BF16)
        kq = sb("kq", [128, 2 * 8 * KVQ], BF16)
        vq = sb("vq", [128, 2 * (KVQ // 128) * 1040], BF16)
        cmask = sb("cmask", [128, 4 * 512], BF16)
        pexp = sb("pexp", [128, 3 * 1024], BF16)
        oacc = sb("oacc", [128, 16 * ntt * 65], F32)
        lrec = sb("lrec", [128, 64], F32)
        onb = sb("onb", [128, 4 * 64], F32)
        oT = sb("oT", [128, 8 * tpp], BF16)
        outT = sb("outT", [128, 8 * tpp], F32)
        neg8 = sb("neg8", [128, 1], F32)

        p = Prog(nc)
        emit_identities(p, B.idb, B.idf, B.epsc)
        p.g.memset(neg8[:, :], -8.0)
        p.g.memset(cmask[:, :], 1.0)
        p.g.affine_select(out=cmask.rearrange("p (r c) -> p r c", r=4),
                          in_=cmask.rearrange("p (r c) -> p r c", r=4),
                          compare_op=OP.is_ge, fill=0.0, base=0,
                          pattern=[[-128, 4], [1, 512]], channel_multiplier=-1)
        p.g.dma_start(out=woc[:, :], in_=wo[:, :])

        gpair = [0]
        for pp in range(npass):
            S = slots[pp]
            off = sum(slots[:pp])
            p.y.dma_start(out=B.xt.rearrange("p (n d) -> p n d", d=1024),
                          in_=xr_[:, pp * ntt:(pp + 1) * ntt, :])
            p.y.dma_start(out=qTp.rearrange("p (h t) -> p h t", h=16),
                          in_=qT_i.rearrange("p (h t) -> p h t", h=16)[:, :, pp * tpp:(pp + 1) * tpp])
            qpos0 = S - tpp
            frontier = S
            nq = S // KVQ
            for qq in range(nq):
                kb0 = qq * KVQ
                nkc = KVQ // KB
                buf = qq % 2
                kqb = kq.rearrange("p (b h t) -> p b h t", b=2, h=8)
                p.y.dma_start(out=kqb[:, buf, :, :],
                              in_=kT_f.rearrange("p (h t) -> p h t", h=8)[:, :, off + kb0:off + kb0 + KVQ])
                p.y.dma_start(out=vq.rearrange("p (b n d) -> p b n d", b=2, d=1040)[:, buf, :, :],
                              in_=vtm_fr[:, (off + kb0) // 128:(off + kb0) // 128 + nkc, :])
                for h in range(16):
                    og = h // 2
                    po = B.psmm[h % 2]
                    qs = qTp[:, h * tpp:(h + 1) * tpp]
                    npair = nkc // 2
                    base = gpair[0]
                    gpair[0] += npair

                    def score_pair(j):
                        for side in range(2):
                            c = 2 * j + side
                            ks = kq[:, (buf * 8 + og) * KVQ + c * KB:(buf * 8 + og) * KVQ + (c + 1) * KB]
                            p.t.matmul(pss2[(base + j) % 3][:, side * 512:(side + 1) * 512],
                                       ks, qs, start=True, stop=True)

                    score_pair(0)
                    score_pair(1)
                    iomm = 0
                    for j in range(npair):
                        pexj = pexp[:, ((base + j) % 3) * 1024:((base + j) % 3 + 1) * 1024]
                        p.s.activation(pexj, pss2[(base + j) % 3][:, :], AF.Exp,
                                       scale=0.125, bias=neg8[:, 0:1])
                        if j + 2 < npair:
                            score_pair(j + 2)
                        for side in range(2):
                            c = 2 * j + side
                            kidx = kb0 + c * KB
                            pex = pexj[:, side * 512:(side + 1) * 512]
                            r = (kidx - qpos0) // 128
                            if 0 <= r <= 3:
                                p.v.tensor_tensor(pex, pex,
                                                  cmask[:, r * 512:(r + 1) * 512], OP.mult)
                            vs = vq[:, (buf * (KVQ // 128) + c) * 1040 + h * 65:
                                    (buf * (KVQ // 128) + c) * 1040 + (h + 1) * 65]
                            for k in range(ntt):
                                if kidx < qpos0 + (k + 1) * 128:
                                    p.t.matmul(po[:, k * 65:(k + 1) * 65],
                                               pex[:, k * 128:(k + 1) * 128], vs,
                                               start=(c == 0 and k == 0),
                                               stop=(c == nkc - 1 and k == ntt - 1))
                    sl = oacc[:, h * ntt * 65:(h * ntt + 4) * 65]
                    if qq == 0:
                        p.v.tensor_copy(sl, po[:, 0:4 * 65])
                    else:
                        p.v.tensor_tensor(sl, sl, po[:, 0:4 * 65], OP.add)
            p.v.reciprocal(lrec.rearrange("p (j c) -> p j c", c=1),
                           oacc.rearrange("p (j c) -> p j c", c=65)[:, :, 64:65])
            for og in range(8):
                for half in range(2):
                    h = 2 * og + half
                    for k in range(ntt):
                        j = h * ntt + k
                        ob = onb[:, k * 64:(k + 1) * 64]
                        p.v.tensor_scalar(ob, oacc[:, j * 65:j * 65 + 64],
                                          lrec[:, j:j + 1], None, OP.mult)
                        p.t.transpose(B.pstf[0:64, k * 128:(k + 1) * 128], ob, B.idf[:, :])
                    p.v.tensor_copy(oT[half * 64:half * 64 + 64, og * tpp:(og + 1) * tpp],
                                    B.pstf[0:64, :])
            for og in range(8):
                ps = B.psmm[og % 2]
                for kt in range(8):
                    p.t.matmul(ps[:, :], woc[:, (og * 8 + kt) * 128:(og * 8 + kt + 1) * 128],
                               oT[:, kt * tpp:(kt + 1) * tpp],
                               start=(kt == 0), stop=(kt == 7))
                p.v.tensor_copy(outT[:, og * tpp:(og + 1) * tpp], ps[:, :])
            emit_tail(p, B, outT, xro, pp)

        with nc.Block() as block:
            p.emit(block, stack)
    return nc


def run_l3_attn(x_full, pm, tpp=512):
    xf = np.ascontiguousarray(x_full.reshape(16384, 1024).astype(np.float32))
    tok = xf.shape[0] // NCORES
    seq = 8192
    ncA = build_l3a_qkv(tok=tok, tpp=tpp)
    WQ = pm["W_Q"].reshape(1024, 1024)
    WK = pm["W_K"].reshape(1024, 1024)
    WV = pm["W_V"].reshape(1024, 1024)
    baseA = {"wq": _lhsT_resident(WQ), "wk": _lhsT_resident(WK), "wvr": _rhs_resident(WV)}
    in_maps = []
    for c in range(NCORES):
        m = dict(baseA)
        m["x_in"] = xf[c * tok:(c + 1) * tok]
        in_maps.append(m)
    resA = run_bass_kernel_spmd(ncA, in_maps, core_ids=list(range(NCORES)))
    slots = L3_SLOTS
    KVTOT = sum(slots)
    ncB = build_l3a_out(tok=tok, tpp=tpp, slots=slots)
    baseB = {"wo": _lhsT_resident(pm["W_O"])}
    kT_b = {}
    v_b = {}
    qT_b = {}
    for b in range(2):
        kT_b[b] = np.concatenate([resA.results[b * 4 + i]["kT_o"].reshape(128, 8, tok)
                                  for i in range(4)], axis=2)
        qT_b[b] = np.concatenate([resA.results[b * 4 + i]["qT_o"].reshape(128, 8, tok)
                                  for i in range(4)], axis=2)
        vr = np.concatenate([resA.results[b * 4 + i]["vtm_o"] for i in range(4)],
                            axis=0).reshape(seq, 16, 64)
        v_b[b] = np.concatenate(
            [np.concatenate([vr[:, i, :], np.ones((seq, 1), vr.dtype)], axis=1)
             for i in range(16)], axis=1)
    in_maps = []
    for c in range(NCORES):
        b, posq = c // 4, (c % 4)
        blocks = [posq, 7 - posq, 8 + posq, 15 - posq]
        m = dict(baseB)
        xb = xf[b * seq:(b + 1) * seq]
        m["x_in"] = np.ascontiguousarray(
            np.concatenate([xb[j * tpp:(j + 1) * tpp] for j in blocks], axis=0))
        qg = np.concatenate([qT_b[b][:, :, j * tpp:(j + 1) * tpp] for j in blocks],
                            axis=2)
        qz = np.zeros((128, 16, tok), dtype=qg.dtype)
        for h in range(16):
            rows = slice(0, 64) if h % 2 == 0 else slice(64, 128)
            qz[rows, h] = qg[rows, h // 2]
        m["qT_i"] = np.ascontiguousarray(qz.reshape(128, 16 * tok))
        kT = np.zeros((128, 8, KVTOT), dtype=kT_b[b].dtype)
        v = np.zeros((KVTOT, 1040), dtype=v_b[b].dtype)
        off = 0
        for pp, j in enumerate(blocks):
            S = slots[pp]
            L = (j + 1) * tpp
            kT[:, :, off + S - L:off + S] = kT_b[b][:, :, :L]
            v[off + S - L:off + S] = v_b[b][:L]
            off += S
        m["kT_f"] = np.ascontiguousarray(kT.reshape(128, 8 * KVTOT))
        m["vtm_f"] = v
        in_maps.append(m)
    resB = run_bass_kernel_spmd(ncB, in_maps, core_ids=list(range(NCORES)))
    out = np.empty((2, seq, 1024), np.float32)
    for c in range(NCORES):
        b, posq = c // 4, (c % 4)
        blocks = [posq, 7 - posq, 8 + posq, 15 - posq]
        xo = resB.results[c]["x_out"]
        for pp, j in enumerate(blocks):
            out[b, j * tpp:(j + 1) * tpp] = xo[pp * tpp:(pp + 1) * tpp]
    return out




def build_m1(tok=2048):
    nc = bass.Bass("TRN2", target_bir_lowering=False)
    ntt = tok // 128
    x_in = nc.dram_tensor("x_in", [tok, 1024], F32, kind="ExternalInput").ap()
    xln_o = nc.dram_tensor("xln_o", [tok, 1024], BF16, kind="ExternalOutput").ap()
    xr_ = x_in.rearrange("(n p) d -> p n d", p=128)
    xlo = xln_o.rearrange("(n p) d -> p n d", p=128)
    with ExitStack() as stack:
        sb = lambda n, sh, dt: stack.enter_context(nc.sbuf_tensor(n, sh, dt))
        xt = sb("xt", [128, 4 * 1024], F32)
        xl = sb("xl", [128, 4 * 1024], BF16)
        stat = sb("stat", [128, 64], F32)
        sq = sb("sq", [128, 1024], F32)
        epsc = sb("epsc", [128, 1], F32)
        p = Prog(nc)
        p.g.memset(epsc[:, :], LNEPS)
        for pp in range(tok // 512):
            p.y.dma_start(out=xt.rearrange("p (n d) -> p n d", d=1024),
                          in_=xr_[:, pp * 4:(pp + 1) * 4, :])
            xts = [xt[:, k * 1024:(k + 1) * 1024] for k in range(4)]
            xls = [xl[:, k * 1024:(k + 1) * 1024] for k in range(4)]
            emit_ln(p, xts, xls, stat, sq, epsc)
            for k in range(4):
                p.g.dma_start(out=xlo[:, pp * 4 + k, :], in_=xl[:, k * 1024:(k + 1) * 1024])
        with nc.Block() as block:
            p.emit(block, stack)
    return nc


def build_m2(T=16384, tc=512):
    nc = bass.Bass("TRN2", target_bir_lowering=False)
    nch = T // tc
    BCH = 8192 // tc
    win = nc.dram_tensor("win", [2, 128, 4096], F32, kind="ExternalInput").ap()
    xlnT = nc.dram_tensor("xlnT", [1024, T], BF16, kind="ExternalInput").ap()
    xlnTr = nc.dram_tensor("xlnTr", [1024, T], BF16, kind="ExternalInput").ap()
    convw = nc.dram_tensor("convw", [128, 16], F32, kind="ExternalInput").ap()
    convb = nc.dram_tensor("convb", [128, 4], F32, kind="ExternalInput").ap()
    wx = nc.dram_tensor("wx", [2, 128, 192], F32, kind="ExternalInput").ap()
    u_o = nc.dram_tensor("u_o", [2, 256, T], BF16, kind="ExternalOutput").ap()
    sz_o = nc.dram_tensor("sz_o", [2, 256, T], BF16, kind="ExternalOutput").ap()
    px_o = nc.dram_tensor("px_o", [2, 96, T], F32, kind="ExternalOutput").ap()

    with ExitStack() as stack:
        sb = lambda n, sh, dt: stack.enter_context(nc.sbuf_tensor(n, sh, dt))
        psal = lambda n, sh, dt: stack.enter_context(nc.psum_tensor(n, sh, dt))
        wc = [sb(f"wc{s}", [128, 4096], BF16) for s in range(2)]
        wxc = [sb(f"wxc{s}", [128, 192], BF16) for s in range(2)]
        cwc = sb("cwc", [128, 16], F32)
        cbc = sb("cbc", [128, 4], F32)
        xb = sb("xb", [128, 8 * tc], BF16)
        xs = sb("xs", [128, 2 * 2 * (tc + 3)], F32)
        cacc = sb("cacc", [128, 2 * tc], F32)
        ub = sb("ub", [128, 2 * tc], BF16)
        zb = sb("zb", [128, 2 * tc], BF16)
        pxs = sb("pxs", [128, tc], F32)
        ps = [psal(f"ps{i}", [128, 512], F32) for i in range(4)]
        psx = psal("psx", [128, 512], F32)

        p = Prog(nc)
        for s in range(2):
            p.g.dma_start(out=wc[s][:, :], in_=win[s])
            p.g.dma_start(out=wxc[s][:, :], in_=wx[s])
        p.y.dma_start(out=cwc[:, :], in_=convw[:, :])
        p.y.dma_start(out=cbc[:, :], in_=convb[:, :])

        for s in range(2):
            xsr = (xlnT if s == 0 else xlnTr).rearrange("(kt p) t -> p kt t", p=128)
            for c in range(nch):
                p.y.dma_start(out=xb.rearrange("p (kt t) -> p kt t", kt=8),
                              in_=xsr[:, :, c * tc:(c + 1) * tc])
                for og in range(4):
                    for kt in range(8):
                        p.t.matmul(ps[og][:, :], wc[s][:, (og * 8 + kt) * 128:(og * 8 + kt + 1) * 128],
                                   xb[:, kt * tc:(kt + 1) * tc],
                                   start=(kt == 0), stop=(kt == 7))
                for g in range(2):
                    hbuf = xs[:, (s * 2 + g) * (tc + 3):(s * 2 + g) * (tc + 3) + tc + 3]
                    if c % BCH == 0:
                        p.v.memset(hbuf[:, 0:3], 0.0)
                    else:
                        p.v.tensor_copy(hbuf[:, 0:3], hbuf[:, tc:tc + 3])
                    p.v.tensor_copy(hbuf[:, 3:3 + tc], ps[g][:, :])
                    ca = cacc[:, g * tc:(g + 1) * tc]
                    p.v.tensor_scalar(ca, hbuf[:, 0:tc], cwc[:, (s * 2 + g) * 4:(s * 2 + g) * 4 + 1],
                                      None, OP.mult)
                    for k in range(1, 4):
                        p.v.scalar_tensor_tensor(ca, hbuf[:, k:k + tc],
                                                 cwc[:, (s * 2 + g) * 4 + k:(s * 2 + g) * 4 + k + 1],
                                                 ca, OP.mult, OP.add)
                    p.s.activation(ub[:, g * tc:(g + 1) * tc], ca, AF.Silu,
                                   bias=cbc[:, s * 2 + g:s * 2 + g + 1])
                    p.s.activation(zb[:, g * tc:(g + 1) * tc], ps[2 + g][:, :], AF.Silu)
                p.g.dma_start(out=u_o[s].rearrange("(g p) t -> p g t", p=128)[:, :, c * tc:(c + 1) * tc],
                              in_=ub.rearrange("p (g t) -> p g t", g=2))
                p.g.dma_start(out=sz_o[s].rearrange("(g p) t -> p g t", p=128)[:, :, c * tc:(c + 1) * tc],
                              in_=zb.rearrange("p (g t) -> p g t", g=2))
                for g in range(2):
                    p.t.matmul(psx[0:96, 0:tc], wxc[s][:, g * 96:(g + 1) * 96],
                               ub[:, g * tc:(g + 1) * tc], start=(g == 0), stop=(g == 1))
                p.v.tensor_copy(pxs[0:96, :], psx[0:96, 0:tc])
                p.g.dma_start(out=px_o[s][:, c * tc:(c + 1) * tc], in_=pxs[0:96, :])

        with nc.Block() as block:
            p.emit(block, stack)
    return nc


def build_m3(T=16384, tc=1024):
    nc = bass.Bass("TRN2", target_bir_lowering=False)
    nch = T // tc
    BCH = 8192 // tc
    NH = tc // 512
    u_i = nc.dram_tensor("u_i", [2, 256, T], BF16, kind="ExternalInput").ap()
    sz_i = nc.dram_tensor("sz_i", [2, 256, T], BF16, kind="ExternalInput").ap()
    px_all = nc.dram_tensor("px_all", [8, 2, 96, T], F32, kind="ExternalInput").ap()
    wdt = nc.dram_tensor("wdt", [2, 64, 256], F32, kind="ExternalInput").ap()
    bdt = nc.dram_tensor("bdt", [128, 4], F32, kind="ExternalInput").ap()
    alog = nc.dram_tensor("alog", [128, 64], F32, kind="ExternalInput").ap()
    dvec = nc.dram_tensor("dvec", [128, 4], F32, kind="ExternalInput").ap()
    y_o = nc.dram_tensor("y_o", [2, 256, T], BF16, kind="ExternalOutput").ap()
    bcscr = [nc.dram_tensor(f"bcscr{i}", [32, tc], BF16) for i in range(2)]

    with ExitStack() as stack:
        sb = lambda n, sh, dt: stack.enter_context(nc.sbuf_tensor(n, sh, dt))
        psal = lambda n, sh, dt: stack.enter_context(nc.psum_tensor(n, sh, dt))
        wdtc = [sb(f"wdtc{s}", [64, 256], F32) for s in range(2)]
        bdtc = sb("bdtc", [128, 4], F32)
        acol = sb("acol", [128, 64], F32)
        dcol = sb("dcol", [128, 4], F32)
        pxsum = sb("pxsum", [128, tc], F32)
        pxtmp = sb("pxtmp", [128, 4 * tc], F32)
        bc32 = sb("bc32", [32, tc], BF16)
        dtb = sb("dtb", [128, 2 * tc], F32)
        dtt = sb("dtt", [128, tc], F32)
        Blo = sb("Blo", [128, 8 * tc], BF16)
        Bhi = sb("Bhi", [128, 8 * tc], BF16)
        Clo = sb("Clo", [128, 8 * tc], BF16)
        Chi = sb("Chi", [128, 8 * tc], BF16)
        ubb = sb("ubb", [128, 2 * tc], BF16)
        szb = sb("szb", [128, 2 * tc], BF16)
        dub = sb("dub", [128, 2 * tc], F32)
        dab = sb("dab", [128, 2 * tc], F32)
        bbb = sb("bbb", [128, 2 * tc], F32)
        hbb = sb("hbb", [128, 3 * tc], F32)
        hcb = sb("hcb", [128, 4 * tc], F32)
        yac = sb("yac", [128, 2 * tc], F32)
        yacg = sb("yacg", [128, 2 * tc], F32)
        ybf = sb("ybf", [128, 2 * tc], BF16)
        carry = sb("carry", [128, 64], F32)
        psA = psal("psA", [128, 512], F32)

        p = Prog(nc)
        for s in range(2):
            p.y.dma_start(out=wdtc[s][:, :], in_=wdt[s])
        p.y.dma_start(out=bdtc[:, :], in_=bdt[:, :])
        p.y.dma_start(out=dcol[:, :], in_=dvec[:, :])
        p.y.dma_start(out=acol[:, :], in_=alog[:, :])
        p.s.activation(acol[:, :], acol[:, :], AF.Exp)
        p.v.tensor_scalar(acol[:, :], acol[:, :], -1.0, None, OP.mult)

        def Bn(n):
            return (Blo if n < 8 else Bhi)[:, (n % 8) * tc:(n % 8 + 1) * tc]

        def Cn(n):
            return (Clo if n < 8 else Chi)[:, (n % 8) * tc:(n % 8 + 1) * tc]

        def prep(s, c):
            scr = bcscr[c % 2].ap()
            p.y.dma_start(out=pxsum[0:96, :], in_=px_all[0][s][:, c * tc:(c + 1) * tc])
            for r in range(1, 5):
                j = (r - 1) % 4
                p.y.dma_start(out=pxtmp[0:96, j * tc:(j + 1) * tc],
                              in_=px_all[r][s][:, c * tc:(c + 1) * tc])
            for r in range(1, 8):
                j = (r - 1) % 4
                p.v.tensor_tensor(pxsum[0:96, :], pxsum[0:96, :],
                                  pxtmp[0:96, j * tc:(j + 1) * tc], OP.add)
                if r + 4 < 8:
                    j2 = (r + 3) % 4
                    p.y.dma_start(out=pxtmp[0:96, j2 * tc:(j2 + 1) * tc],
                                  in_=px_all[r + 4][s][:, c * tc:(c + 1) * tc])
            p.v.tensor_copy(bc32[:, :], pxsum[64:96, :])
            p.g.dma_start(out=scr[:, :], in_=bc32[:, :])
            flat = scr.rearrange("n t -> (n t)")
            p.y.dma_start(out=Blo.rearrange("p (n t) -> p n t", n=8),
                          in_=flat[0:8 * tc].rearrange("(a n t) -> a n t", a=1, n=8).broadcast_to([128, 8, tc]))
            p.y.dma_start(out=Clo.rearrange("p (n t) -> p n t", n=8),
                          in_=flat[16 * tc:24 * tc].rearrange("(a n t) -> a n t", a=1, n=8).broadcast_to([128, 8, tc]))
            p.y.dma_start(out=Bhi.rearrange("p (n t) -> p n t", n=8),
                          in_=flat[8 * tc:16 * tc].rearrange("(a n t) -> a n t", a=1, n=8).broadcast_to([128, 8, tc]))
            p.y.dma_start(out=Chi.rearrange("p (n t) -> p n t", n=8),
                          in_=flat[24 * tc:32 * tc].rearrange("(a n t) -> a n t", a=1, n=8).broadcast_to([128, 8, tc]))
            for g in range(2):
                dgt = dtb[:, g * tc:(g + 1) * tc]
                bcol = bdtc[:, s * 2 + g:s * 2 + g + 1]
                for hh in range(NH):
                    cols = slice(hh * 512, (hh + 1) * 512)
                    p.t.matmul(psA[:, :], wdtc[s][:, g * 128:(g + 1) * 128],
                               pxsum[0:64, cols], start=True, stop=True)
                    p.v.tensor_scalar(dgt[:, cols], psA[:, :], bcol, 0.0, OP.add, OP.max)
                    p.s.activation(dtt[:, cols], psA[:, :], AF.Abs, bias=bcol)
                p.s.activation(dtt[:, :], dtt[:, :], AF.Exp, scale=-1.0)
                p.s.activation(dtt[:, :], dtt[:, :], AF.Ln, bias=1.0)
                p.v.tensor_tensor(dgt, dgt, dtt[:, :], OP.add)
            p.y.dma_start(out=ubb.rearrange("p (g t) -> p g t", g=2),
                          in_=u_i[s].rearrange("(g p) t -> p g t", p=128)[:, :, c * tc:(c + 1) * tc])
            p.y.dma_start(out=szb.rearrange("p (g t) -> p g t", g=2),
                          in_=sz_i[s].rearrange("(g p) t -> p g t", p=128)[:, :, c * tc:(c + 1) * tc])

        for s in range(2):
            for c in range(nch):
                prep(s, c)
                for g in range(2):
                    dgt = dtb[:, g * tc:(g + 1) * tc]
                    du = dub[:, g * tc:(g + 1) * tc]
                    p.v.tensor_tensor(du, dgt, ubb[:, g * tc:(g + 1) * tc], OP.mult)
                    ya = yac[:, g * tc:(g + 1) * tc]
                    yag = yacg[:, g * tc:(g + 1) * tc]
                    pool_ns = {2, 5, 8, 11, 14}

                    def emit_dA(n):
                        ac = acol[:, (s * 2 + g) * 16 + n:(s * 2 + g) * 16 + n + 1]
                        p.s.activation(dab[:, (n % 2) * tc:(n % 2 + 1) * tc], dgt,
                                       AF.Exp, scale=ac)

                    def emit_b(n):
                        p.g.tensor_tensor(bbb[:, (n % 2) * tc:(n % 2 + 1) * tc], du,
                                          Bn(n), OP.mult)

                    def emit_scan(n):
                        cidx = (s * 2 + g) * 16 + n
                        init = 0.0 if c % BCH == 0 else carry[:, cidx:cidx + 1]
                        p.v.tensor_tensor_scan(hbb[:, (n % 3) * tc:(n % 3 + 1) * tc],
                                               dab[:, (n % 2) * tc:(n % 2 + 1) * tc],
                                               bbb[:, (n % 2) * tc:(n % 2 + 1) * tc],
                                               init, OP.mult, OP.add)
                        p.v.tensor_copy(carry[:, cidx:cidx + 1],
                                        hbb[:, (n % 3) * tc + tc - 1:(n % 3) * tc + tc])

                    def emit_hc(n):
                        eng = p.g if n in pool_ns else p.v
                        eng.tensor_tensor(hcb[:, (n % 4) * tc:(n % 4 + 1) * tc],
                                          hbb[:, (n % 3) * tc:(n % 3 + 1) * tc],
                                          Cn(n), OP.mult)

                    def emit_acc(n):
                        if n in pool_ns:
                            if n == 2:
                                p.g.tensor_copy(yag, hcb[:, (n % 4) * tc:(n % 4 + 1) * tc])
                            else:
                                p.g.tensor_tensor(yag, yag, hcb[:, (n % 4) * tc:(n % 4 + 1) * tc], OP.add)
                        elif n == 0:
                            p.v.tensor_copy(ya, hcb[:, 0:tc])
                        else:
                            p.v.tensor_tensor(ya, ya, hcb[:, (n % 4) * tc:(n % 4 + 1) * tc], OP.add)

                    emit_dA(0); emit_b(0); emit_scan(0)
                    for n in range(1, 16):
                        emit_dA(n)
                        emit_b(n)
                        emit_hc(n - 1)
                        emit_scan(n)
                        emit_acc(n - 1)
                    emit_hc(15)
                    emit_acc(15)
                    p.v.tensor_tensor(ya, ya, yag, OP.add)
                    p.v.scalar_tensor_tensor(ya, ubb[:, g * tc:(g + 1) * tc],
                                             dcol[:, s * 2 + g:s * 2 + g + 1], ya,
                                             OP.mult, OP.add)
                    p.v.tensor_tensor(ybf[:, g * tc:(g + 1) * tc], ya,
                                      szb[:, g * tc:(g + 1) * tc], OP.mult)
                p.y.dma_start(out=y_o[s].rearrange("(g p) t -> p g t", p=128)[:, :, c * tc:(c + 1) * tc],
                              in_=ybf.rearrange("p (g t) -> p g t", g=2))

        with nc.Block() as block:
            p.emit(block, stack)
    return nc


def build_m4(tok=2048, tpp=512):
    nc = bass.Bass("TRN2", target_bir_lowering=False)
    npass, ntt = tok // tpp, tpp // 128
    assert tpp == 512
    x_in = nc.dram_tensor("x_in", [tok, 1024], F32, kind="ExternalInput").ap()
    yfw = nc.dram_tensor("yfw", [2048, tok], BF16, kind="ExternalInput").ap()
    ybw = nc.dram_tensor("ybw", [2048, tok], BF16, kind="ExternalInput").ap()
    wof = nc.dram_tensor("wof", [128, 16384], F32, kind="ExternalInput").ap()
    wob = nc.dram_tensor("wob", [128, 16384], F32, kind="ExternalInput").ap()
    wfu = nc.dram_tensor("wfu", [128, 16384], F32, kind="ExternalInput").ap()
    fbr = nc.dram_tensor("fbr", [128, 8], F32, kind="ExternalInput").ap()
    x_out = nc.dram_tensor("x_out", [tok, 1024], F32, kind="ExternalOutput").ap()

    xr_ = x_in.rearrange("(n p) d -> p n d", p=128)
    xro = x_out.rearrange("(n p) d -> p n d", p=128)

    with ExitStack() as stack:
        sb = lambda n, sh, dt: stack.enter_context(nc.sbuf_tensor(n, sh, dt))
        B = alloc_common(nc, stack, ntt, tpp, ln_bufs=False)
        wofc = sb("wofc", [128, 16384], BF16)
        wobc = sb("wobc", [128, 16384], BF16)
        wfuc = sb("wfuc", [128, 16384], BF16)
        fbc = sb("fbc", [128, 8], F32)
        yfb = sb("yfb", [128, 16 * tpp], BF16)
        ybb = sb("ybb", [128, 16 * tpp], BF16)
        foT = sb("foT", [128, 8 * tpp], BF16)
        boT = sb("boT", [128, 8 * tpp], BF16)
        outT = sb("outT", [128, 8 * tpp], F32)

        p = Prog(nc)
        emit_identities(p, B.idb, B.idf, B.epsc)
        p.g.dma_start(out=wofc[:, :], in_=wof[:, :])
        p.g.dma_start(out=wobc[:, :], in_=wob[:, :])
        p.g.dma_start(out=wfuc[:, :], in_=wfu[:, :])
        p.y.dma_start(out=fbc[:, :], in_=fbr[:, :])

        for pp in range(npass):
            p.y.dma_start(out=B.xt.rearrange("p (n d) -> p n d", d=1024),
                          in_=xr_[:, pp * ntt:(pp + 1) * ntt, :])
            p.y.dma_start(out=yfb.rearrange("p (kt t) -> p kt t", kt=16),
                          in_=yfw.rearrange("(kt p) t -> p kt t", p=128)[:, :, pp * tpp:(pp + 1) * tpp])
            p.y.dma_start(out=ybb.rearrange("p (kt t) -> p kt t", kt=16),
                          in_=ybw.rearrange("(kt p) t -> p kt t", p=128)[:, :, pp * tpp:(pp + 1) * tpp])
            for wsrc, ysrc, dst in ((wofc, yfb, foT), (wobc, ybb, boT)):
                for og in range(8):
                    ps = B.psmm[og % 2]
                    for kt in range(16):
                        p.t.matmul(ps[:, :], wsrc[:, (og * 16 + kt) * 128:(og * 16 + kt + 1) * 128],
                                   ysrc[:, kt * tpp:(kt + 1) * tpp],
                                   start=(kt == 0), stop=(kt == 15))
                    p.v.tensor_copy(dst[:, og * tpp:(og + 1) * tpp], ps[:, :])
            for og in range(8):
                ps = B.psmm[2 + og % 2]
                for kt in range(16):
                    rhs = foT[:, kt * tpp:(kt + 1) * tpp] if kt < 8 else boT[:, (kt - 8) * tpp:(kt - 7) * tpp]
                    p.t.matmul(ps[:, :], wfuc[:, (og * 16 + kt) * 128:(og * 16 + kt + 1) * 128],
                               rhs, start=(kt == 0), stop=(kt == 15))
                p.s.activation(outT[:, og * tpp:(og + 1) * tpp], ps[:, :],
                               AF.Identity, bias=fbc[:, og:og + 1])
            emit_tail(p, B, outT, xro, pp)

        with nc.Block() as block:
            p.emit(block, stack)
    return nc


def run_l0_attn(x_full, pm, T=16384):
    xf = np.ascontiguousarray(x_full.reshape(T, 1024).astype(np.float32))
    tok = T // NCORES
    nc1 = build_m1(tok=tok)
    res1 = run_bass_kernel_spmd(nc1, [{"x_in": xf[c * tok:(c + 1) * tok]} for c in range(NCORES)],
                                core_ids=list(range(NCORES)))
    xln = np.concatenate([res1.results[c]["xln_o"] for c in range(NCORES)], axis=0)
    xlnT = np.ascontiguousarray(xln.T)
    xlnr = xln.reshape(2, T // 2, 1024)[:, ::-1, :].reshape(T, 1024)
    xlnTr = np.ascontiguousarray(xlnr.T)
    nc2 = build_m2(T=T)
    in_maps = []
    for c in range(NCORES):
        sl = slice(c * 256, (c + 1) * 256)
        m = {"xlnT": xlnT, "xlnTr": xlnTr}
        wrows = {}
        for s, pref in ((0, "fw"), (1, "bw")):
            Win = pm[pref + "_Win"]
            rows = np.concatenate([Win[sl], Win[2048 + (c * 256):(2048 + (c + 1) * 256)]], axis=0)
            wrows[s] = _lhsT_resident(rows)
        m["win"] = np.stack([wrows[0], wrows[1]], axis=0)
        cw = np.zeros((128, 16), np.float32)
        cb = np.zeros((128, 4), np.float32)
        wxs = []
        for s, pref in ((0, "fw"), (1, "bw")):
            cwf = pm[pref + "_convw"][sl, 0, :]
            cw[:, s * 8:s * 8 + 4] = cwf[:128]
            cw[:, s * 8 + 4:s * 8 + 8] = cwf[128:]
            cbf = pm[pref + "_convb"][sl]
            cb[:, s * 2] = cbf[:128]
            cb[:, s * 2 + 1] = cbf[128:]
            Wx = pm[pref + "_Wx"]
            wxr = Wx[:, sl]
            a = np.zeros((128, 192), np.float32)
            a[:, 0:96] = wxr.T[:128]
            a[:, 96:192] = wxr.T[128:]
            wxs.append(a)
        m["convw"] = np.ascontiguousarray(cw[:, [0,1,2,3,8,9,10,11,4,5,6,7,12,13,14,15]]) if False else cw
        m["convb"] = cb
        m["wx"] = np.stack(wxs, axis=0)
        in_maps.append(m)
    res2 = run_bass_kernel_spmd(nc2, in_maps, core_ids=list(range(NCORES)))
    px_all = np.stack([res2.results[c]["px_o"] for c in range(NCORES)], axis=0)
    nc3 = build_m3(T=T)
    in_maps = []
    for c in range(NCORES):
        sl = slice(c * 256, (c + 1) * 256)
        m = {"px_all": px_all,
             "u_i": res2.results[c]["u_o"], "sz_i": res2.results[c]["sz_o"]}
        wdts, bdt_, alog_, dv_ = [], np.zeros((128, 4), np.float32), np.zeros((128, 64), np.float32), np.zeros((128, 4), np.float32)
        for s, pref in ((0, "fw"), (1, "bw")):
            Wdt = pm[pref + "_Wdt"][sl]
            wdts.append(np.ascontiguousarray(Wdt.T))
            b = pm[pref + "_bdt"][sl]
            bdt_[:, s * 2] = b[:128]
            bdt_[:, s * 2 + 1] = b[128:]
            Al = pm[pref + "_Alog"][sl]
            alog_[:, (s * 2) * 16:(s * 2 + 1) * 16] = Al[:128]
            alog_[:, (s * 2 + 1) * 16:(s * 2 + 2) * 16] = Al[128:]
            D = pm[pref + "_D"][sl]
            dv_[:, s * 2] = D[:128]
            dv_[:, s * 2 + 1] = D[128:]
        m["wdt"] = np.stack(wdts, axis=0)
        m["bdt"] = bdt_
        m["alog"] = alog_
        m["dvec"] = dv_
        in_maps.append(m)
    res3 = run_bass_kernel_spmd(nc3, in_maps, core_ids=list(range(NCORES)))
    yfw = np.concatenate([res3.results[c]["y_o"][0] for c in range(NCORES)], axis=0)
    ybw_r = np.concatenate([res3.results[c]["y_o"][1] for c in range(NCORES)], axis=0)
    ybw = ybw_r.reshape(2048, 2, T // 2)[:, :, ::-1].reshape(2048, T)
    nc4 = build_m4(tok=tok)
    base = {
        "wof": _lhsT_resident(pm["fw_Wout"]), "wob": _lhsT_resident(pm["bw_Wout"]),
        "wfu": _lhsT_resident(pm["fuse_W"]),
        "fbr": np.ascontiguousarray(pm["fuse_b"].reshape(8, 128).T),
    }
    in_maps = []
    for c in range(NCORES):
        m = dict(base)
        m["x_in"] = xf[c * tok:(c + 1) * tok]
        m["yfw"] = np.ascontiguousarray(yfw[:, c * tok:(c + 1) * tok])
        m["ybw"] = np.ascontiguousarray(ybw[:, c * tok:(c + 1) * tok])
        in_maps.append(m)
    res4 = run_bass_kernel_spmd(nc4, in_maps, core_ids=list(range(NCORES)))
    out = np.concatenate([res4.results[c]["x_out"] for c in range(NCORES)], axis=0)
    return out.reshape(2, 8192, 1024)


def _grp(inputs, pref):
    return {k[len(pref):]: np.asarray(v) for k, v in inputs.items() if k.startswith(pref)}


def kernel(**inputs):
    x = np.asarray(inputs["L0_attn_x"]).astype(np.float32)
    x = run_l0_attn(x, _grp(inputs, "L0_attn_"))
    x = run_ffn0(x, _grp(inputs, "L0_ffn_"))
    x = run_l1_attn(x, _grp(inputs, "L1_attn_"))
    x = run_rwkv(x, _grp(inputs, "L1_ffn_"))
    x = run_l2_attn(x, _grp(inputs, "L2_attn_"))
    x = run_moe(x, _grp(inputs, "L2_ffn_"))
    x = run_l3_attn(x, _grp(inputs, "L3_attn_"))
    x = run_rwkv(x, _grp(inputs, "L3_ffn_"))
    return np.ascontiguousarray(x.astype(np.float32))
